# version 01
# baseline (speedup 1.0000x reference)
import numpy as np
import ml_dtypes
import concourse.bass as bass
import concourse.mybir as mybir
from concourse.bass_utils import run_bass_kernel_spmd

F32 = mybir.dt.float32
BF16 = mybir.dt.bfloat16
AF = mybir.ActivationFunctionType
ALU = mybir.AluOpType
AXX = mybir.AxisListType.X

S, D = 8192, 2048
NCORE = 8
TOK = S // NCORE
NT = TOK // 128
CH = 128
NCH = TOK // CH
H, P, NS = 64, 64, 128
INTER, CONV_DIM, PROJ = 4096, 4352, 8512
NIC = 16
NOC = 67
HA, DH = 16, 128
FF = 8192
RNK = 32
EPS = 1e-5
ASHIFT = 40.0
ASCALE = float(1.0 / np.sqrt(DH))

_CACHE = {}


class Prog:

    def __init__(self):
        self.items = {e: [] for e in ("sync", "gpsimd", "tensor", "vector", "scalar")}
        self.cum = 0

    def add(self, eng, fn, n=1, unit=1, pre=None):
        self.items[eng].append((self.cum, fn, n, unit, pre))
        self.cum += n * unit

    def dma(self, eng, fn, n=1, pre=None):
        self.add(eng, fn, n=n, unit=16, pre=pre)

    def async_op(self, eng, fn, done_sem):
        self.items[eng].append((self.cum, fn, 0, done_sem, None))

    def realize(self, block, sem):
        def mk(name):
            items = self.items[name]

            def body(e):
                for wait, fn, n, unit, pre in items:
                    if pre is not None:
                        for psem, pval in pre:
                            e.wait_ge(psem, pval)
                    if wait > 0:
                        e.wait_ge(sem, wait)
                    r = fn(e)
                    insts = r if isinstance(r, (list, tuple)) else [r]
                    if n == 0:
                        insts[-1].then_inc(unit, 1)
                    else:
                        assert len(insts) == n, f"{name}: got {len(insts)} want {n}"
                        for i in insts:
                            i.then_inc(sem, unit)
            return body
        block.sync(mk("sync"))
        block.gpsimd(mk("gpsimd"))
        block.tensor(mk("tensor"))
        block.vector(mk("vector"))
        block.scalar(mk("scalar"))


def build_program(debug=False):
    nc = bass.Bass("TRN2", target_bir_lowering=False)
    pr = Prog()

    def din(name, shape, dt=F32):
        return nc.dram_tensor(name, shape, dt, kind="ExternalInput").ap()

    def dint(name, shape, dt=F32):
        return nc.dram_tensor(name, shape, dt).ap()

    x_own = din("x_own", [TOK, D])
    x_halo = din("x_halo", [3, D])
    w_ipT = din("w_ipT", [D, PROJ])
    conv_w = din("conv_w", [CONV_DIM, 4])
    conv_b = din("conv_b", [CONV_DIM])
    a_log = din("a_log", [H])
    dt_bias = din("dt_bias", [H])
    d_vec = din("d_vec", [H])
    norm_w = din("norm_w", [INTER])
    w_opT = din("w_opT", [INTER, D])
    sbias = din("sbias", [D])
    w_dT = din("w_dT", [D, RNK])
    w_uT = din("w_uT", [RNK, D])
    w_rT = din("w_rT", [D, 3])
    ln1_s = din("ln1_s", [D])
    ln1_b = din("ln1_b", [D])
    w_qT = din("w_qT", [D, 2 * DH])
    w_kT = din("w_kT", [D, 2 * DH])
    w_vT = din("w_vT", [D, 2 * DH])
    w_oTs = din("w_oTs", [2 * DH, D])
    ln2_s = din("ln2_s", [D])
    ln2_b = din("ln2_b", [D])
    w_f1T = din("w_f1T", [D, FF])
    w_f2T = din("w_f2T", [FF, D])
    identf = din("identf", [128, 128])
    identb_src = din("identb_src", [128, 128])
    mask_sle = din("mask_sle", [128, 128])
    amask = din("amask", [4, 128, 512])
    gates = din("gates", [1, NCORE])
    onesf = din("onesf", [1, 128])
    out = nc.dram_tensor("out", [TOK, D], F32, kind="ExternalOutput").ap()

    g_xn2_in = dint("g_xn2_in", [D, TOK], BF16)
    g_xn2 = dint("g_xn2", [NCORE * D, TOK], BF16)
    g_st_in = dint("g_st_in", [1, NS * H * P], BF16)
    g_st = dint("g_st", [NCORE, NS * H * P], BF16)
    g_T_in = dint("g_T_in", [1, H])
    g_T = dint("g_T", [NCORE, H])
    rs_in = dint("rs_in", [S, D])
    rs_out = dint("rs_out", [TOK, D])

    d_xnfm = dint("d_xnfm", [NIC, 128, TOK], BF16)
    d_m = dint("d_m", [2, TOK])
    d_xt = dint("d_xt", [NCH, 128, H * P], BF16)
    d_sg = dint("d_sg", [32, 128, TOK], BF16)
    d_dres = dint("d_dres", [32, 128, TOK], BF16)
    d_Cfm = dint("d_Cfm", [NCH, 128, CH], BF16)
    d_gmt = dint("d_gmt", [NCH, 128, CH])
    d_acum = dint("d_acum", [NCH, H, CH])
    d_nacT = dint("d_nacT", [NCH, CH, H])
    d_sin = dint("d_sin", [NCH, 128, H * P], BF16)
    d_lam = dint("d_lam", [NCH, H])
    d_yn = dint("d_yn", [NCH, 128, INTER], BF16)
    d_s = dint("d_s", [NT, 128, D], BF16)

    dbg = {}
    if debug:
        def dbgout(name, shape, dt=F32):
            dbg[name] = nc.dram_tensor(name, shape, dt, kind="ExternalOutput").ap()
        dbgout("dbg_xn2", [D, TOK])
        dbgout("dbg_csl", [34, 128, 512])
        dbgout("dbg_dt", [64, 1024])
        dbgout("dbg_sin", [NCH, 128, H * P])
        dbgout("dbg_stot", [NCH, 128, H * P])
        dbgout("dbg_yg", [NCH, 128, H * P])
        dbgout("dbg_s", [NT, 128, D])
        dbgout("dbg_att", [2 * DH, S])
        dbgout("dbg_rs", [TOK, D])
        dbgout("dbg_ffn", [NT, 128, D])

    import contextlib
    stack = contextlib.ExitStack()

    def sb(name, shape, dt=F32):
        return stack.enter_context(nc.sbuf_tensor(name, shape, dt))

    def psx(name, shape, dt=F32):
        return stack.enter_context(nc.psum_tensor(name, shape, dt))

    with stack:
        sem = nc.alloc_semaphore("gseq")
        cc1 = nc.alloc_semaphore("cc1")
        cc2 = nc.alloc_semaphore("cc2")
        cc3 = nc.alloc_semaphore("cc3")
        cc4 = nc.alloc_semaphore("cc4")

        IDf = sb("IDf", [128, 128], F32)
        IDb = sb("IDb", [128, 128], BF16)
        MSL = sb("MSL", [128, 128], F32)
        AMK = sb("AMK", [128, 4 * 512], BF16)
        ONESB = sb("ONESB", [1, 128], BF16)
        ONESCB = sb("ONESCB", [128, 1], BF16)
        ONESCF = sb("ONESCF", [128, 1], F32)
        ONES64 = sb("ONES64", [64, 128], F32)
        CONVW = sb("CONVW", [128, 34 * 4], F32)
        CONVB = sb("CONVB", [128, 34], F32)
        DCOL = sb("DCOL", [128, 32], F32)
        NWCOL = sb("NWCOL", [128, 32], F32)
        ACOL = sb("ACOL", [64, 1], F32)
        DTBC = sb("DTBC", [64, 1], F32)
        L1S = sb("L1S", [128, 16], F32)
        L1B = sb("L1B", [128, 16], F32)
        L2S = sb("L2S", [128, 16], F32)
        L2B = sb("L2B", [128, 16], F32)
        WRT = sb("WRT", [128, 48], F32)
        GATES = sb("GATES", [1, NCORE], F32)
        GREP = sb("GREP", [128, NCORE], F32)
        WDT = sb("WDT", [128, 16 * RNK], BF16)
        BREP = sb("BREP", [128, D], BF16)
        ROWS = sb("ROWS", [1, 1024], F32)
        ROWB = sb("ROWB", [1, 1024], BF16)
        COLS = sb("COLS", [128, 32], F32)
        SHFT = sb("SHFT", [128, 1], F32)
        SMF = sb("SMF", [128, 640], F32)

        PSF = [psx(f"PS{i}", [128, 512], F32) for i in range(7)]
        PS_HB = psx("PS_HB", [128, 1024], BF16)

        A1 = sb("A1", [128, 17408], BF16)
        A2 = sb("A2", [128, 16384], F32)
        A3 = sb("A3", [128, 16384], BF16)
        A4 = sb("A4", [128, 16384], BF16)
        A5 = sb("A5", [128, 8192], BF16)
        A7 = sb("A7", [128, 2048], BF16)
        A2b = A2.ap().bitcast(BF16)

        pr.dma("sync", lambda e: e.dma_start(out=IDf[:, :], in_=identf))
        pr.dma("gpsimd", lambda e: e.dma_start(out=IDb[:, :], in_=identb_src))
        pr.dma("sync", lambda e: e.dma_start(out=MSL[:, :], in_=mask_sle))
        pr.dma("gpsimd", lambda e: e.dma_start(
            out=AMK[:, :].rearrange("p (j t) -> p j t", t=512),
            in_=amask.rearrange("j p t -> p j t")))
        pr.dma("gpsimd", lambda e: e.dma_start(out=ONESB[:, :], in_=onesf))
        pr.dma("gpsimd", lambda e: e.dma_start(
            out=ONESCB[:, :], in_=onesf.rearrange("a (p o) -> (a p) o", o=1)))
        pr.dma("sync", lambda e: e.dma_start(
            out=ONESCF[:, :], in_=onesf.rearrange("a (p o) -> (a p) o", o=1)))
        pr.add("vector", lambda e: e.memset(ONES64[:, :], 1.0))
        pr.add("vector", lambda e: e.memset(SHFT[:, :], -ASHIFT))
        pr.dma("sync", lambda e: e.dma_start(
            out=CONVW[:, :].rearrange("p (fc k) -> p fc k", k=4),
            in_=conv_w.rearrange("(fc p) k -> p fc k", p=128)))
        pr.dma("sync", lambda e: e.dma_start(
            out=CONVB[:, :], in_=conv_b.rearrange("(fc p) -> p fc", p=128), allow_slow_non_contiguous=True))
        pr.dma("sync", lambda e: e.dma_start(
            out=DCOL[0:64, :],
            in_=d_vec.rearrange("(fc h2) -> fc h2", h2=2)[:, 0:1]
                     .rearrange("fc a -> a fc").broadcast_to((64, 32)), allow_slow_non_contiguous=True))
        pr.dma("sync", lambda e: e.dma_start(
            out=DCOL[64:128, :],
            in_=d_vec.rearrange("(fc h2) -> fc h2", h2=2)[:, 1:2]
                     .rearrange("fc a -> a fc").broadcast_to((64, 32)), allow_slow_non_contiguous=True))
        pr.dma("sync", lambda e: e.dma_start(
            out=NWCOL[:, :], in_=norm_w.rearrange("(fc q) -> q fc", q=128), allow_slow_non_contiguous=True))
        pr.dma("sync", lambda e: e.dma_start(out=ACOL[:, :], in_=a_log.rearrange("(h o) -> h o", o=1)))
        pr.dma("sync", lambda e: e.dma_start(out=DTBC[:, :], in_=dt_bias.rearrange("(h o) -> h o", o=1)))
        for t, src in ((L1S, ln1_s), (L1B, ln1_b), (L2S, ln2_s), (L2B, ln2_b)):
            pr.dma("sync", lambda e, t=t, src=src: e.dma_start(
                out=t[:, :], in_=src.rearrange("(fc q) -> q fc", q=128), allow_slow_non_contiguous=True))
        pr.dma("gpsimd", lambda e: e.dma_start(
            out=BREP[:, :], in_=sbias.rearrange("(o d) -> o d", o=1).broadcast_to((128, D))))
        pr.dma("sync", lambda e: e.dma_start(
            out=WRT[:, :].rearrange("q (ic r) -> q ic r", r=3),
            in_=w_rT.rearrange("(ic q) r -> q ic r", q=128)))
        pr.dma("sync", lambda e: e.dma_start(out=GATES[:, :], in_=gates))
        pr.dma("sync", lambda e: e.dma_start(out=GREP[:, :], in_=gates.broadcast_to((128, NCORE))))
        pr.dma("gpsimd", lambda e: e.dma_start(
            out=WDT[:, :].rearrange("q (ic r) -> q ic r", r=RNK),
            in_=w_dT.rearrange("(ic q) r -> q ic r", q=128)))
        pr.add("scalar", lambda e: e.activation(ACOL[:, 0:1], ACOL[:, 0:1], AF.Exp))
        pr.add("vector", lambda e: e.tensor_scalar_mul(ACOL[:, 0:1], ACOL[:, 0:1], -1.0))

        XNFM = A4
        for t in range(NT):
            xs = A2[:, 0:2048]
            xn = A2[:, 2048:4096]
            o1 = A2[:, 4096:6144]
            scr = A2[:, 6144:8192]
            xnf32 = A2[:, 8192:10240]
            xn2t = A7[:, 0:2048]
            pr.dma("sync", lambda e, t=t, xs=xs: e.dma_start(
                out=xs, in_=x_own[t * 128:(t + 1) * 128, :]))
            pr.add("vector", lambda e, xs=xs: e.reduce_sum(COLS[:, 0:1], xs, axis=AXX))
            pr.add("vector", lambda e: e.tensor_scalar_mul(COLS[:, 1:2], COLS[:, 0:1], -1.0 / D))
            pr.add("scalar", lambda e, xs=xs, scr=scr: e.activation(
                scr, xs, AF.Square, bias=COLS[:, 1:2], accum_out=COLS[:, 2:3]))
            pr.add("vector", lambda e: e.tensor_scalar(COLS[:, 3:4], COLS[:, 2:3], 1.0 / D, EPS, ALU.mult, ALU.add))
            pr.add("scalar", lambda e: e.activation(COLS[:, 4:5], COLS[:, 3:4], AF.Sqrt))
            pr.add("vector", lambda e: e.reciprocal(COLS[:, 5:6], COLS[:, 4:5]))
            pr.add("vector", lambda e, xs=xs, xn=xn: e.tensor_scalar(
                xn, xs, COLS[:, 1:2], COLS[:, 5:6], ALU.add, ALU.mult))
            pr.add("vector", lambda e, xn=xn: e.reduce_sum(COLS[:, 8:9], xn, axis=AXX))
            pr.add("vector", lambda e: e.tensor_scalar_mul(COLS[:, 9:10], COLS[:, 8:9], -1.0 / D))
            pr.add("scalar", lambda e, xn=xn, scr=scr: e.activation(
                scr, xn, AF.Square, bias=COLS[:, 9:10], accum_out=COLS[:, 10:11]))
            pr.add("vector", lambda e: e.tensor_scalar(COLS[:, 11:12], COLS[:, 10:11], 1.0 / D, EPS, ALU.mult, ALU.add))
            pr.add("scalar", lambda e: e.activation(COLS[:, 12:13], COLS[:, 11:12], AF.Sqrt))
            pr.add("vector", lambda e: e.reciprocal(COLS[:, 13:14], COLS[:, 12:13]))
            pr.add("vector", lambda e, xn=xn, o1=o1: e.tensor_scalar(
                o1, xn, COLS[:, 9:10], COLS[:, 13:14], ALU.add, ALU.mult))
            for fc in range(16):
                pr.add("tensor", lambda e, fc=fc, xn=xn: e.transpose(
                    PSF[6][:, 0:128], xn[:, fc * 128:(fc + 1) * 128], IDf[:, :]))
                pr.add("vector", lambda e, fc=fc, t=t: e.tensor_copy(
                    XNFM[:, fc * 1024 + t * 128: fc * 1024 + t * 128 + 128], PSF[6][:, 0:128]))
                pr.add("vector", lambda e, fc=fc, xnf32=xnf32: e.tensor_copy(
                    xnf32[:, fc * 128:(fc + 1) * 128], PSF[6][:, 0:128]))
                pr.add("tensor", lambda e, fc=fc, o1=o1: e.transpose(
                    PSF[6][:, 128:256], o1[:, fc * 128:(fc + 1) * 128], IDf[:, :]))
                pr.add("vector", lambda e, fc=fc, xn2t=xn2t: e.tensor_scalar(
                    xn2t[:, fc * 128:(fc + 1) * 128], PSF[6][:, 128:256],
                    L1S[:, fc:fc + 1], L1B[:, fc:fc + 1], ALU.mult, ALU.add))
            pr.dma("sync", lambda e, t=t: [e.dma_start(
                out=d_xnfm[fc, :, t * 128:(t + 1) * 128],
                in_=XNFM[:, fc * 1024 + t * 128: fc * 1024 + t * 128 + 128])
                for fc in range(16)], n=16)
            pr.dma("gpsimd", lambda e, t=t, xn2t=xn2t: [e.dma_start(
                out=g_xn2_in.rearrange("(fc q) t -> fc q t", q=128)[fc, :, t * 128:(t + 1) * 128],
                in_=xn2t[:, fc * 128:(fc + 1) * 128]) for fc in range(16)], n=16)

            def mm_rt(e, xnf32=xnf32):
                return [e.matmul(PSF[5][:, 0:3], xnf32[:, ic * 128:(ic + 1) * 128],
                                 WRT[:, ic * 3:(ic + 1) * 3],
                                 start=(ic == 0), stop=(ic == 15)) for ic in range(16)]
            pr.add("tensor", mm_rt, n=16)
            pr.add("vector", lambda e: e.tensor_copy(COLS[:, 24:27], PSF[5][:, 0:3]))
            pr.add("vector", lambda e: e.tensor_tensor(COLS[:, 16:17], COLS[:, 26:27], COLS[:, 24:25], ALU.is_gt))
            pr.add("vector", lambda e: e.tensor_tensor(COLS[:, 17:18], COLS[:, 26:27], COLS[:, 25:26], ALU.is_gt))
            pr.add("vector", lambda e: e.tensor_tensor(COLS[:, 18:19], COLS[:, 25:26], COLS[:, 24:25], ALU.is_gt))
            pr.add("vector", lambda e: e.tensor_tensor(COLS[:, 19:20], COLS[:, 16:17], COLS[:, 17:18], ALU.mult))
            pr.add("vector", lambda e: e.tensor_scalar(COLS[:, 20:21], COLS[:, 17:18], -1.0, 1.0, ALU.mult, ALU.add))
            pr.add("vector", lambda e: e.tensor_tensor(COLS[:, 21:22], COLS[:, 18:19], COLS[:, 20:21], ALU.mult))
            pr.dma("sync", lambda e, t=t: e.dma_start(
                out=d_m.rearrange("a (t o) -> a t o", o=1)[1, t * 128:(t + 1) * 128], in_=COLS[:, 19:20]))
            pr.dma("sync", lambda e, t=t: e.dma_start(
                out=d_m.rearrange("a (t o) -> a t o", o=1)[0, t * 128:(t + 1) * 128], in_=COLS[:, 21:22]))
        if debug:
            pr.dma("gpsimd", lambda e: e.dma_start(out=dbg["dbg_xn2"], in_=g_xn2_in))
        pr.async_op("gpsimd", lambda e: e.collective_compute(
            "AllGather", ALU.bypass, replica_groups=[list(range(NCORE))],
            ins=[g_xn2_in.opt()], outs=[g_xn2.opt()]), cc1)

        pr.dma("sync", lambda e: e.dma_start(out=A2[0:3, 0:2048], in_=x_halo))
        pr.add("vector", lambda e: e.reduce_sum(COLS[0:3, 0:1], A2[0:3, 0:2048], axis=AXX))
        pr.add("vector", lambda e: e.tensor_scalar_mul(COLS[0:3, 1:2], COLS[0:3, 0:1], -1.0 / D))
        pr.add("scalar", lambda e: e.activation(A2[0:3, 2048:4096], A2[0:3, 0:2048], AF.Square,
                                                bias=COLS[0:3, 1:2], accum_out=COLS[0:3, 2:3]))
        pr.add("vector", lambda e: e.tensor_scalar(COLS[0:3, 3:4], COLS[0:3, 2:3], 1.0 / D, EPS, ALU.mult, ALU.add))
        pr.add("scalar", lambda e: e.activation(COLS[0:3, 4:5], COLS[0:3, 3:4], AF.Sqrt))
        pr.add("vector", lambda e: e.reciprocal(COLS[0:3, 5:6], COLS[0:3, 4:5]))
        pr.add("vector", lambda e: e.tensor_scalar(A2[0:3, 0:2048], A2[0:3, 0:2048],
                                                   COLS[0:3, 1:2], COLS[0:3, 5:6], ALU.add, ALU.mult))
        XNH = sb("XNH", [128, 48], BF16)
        for ic in range(16):
            pr.add("tensor", lambda e, ic=ic: e.transpose(
                PSF[6][:, 0:3], A2[0:3, ic * 128:(ic + 1) * 128], IDf[0:3, 0:3]))
            pr.add("vector", lambda e, ic=ic: e.tensor_copy(XNH[:, ic * 3:ic * 3 + 3], PSF[6][:, 0:3]))

        CSL = A1
        IPO = A7
        HALO = sb("HALO", [128, 102], BF16)
        WSL = A3
        SILG = A2b[:, 24576:25600]
        DTX = A2[0:64, 6144:8192]
        ACM = sb("ACM", [64, 384], F32)
        TSEG = sb("TSEG", [64, 1], F32)
        XTI = A2b[:, 16384:20480]
        X2T = A2b[:, 20480:24576]
        KSM = sb("KSM", [128, 256], BF16)
        RUNA = A2b
        pr.add("vector", lambda e: e.memset(TSEG[:, :], 0.0))
        pr.add("vector", lambda e: e.memset(RUNA[:, 0:4096], 0.0))

        for tb in range(2):
            for og in range(17):
                ocs = list(range(og * 4, min(og * 4 + 4, NOC)))
                buf = og % 2
                Wb = WSL[:, buf * 8192: buf * 8192 + 8192]

                def ld_w(e, ocs=ocs, Wb=Wb):
                    ins = []
                    for j, oc in enumerate(ocs):
                        ow = 128 if oc < 66 else 64
                        ins.append(e.dma_start(
                            out=Wb[:, j * 2048: j * 2048 + 16 * ow]
                                .rearrange("q (ic o) -> q ic o", o=ow),
                            in_=w_ipT.rearrange("(ic q) o -> q ic o", q=128)[:, :, oc * 128:oc * 128 + ow]))
                    return ins
                pr.dma("gpsimd", ld_w, n=len(ocs))

                for j, oc in enumerate(ocs):
                    ow = 128 if oc < 66 else 64
                    bank = PSF[j]

                    def mm_ip(e, j=j, ow=ow, bank=bank, Wb=Wb, tb=tb):
                        return [e.matmul(bank[0:ow, 0:512],
                                         Wb[:, j * 2048 + ic * ow: j * 2048 + (ic + 1) * ow],
                                         XNFM[:, ic * 1024 + tb * 512: ic * 1024 + tb * 512 + 512],
                                         start=(ic == 0), stop=(ic == 15)) for ic in range(16)]
                    pr.add("tensor", mm_ip, n=16)
                    if tb == 0 and 32 <= oc <= 65:
                        def mm_halo(e, j=j, Wb=Wb):
                            return [e.matmul(PSF[4][0:128, 500 + j * 3:503 + j * 3],
                                             Wb[:, j * 2048 + ic * 128: j * 2048 + (ic + 1) * 128],
                                             XNH[:, ic * 3:(ic + 1) * 3],
                                             start=(ic == 0), stop=(ic == 15)) for ic in range(16)]
                        pr.add("tensor", mm_halo, n=16)
                        pr.add("vector", lambda e, j=j, oc=oc: e.tensor_copy(
                            HALO[:, (oc - 32) * 3:(oc - 32) * 3 + 3],
                            PSF[4][0:128, 500 + j * 3:503 + j * 3]))
                    if oc == 66:
                        pr.add("vector", lambda e, bank=bank: e.tensor_copy(DTX[:, 0:512], bank[0:64, 0:512]))
                    else:
                        pr.add("vector", lambda e, j=j, bank=bank: e.tensor_copy(
                            IPO[:, (j % 4) * 512:(j % 4) * 512 + 512], bank[0:128, 0:512]))
                for j, oc in enumerate(ocs):
                    src = IPO[:, (j % 4) * 512:(j % 4) * 512 + 512]
                    if oc < 32:
                        pr.add("scalar", lambda e, src=src: e.activation(SILG[:, 0:512], src, AF.Silu))
                        pr.dma("sync", lambda e, oc=oc, tb=tb: e.dma_start(
                            out=d_sg[oc, :, tb * 512:(tb + 1) * 512], in_=SILG[:, 0:512]))
                    elif oc < 66:
                        fc = oc - 32
                        acc = SMF[:, 0:512]
                        pr.add("vector", lambda e, src=src, fc=fc, acc=acc: e.tensor_scalar(
                            acc, src, CONVW[:, fc * 4 + 3:fc * 4 + 4], None, ALU.mult))
                        for k in (2, 1, 0):
                            sh = 3 - k
                            pr.add("vector", lambda e, src=src, fc=fc, k=k, sh=sh, acc=acc:
                                   e.scalar_tensor_tensor(
                                       acc[:, sh:512], src[:, 0:512 - sh],
                                       CONVW[:, fc * 4 + k:fc * 4 + k + 1],
                                       acc[:, sh:512], ALU.mult, ALU.add))
                            pr.add("vector", lambda e, fc=fc, k=k, sh=sh, acc=acc:
                                   e.scalar_tensor_tensor(
                                       acc[:, 0:sh], HALO[:, fc * 3 + 3 - sh:fc * 3 + 3],
                                       CONVW[:, fc * 4 + k:fc * 4 + k + 1],
                                       acc[:, 0:sh], ALU.mult, ALU.add))
                        pr.add("vector", lambda e, src=src, fc=fc: e.tensor_copy(
                            HALO[:, fc * 3:fc * 3 + 3], src[:, 509:512]))
                        pr.add("scalar", lambda e, fc=fc, acc=acc: e.activation(
                            CSL[:, fc * 512:fc * 512 + 512], acc, AF.Silu, bias=CONVB[:, fc:fc + 1]))
            for fc in range(32):
                pr.dma("sync", lambda e, fc=fc, tb=tb: e.dma_start(
                    out=SILG[:, 0:512], in_=d_sg[fc, :, tb * 512:(tb + 1) * 512]))
                pr.add("vector", lambda e, fc=fc: e.scalar_tensor_tensor(
                    SILG[:, 512:1024], CSL[:, fc * 512:fc * 512 + 512],
                    DCOL[:, fc:fc + 1], SILG[:, 0:512], ALU.mult, ALU.mult))
                pr.dma("sync", lambda e, fc=fc, tb=tb: e.dma_start(
                    out=d_dres[fc, :, tb * 512:(tb + 1) * 512], in_=SILG[:, 512:1024]))
            if debug and tb == 0:
                pr.dma("gpsimd", lambda e: [e.dma_start(
                    out=dbg["dbg_csl"][fc], in_=CSL[:, fc * 512:(fc + 1) * 512]) for fc in range(34)], n=34)

            pr.add("vector", lambda e: e.tensor_scalar(DTX[:, 512:1024], DTX[:, 0:512], DTBC[:, 0:1], None, ALU.add))
            pr.add("vector", lambda e: e.tensor_scalar(DTX[:, 1024:1536], DTX[:, 512:1024], -1.0, None, ALU.mult))
            pr.add("vector", lambda e: e.tensor_max(DTX[:, 1024:1536], DTX[:, 1024:1536], DTX[:, 512:1024]))
            pr.add("scalar", lambda e: e.activation(DTX[:, 1024:1536], DTX[:, 1024:1536], AF.Exp, scale=-1.0))
            pr.add("scalar", lambda e: e.activation(DTX[:, 1024:1536], DTX[:, 1024:1536], AF.Ln, bias=1.0))
            pr.add("vector", lambda e: e.tensor_scalar_max(DTX[:, 512:1024], DTX[:, 512:1024], 0.0))
            pr.add("vector", lambda e: e.tensor_add(DTX[:, 512:1024], DTX[:, 512:1024], DTX[:, 1024:1536]))
            pr.add("vector", lambda e: e.tensor_scalar(DTX[:, 1536:2048], DTX[:, 512:1024], ACOL[:, 0:1], None, ALU.mult))
            if debug and tb == 0:
                pr.dma("sync", lambda e: e.dma_start(out=dbg["dbg_dt"][:, 0:512], in_=DTX[:, 512:1024]))
                pr.dma("sync", lambda e: e.dma_start(out=dbg["dbg_dt"][:, 512:1024], in_=DTX[:, 1536:2048]))

            for ci in range(4):
                c = tb * 4 + ci
                cs = slice(ci * 128, ci * 128 + 128)
                pr.add("vector", lambda e, cs=cs: e.tensor_tensor_scan(
                    ACM[:, 0:128], ONES64[:, :], DTX[:, 1536:2048][:, cs], 0.0, ALU.mult, ALU.add))
                pr.add("vector", lambda e: e.tensor_add(TSEG[:, 0:1], TSEG[:, 0:1], ACM[:, 127:128]))
                pr.dma("sync", lambda e, c=c: e.dma_start(
                    out=d_lam.rearrange("c (h o) -> c h o", o=1)[c], in_=ACM[:, 127:128]))
                pr.dma("sync", lambda e, c=c: e.dma_start(out=d_acum[c], in_=ACM[:, 0:128]))
                pr.add("tensor", lambda e: e.transpose(PSF[6][:, 0:64], ACM[:, 0:128], IDf[0:64, 0:64]))
                pr.add("vector", lambda e: e.tensor_scalar_mul(SMF[:, 512:576], PSF[6][:, 0:64], -1.0))
                pr.dma("sync", lambda e, c=c: e.dma_start(out=d_nacT[c], in_=SMF[:, 512:576]))
                pr.add("vector", lambda e: e.tensor_scalar(
                    ACM[:, 128:256], ACM[:, 0:128], -1.0, ACM[:, 127:128], ALU.mult, ALU.add))
                pr.add("scalar", lambda e: e.activation(ACM[:, 128:256], ACM[:, 128:256], AF.Exp))
                pr.add("tensor", lambda e: e.transpose(PSF[6][:, 64:128], ACM[:, 128:256], IDf[0:64, 0:64]))
                pr.add("vector", lambda e: e.tensor_copy(KSM[:, 0:64], PSF[6][:, 64:128]))
                pr.dma("sync", lambda e, c=c: e.dma_start(out=ROWS[0:1, 0:64], in_=d_lam[c:c + 1, :]))
                pr.add("scalar", lambda e: e.activation(ROWS[0:1, 64:128], ROWS[0:1, 0:64], AF.Exp))
                pr.add("vector", lambda e: e.tensor_copy(ROWB[0:1, 0:64], ROWS[0:1, 64:128]))
                pr.add("tensor", lambda e, cs=cs: e.transpose(
                    PSF[6][:, 128:192], DTX[:, 512:1024][:, cs], IDf[0:64, 0:64]))
                pr.add("vector", lambda e: e.tensor_copy(KSM[:, 64:128], PSF[6][:, 128:192]))
                pr.dma("sync", lambda e, c=c, cs=cs: e.dma_start(
                    out=d_Cfm[c], in_=CSL[:, 33 * 512:34 * 512][:, cs]))
                pr.add("tensor", lambda e, cs=cs: e.transpose(
                    PS_HB[:, 0:128], CSL[:, 32 * 512:33 * 512][:, cs], IDb[:, :]))
                pr.add("vector", lambda e: e.tensor_copy(KSM[:, 128:256], PS_HB[:, 0:128]))
                pr.add("tensor", lambda e, cs=cs: e.matmul(
                    PSF[6][:, 192:320], CSL[:, 32 * 512:33 * 512][:, cs],
                    CSL[:, 33 * 512:34 * 512][:, cs], start=True, stop=True))
                pr.add("vector", lambda e: e.tensor_tensor(SMF[:, 0:128], PSF[6][:, 192:320], MSL[:, :], ALU.mult))
                pr.dma("sync", lambda e, c=c: e.dma_start(out=d_gmt[c], in_=SMF[:, 0:128]))
                for fc in range(32):
                    pr.add("tensor", lambda e, fc=fc, cs=cs: e.transpose(
                        PS_HB[:, 128:256], CSL[:, fc * 512:(fc + 1) * 512][:, cs], IDb[:, :]))
                    pr.add("vector", lambda e, fc=fc: e.tensor_tensor(
                        XTI[:, fc * 128:(fc + 1) * 128].rearrange("p (h w) -> p h w", h=2),
                        PS_HB[:, 128:256].rearrange("p (h w) -> p h w", h=2),
                        KSM[:, 64 + 2 * fc:64 + 2 * fc + 2].rearrange("p (h o) -> p h o", o=1)
                        .broadcast_to((128, 2, 64)), ALU.mult))
                pr.dma("sync", lambda e, c=c: e.dma_start(out=d_xt[c], in_=XTI[:, :]))
                pr.add("vector", lambda e: e.tensor_tensor(
                    X2T[:, :].rearrange("p (h w) -> p h w", h=64),
                    XTI[:, :].rearrange("p (h w) -> p h w", h=64),
                    KSM[:, 0:64].rearrange("p (h o) -> p h o", o=1).broadcast_to((128, 64, 64)),
                    ALU.mult))
                for jj in range(8):
                    pr.add("tensor", lambda e, jj=jj: e.matmul(
                        PSF[jj % 2][:, 0:512], KSM[:, 128:256],
                        X2T[:, jj * 512:(jj + 1) * 512], start=True, stop=True))
                    pr.add("vector", lambda e, jj=jj: e.tensor_copy(
                        RUNA[:, 8192 + jj * 512:8192 + (jj + 1) * 512], PSF[jj % 2][:, 0:512]))
                pr.dma("sync", lambda e, c=c: e.dma_start(out=d_sin[c], in_=RUNA[:, 0:4096]))
                for jj in range(8):
                    pr.add("tensor", lambda e, jj=jj: e.matmul(
                        PSF[2 + jj % 2][:, 0:512], ONESB[0:1, :],
                        ROWB[0:1, jj * 8:jj * 8 + 8].rearrange("a (h o) -> a h o", o=1)
                        .broadcast_to((1, 8, 64)), start=True, stop=True))
                    pr.add("vector", lambda e, jj=jj: e.tensor_copy(
                        RUNA[:, 4096 + jj * 512:4096 + (jj + 1) * 512], PSF[2 + jj % 2][:, 0:512]))
                pr.add("vector", lambda e: e.tensor_tensor(
                    RUNA[:, 0:4096], RUNA[:, 0:4096], RUNA[:, 4096:8192], ALU.mult))
                pr.add("vector", lambda e: e.tensor_tensor(
                    RUNA[:, 0:4096], RUNA[:, 0:4096], RUNA[:, 8192:12288], ALU.add))

        pr.dma("sync", lambda e: e.dma_start(
            out=g_st_in.rearrange("a (n hp) -> (a n) hp", n=128), in_=RUNA[:, 0:4096]))
        pr.dma("sync", lambda e: e.dma_start(
            out=g_T_in.rearrange("a (h o) -> (a h) o", o=1), in_=TSEG[:, 0:1]))
        pr.async_op("gpsimd", lambda e: e.collective_compute(
            "AllGather", ALU.bypass, replica_groups=[list(range(NCORE))],
            ins=[g_st_in.opt()], outs=[g_st.opt()]), cc2)
        pr.async_op("gpsimd", lambda e: e.collective_compute(
            "AllGather", ALU.bypass, replica_groups=[list(range(NCORE))],
            ins=[g_T_in.opt()], outs=[g_T.opt()]), cc3)
        if debug:
            pr.dma("gpsimd", lambda e: [e.dma_start(out=dbg["dbg_sin"][c], in_=d_sin[c])
                                      for c in range(NCH)], n=NCH)

        QKH = A2b
        VT = A3
        WQ3 = A4
        X2S = A1
        OT2 = A5
        OT1 = A1
        PT = A7[:, 0:1024]
        OEV = A7[:, 1024:2048]
        pr.dma("gpsimd", lambda e: [e.dma_start(
            out=WQ3[:, i * 4096:(i + 1) * 4096].rearrange("q (ic o) -> q ic o", o=256),
            in_=w.rearrange("(ic q) o -> q ic o", q=128))
            for i, w in enumerate((w_qT, w_kT, w_vT))], n=3, pre=[(cc1, 1)])

        for qb in range(16):
            src_core = qb // 2
            tloc = (qb % 2) * 512
            pr.dma("sync", lambda e, src_core=src_core, tloc=tloc: [e.dma_start(
                out=X2S[:, ic * 512:(ic + 1) * 512],
                in_=g_xn2[src_core * D + ic * 128: src_core * D + (ic + 1) * 128, tloc:tloc + 512])
                for ic in range(16)], n=16)
            for wi in range(2):
                for hc in range(2):
                    def mm_qk(e, wi=wi, hc=hc):
                        return [e.matmul(PSF[wi * 2 + hc][:, 0:512],
                                         WQ3[:, wi * 4096 + ic * 256 + hc * 128:
                                             wi * 4096 + ic * 256 + hc * 128 + 128],
                                         X2S[:, ic * 512:(ic + 1) * 512],
                                         start=(ic == 0), stop=(ic == 15)) for ic in range(16)]
                    pr.add("tensor", mm_qk, n=16)
                    off = wi * 16384 + hc * 8192 + qb * 512
                    pr.add("vector", lambda e, wi=wi, hc=hc, off=off: e.tensor_copy(
                        QKH[:, off:off + 512], PSF[wi * 2 + hc][:, 0:512]))
            for t4 in range(4):
                def mm_v(e, t4=t4):
                    return [e.matmul(PSF[4 + t4 % 2][:, 0:256],
                                     X2S[:, ic * 512 + t4 * 128: ic * 512 + t4 * 128 + 128],
                                     WQ3[:, 2 * 4096 + ic * 256: 2 * 4096 + (ic + 1) * 256],
                                     start=(ic == 0), stop=(ic == 15)) for ic in range(16)]
                pr.add("tensor", mm_v, n=16)
                kb = qb * 4 + t4
                pr.add("vector", lambda e, kb=kb, t4=t4: e.tensor_copy(
                    VT[:, kb * 256:kb * 256 + 256], PSF[4 + t4 % 2][:, 0:256]))

        for h in range(2):
            OST = OT1[:, 8192:16384] if h == 1 else OT2[:, 0:8192]
            for qb in range(16):
                nkb = 4 * qb + 4
                for kb in range(nkb):
                    sbank = PSF[kb % 2]
                    pr.add("tensor", lambda e, h=h, qb=qb, kb=kb, sbank=sbank: e.matmul(
                        sbank[:, 0:512],
                        QKH[:, 16384 + h * 8192 + kb * 128: 16384 + h * 8192 + kb * 128 + 128],
                        QKH[:, h * 8192 + qb * 512: h * 8192 + qb * 512 + 512],
                        start=True, stop=True))
                    pt = PT[:, (kb % 2) * 512:(kb % 2) * 512 + 512]
                    pr.add("scalar", lambda e, pt=pt, sbank=sbank: e.activation(
                        pt, sbank[:, 0:512], AF.Exp, bias=SHFT[:, 0:1], scale=ASCALE))
                    j = kb - 4 * qb
                    if j >= 0:
                        pr.add("vector", lambda e, pt=pt, j=j: e.tensor_tensor(
                            pt, pt, AMK[:, j * 512:(j + 1) * 512], ALU.mult))
                    pr.add("tensor", lambda e, h=h, kb=kb, pt=pt, f=(kb == 0), l=(kb == nkb - 1):
                           e.matmul(PSF[2][:, 0:512],
                                    VT[:, kb * 256 + h * 128: kb * 256 + h * 128 + 128],
                                    pt, start=f, stop=l))
                    pr.add("tensor", lambda e, pt=pt, f=(kb == 0), l=(kb == nkb - 1):
                           e.matmul(PSF[3][0:1, 0:512], ONESCB[:, 0:1], pt, start=f, stop=l))
                pr.add("vector", lambda e: e.reciprocal(ROWS[0:1, 128:640], PSF[3][0:1, 0:512]))
                pr.add("vector", lambda e: e.tensor_copy(ROWB[0:1, 128:640], ROWS[0:1, 128:640]))
                pr.add("tensor", lambda e: e.matmul(PSF[4][:, 0:512], ONESB[0:1, :],
                                                    ROWB[0:1, 128:640], start=True, stop=True))
                pr.add("vector", lambda e: e.tensor_copy(OEV[:, 0:512], PSF[4][:, 0:512]))
                pr.add("vector", lambda e, OST=OST, qb=qb: e.tensor_tensor(
                    OST[:, qb * 512:(qb + 1) * 512], PSF[2][:, 0:512], OEV[:, 0:512], ALU.mult))
        if debug:
            pr.dma("gpsimd", lambda e: e.dma_start(out=dbg["dbg_att"][0:128, :], in_=OT2[:, 0:8192]))
            pr.dma("gpsimd", lambda e: e.dma_start(out=dbg["dbg_att"][128:256, :], in_=OT1[:, 8192:16384]))

        WOB = WQ3[:, 12288:16384]
        RSE = SMF[:, 0:512]
        for osl in range(4):
            pr.dma("gpsimd", lambda e, osl=osl: [e.dma_start(
                out=WOB[:, hc * 2048 + 0:hc * 2048 + 512],
                in_=w_oTs[hc * 128:(hc + 1) * 128, osl * 512:(osl + 1) * 512])
                for hc in range(2)], n=2)
            for tt in range(64):
                def mm_wo(e, osl=osl, tt=tt):
                    ins = []
                    for hc in range(2):
                        lhs = (OT2[:, 0:8192] if hc == 0 else OT1[:, 8192:16384])
                        ins.append(e.matmul(
                            PSF[4 + tt % 2][:, 0:512],
                            lhs[:, tt * 128:(tt + 1) * 128],
                            WOB[:, hc * 2048:hc * 2048 + 512],
                            start=(hc == 0), stop=(hc == 1)))
                    return ins
                pr.add("tensor", mm_wo, n=2)
                pr.add("vector", lambda e, tt=tt: e.tensor_copy(RSE[:, 0:512], PSF[4 + tt % 2][:, 0:512]))
                pr.dma("sync", lambda e, osl=osl, tt=tt: e.dma_start(
                    out=rs_in[tt * 128:(tt + 1) * 128, osl * 512:(osl + 1) * 512],
                    in_=RSE[:, 0:512]))
        pr.async_op("gpsimd", lambda e: e.collective_compute(
            "ReduceScatter", ALU.add, replica_groups=[list(range(NCORE))],
            ins=[rs_in.opt()], outs=[rs_out.opt()]), cc4)

        PREF = A3
        pr.dma("sync", lambda e: e.dma_start(
            out=ROWS[0:1, 0:512], in_=g_T.rearrange("(o c) h -> o (c h)", o=1)),
            pre=[(cc2, 1), (cc3, 1)])
        pr.add("vector", lambda e: e.memset(PREF[:, 0:4096], 0.0))
        for k in range(NCORE - 1):
            pr.dma("sync", lambda e, k=k: e.dma_start(
                out=PREF[:, 4096:8192], in_=g_st[k].rearrange("(n hp) -> n hp", n=128)))
            pr.add("vector", lambda e, k=k: e.tensor_scalar(
                ROWS[0:1, 640:704], ROWS[0:1, k * 64:(k + 1) * 64],
                GATES[0:1, k:k + 1], None, ALU.mult))
            pr.add("scalar", lambda e: e.activation(ROWS[0:1, 704:768], ROWS[0:1, 640:704], AF.Exp))
            pr.add("vector", lambda e: e.tensor_copy(ROWB[0:1, 640:704], ROWS[0:1, 704:768]))
            for jj in range(8):
                pr.add("tensor", lambda e, jj=jj: e.matmul(
                    PSF[jj % 2][:, 0:512], ONESB[0:1, :],
                    ROWB[0:1, 640 + jj * 8:640 + jj * 8 + 8].rearrange("a (h o) -> a h o", o=1)
                    .broadcast_to((1, 8, 64)), start=True, stop=True))
                pr.add("vector", lambda e, jj=jj: e.tensor_copy(
                    PREF[:, 8192 + jj * 512:8192 + (jj + 1) * 512], PSF[jj % 2][:, 0:512]))
            pr.add("vector", lambda e: e.tensor_tensor(
                PREF[:, 0:4096], PREF[:, 0:4096], PREF[:, 8192:12288], ALU.mult))
            pr.add("vector", lambda e, k=k: e.scalar_tensor_tensor(
                PREF[:, 0:4096], PREF[:, 4096:8192], GREP[:, k:k + 1],
                PREF[:, 0:4096], ALU.mult, ALU.add))

        pr.dma("sync", lambda e: e.dma_start(out=ROWS[0:1, 0:512], in_=d_lam.rearrange("(o c) h -> o (c h)", o=1)))
        pr.add("vector", lambda e: e.memset(ROWS[0:1, 512:576], 0.0))

        BCA = A2
        MTC = A5
        STOT = A2b[:, 16384:20480]
        SINB = A2b[:, 20480:24576]
        NACT = sb("NACT", [128, 64], F32)
        GMTC = sb("GMTC", [128, 128], F32)
        CFMC = sb("CFMC", [128, 128], BF16)
        XTC = A4
        SGD = A4[:, 8192:9216]

        for c in range(NCH):
            pr.dma("sync", lambda e, c=c: e.dma_start(out=SINB[:, :], in_=d_sin[c]))
            pr.add("scalar", lambda e: e.activation(ROWS[0:1, 576:640], ROWS[0:1, 512:576], AF.Exp))
            pr.add("vector", lambda e: e.tensor_copy(ROWB[0:1, 576:640], ROWS[0:1, 576:640]))
            for jj in range(8):
                pr.add("tensor", lambda e, jj=jj: e.matmul(
                    PSF[jj % 2][:, 0:512], ONESB[0:1, :],
                    ROWB[0:1, 576 + jj * 8:576 + jj * 8 + 8].rearrange("a (h o) -> a h o", o=1)
                    .broadcast_to((1, 8, 64)), start=True, stop=True))
                pr.add("vector", lambda e, jj=jj: e.tensor_tensor(
                    STOT[:, jj * 512:(jj + 1) * 512], PSF[jj % 2][:, 0:512],
                    PREF[:, jj * 512:(jj + 1) * 512], ALU.mult))
            pr.add("vector", lambda e: e.tensor_tensor(STOT[:, :], STOT[:, :], SINB[:, :], ALU.add))
            pr.add("vector", lambda e, c=c: e.tensor_add(
                ROWS[0:1, 512:576], ROWS[0:1, 512:576], ROWS[0:1, c * 64:(c + 1) * 64]))
            if debug:
                pr.dma("gpsimd", lambda e, c=c: e.dma_start(out=dbg["dbg_stot"][c], in_=STOT[:, :]))

            pr.dma("sync", lambda e, c=c: e.dma_start(out=NACT[:, :], in_=d_nacT[c]))
            pr.dma("sync", lambda e, c=c: e.dma_start(out=GMTC[:, :], in_=d_gmt[c]))
            pr.dma("sync", lambda e, c=c: e.dma_start(out=CFMC[:, :], in_=d_Cfm[c]))
            pr.dma("sync", lambda e, c=c: e.dma_start(out=XTC[:, 0:4096], in_=d_xt[c]))

            for half in range(2):
                pr.dma("gpsimd", lambda e, c=c, half=half: e.dma_start(
                    out=BCA[:, 0:4096],
                    in_=d_acum[c].rearrange("(o h) l -> o (h l)", o=1)[0:1, half * 4096:(half + 1) * 4096]
                    .broadcast_to((128, 4096))))
                for hh in range(32):
                    h = half * 32 + hh
                    pr.add("vector", lambda e, hh=hh, h=h: e.tensor_scalar(
                        BCA[:, 4096 + hh * 128:4096 + (hh + 1) * 128],
                        BCA[:, hh * 128:(hh + 1) * 128],
                        NACT[:, h:h + 1], 0.0, ALU.add, ALU.min))
                pr.add("scalar", lambda e: e.activation(BCA[:, 4096:8192], BCA[:, 4096:8192], AF.Exp))
                pr.add("scalar", lambda e: e.activation(BCA[:, 0:4096], BCA[:, 0:4096], AF.Exp))
                pr.add("vector", lambda e: e.tensor_tensor(
                    MTC[:, 0:4096].rearrange("p (h l) -> p h l", l=128),
                    BCA[:, 4096:8192].rearrange("p (h l) -> p h l", l=128),
                    GMTC[:, :].rearrange("p (o l) -> p o l", o=1).broadcast_to((128, 32, 128)),
                    ALU.mult))
                pr.add("vector", lambda e: e.tensor_tensor(
                    MTC[:, 4096:8192].rearrange("p (h l) -> p h l", l=128),
                    BCA[:, 0:4096].rearrange("p (h l) -> p h l", l=128),
                    CFMC[:, :].rearrange("p (o l) -> p o l", o=1).broadcast_to((128, 32, 128)),
                    ALU.mult))
                for rnd in range(4):
                    for pair in range(4):
                        h0 = half * 32 + rnd * 8 + pair * 2
                        bank = PSF[pair]
                        for i in range(2):
                            h = h0 + i
                            hh = h - half * 32
                            tp = (0, 64) if i == 1 else None
                            oap = bank[64:128, 0:128] if i == 1 else bank[0:64, 0:128]
                            pr.add("tensor", lambda e, h=h, hh=hh, tp=tp, oap=oap: [
                                e.matmul(oap, XTC[:, h * 64:(h + 1) * 64],
                                         MTC[:, hh * 128:(hh + 1) * 128],
                                         start=True, stop=False, tile_position=tp),
                                e.matmul(oap, STOT[:, h * 64:(h + 1) * 64],
                                         MTC[:, 4096 + hh * 128:4096 + (hh + 1) * 128],
                                         start=False, stop=True, tile_position=tp)], n=2)
                    grp = half * 4 + rnd
                    pr.dma("sync", lambda e, grp=grp, c=c: [e.dma_start(
                        out=SGD[:, q * 128:(q + 1) * 128],
                        in_=d_sg[grp * 4 + q, :, c * 128:(c + 1) * 128]) for q in range(4)], n=4)
                    for pair in range(4):
                        fcg = grp * 4 + pair
                        pr.add("vector", lambda e, pair=pair, fcg=fcg: e.tensor_tensor(
                            BCA.ap().bitcast(BF16)[:, 24576 + fcg * 128:24576 + (fcg + 1) * 128]
                            if False else
                            BCA[:, 12288 + fcg * 128:12288 + (fcg + 1) * 128],
                            PSF[pair][:, 0:128], SGD[:, pair * 128:(pair + 1) * 128], ALU.mult))
                    pr.dma("sync", lambda e, grp=grp, c=c: [e.dma_start(
                        out=SGD[:, 512 + q * 128:512 + (q + 1) * 128],
                        in_=d_dres[grp * 4 + q, :, c * 128:(c + 1) * 128]) for q in range(4)], n=4)
                    for pair in range(4):
                        fcg = grp * 4 + pair
                        pr.add("vector", lambda e, pair=pair, fcg=fcg: e.tensor_tensor(
                            BCA[:, 12288 + fcg * 128:12288 + (fcg + 1) * 128],
                            BCA[:, 12288 + fcg * 128:12288 + (fcg + 1) * 128],
                            SGD[:, 512 + pair * 128:512 + (pair + 1) * 128], ALU.add))
            if debug:
                pr.add("vector", lambda e: e.tensor_copy(XTC[:, 4096:8192], BCA[:, 12288:16384]))
                pr.dma("gpsimd", lambda e, c=c: e.dma_start(out=dbg["dbg_yg"][c], in_=XTC[:, 4096:8192]))

            pr.add("scalar", lambda e: e.activation(BCA[:, 0:4096], BCA[:, 12288:16384], AF.Square))

            def rms_mm(e):
                return [e.matmul(PSF[4][0:1, 0:128], ONESCF[:, 0:1],
                                 BCA[:, fcg * 128:(fcg + 1) * 128],
                                 start=(fcg == 0), stop=(fcg == 31)) for fcg in range(32)]
            pr.add("tensor", rms_mm, n=32)
            pr.add("vector", lambda e: e.tensor_scalar(
                ROWS[0:1, 640:768], PSF[4][0:1, 0:128], 1.0 / INTER, EPS, ALU.mult, ALU.add))
            pr.add("scalar", lambda e: e.activation(ROWS[0:1, 640:768], ROWS[0:1, 640:768], AF.Sqrt))
            pr.add("vector", lambda e: e.reciprocal(ROWS[0:1, 768:896], ROWS[0:1, 640:768]))
            pr.add("vector", lambda e: e.tensor_copy(ROWB[0:1, 768:896], ROWS[0:1, 768:896]))
            pr.add("tensor", lambda e: e.matmul(PSF[5][:, 0:128], ONESB[0:1, :],
                                                ROWB[0:1, 768:896], start=True, stop=True))
            pr.add("vector", lambda e: e.tensor_copy(SMF[:, 0:128], PSF[5][:, 0:128]))
            pr.add("vector", lambda e: e.tensor_tensor(
                BCA[:, 12288:16384].rearrange("p (g l) -> p g l", l=128),
                BCA[:, 12288:16384].rearrange("p (g l) -> p g l", l=128),
                SMF[:, 0:128].rearrange("p (o l) -> p o l", o=1).broadcast_to((128, 32, 128)),
                ALU.mult))
            pr.add("vector", lambda e: e.tensor_tensor(
                XTC[:, 4096:8192].rearrange("p (g l) -> p g l", l=128),
                BCA[:, 12288:16384].rearrange("p (g l) -> p g l", l=128),
                NWCOL[:, :].rearrange("p (g o) -> p g o", o=1).broadcast_to((128, 32, 128)),
                ALU.mult))
            pr.dma("sync", lambda e, c=c: e.dma_start(out=d_yn[c], in_=XTC[:, 4096:8192]))

        WOPX = A3
        YNL = A2b[:, 0:8192]
        STM = A4[:, 0:512]
        for osl in range(4):
            pr.dma("gpsimd", lambda e, osl=osl: [e.dma_start(
                out=WOPX[:, fcg * 512:(fcg + 1) * 512],
                in_=w_opT[fcg * 128:(fcg + 1) * 128, osl * 512:(osl + 1) * 512])
                for fcg in range(32)], n=32)
            for c in range(NCH):
                pr.dma("gpsimd", lambda e, c=c: e.dma_start(
                    out=YNL[:, (c % 2) * 4096:(c % 2) * 4096 + 4096], in_=d_yn[c]))

                def mm_op(e, c=c):
                    return [e.matmul(PSF[4 + c % 2][:, 0:512],
                                     YNL[:, (c % 2) * 4096 + fcg * 128:(c % 2) * 4096 + (fcg + 1) * 128],
                                     WOPX[:, fcg * 512:(fcg + 1) * 512],
                                     start=(fcg == 0), stop=(fcg == 31)) for fcg in range(32)]
                pr.add("tensor", mm_op, n=32)
                pr.add("vector", lambda e, c=c: e.tensor_copy(STM[:, 0:512], PSF[4 + c % 2][:, 0:512]))
                pr.dma("sync", lambda e, c=c, osl=osl: e.dma_start(
                    out=d_s[c, :, osl * 512:(osl + 1) * 512], in_=STM[:, 0:512]))
        if debug:
            pr.dma("gpsimd", lambda e: [e.dma_start(out=dbg["dbg_s"][c], in_=d_s[c])
                                      for c in range(NT)], n=NT)

        XFNT = A3
        XNT = A1
        XN3F = A5
        G1 = A2b
        WSLB = A4
        XNFR = A2b[:, 0:16384]
        DWN = sb("DWN", [32, 512], BF16)
        O2ALL = A5
        MC = sb("MC", [128, 2], F32)
        GT = A7[:, 0:2048]
        STL = A4[:, 0:2048]

        for t in range(NT):
            xs = A2[:, 0:2048]
            xn = A2[:, 2048:4096]
            rst = A2[:, 4096:6144]
            scr = A2[:, 6144:8192]
            pr.dma("sync", lambda e, t=t, xs=xs: e.dma_start(
                out=xs, in_=x_own[t * 128:(t + 1) * 128, :]))
            pr.dma("sync", lambda e, t=t, rst=rst: e.dma_start(
                out=rst, in_=rs_out[t * 128:(t + 1) * 128, :]),
                pre=[(cc4, 1)] if t == 0 else None)
            pr.add("vector", lambda e, xs=xs: e.reduce_sum(COLS[:, 0:1], xs, axis=AXX))
            pr.add("vector", lambda e: e.tensor_scalar_mul(COLS[:, 1:2], COLS[:, 0:1], -1.0 / D))
            pr.add("scalar", lambda e, xs=xs, scr=scr: e.activation(
                scr, xs, AF.Square, bias=COLS[:, 1:2], accum_out=COLS[:, 2:3]))
            pr.add("vector", lambda e: e.tensor_scalar(COLS[:, 3:4], COLS[:, 2:3], 1.0 / D, EPS, ALU.mult, ALU.add))
            pr.add("scalar", lambda e: e.activation(COLS[:, 4:5], COLS[:, 3:4], AF.Sqrt))
            pr.add("vector", lambda e: e.reciprocal(COLS[:, 5:6], COLS[:, 4:5]))
            pr.add("vector", lambda e, xs=xs, xn=xn: e.tensor_scalar(
                xn, xs, COLS[:, 1:2], COLS[:, 5:6], ALU.add, ALU.mult))
            pr.add("vector", lambda e, t=t, xn=xn: e.tensor_copy(
                XNT[:, t * 2048:(t + 1) * 2048], xn))
            pr.add("vector", lambda e, t=t, xn=xn, rst=rst: e.tensor_tensor(
                XFNT[:, t * 2048:(t + 1) * 2048], xn, rst, ALU.add))
        if debug:
            pr.dma("sync", lambda e: e.dma_start(out=dbg["dbg_rs"], in_=rs_out))

        for tb in range(2):
            for ti in range(4):
                t = tb * 4 + ti
                xf = A2[:, 0:2048]
                o3 = A2[:, 2048:4096]
                scr = A2[:, 6144:8192]
                pr.add("vector", lambda e, t=t, xf=xf: e.tensor_copy(
                    xf, XFNT[:, t * 2048:(t + 1) * 2048]))
                pr.add("vector", lambda e, xf=xf: e.reduce_sum(COLS[:, 0:1], xf, axis=AXX))
                pr.add("vector", lambda e: e.tensor_scalar_mul(COLS[:, 1:2], COLS[:, 0:1], -1.0 / D))
                pr.add("scalar", lambda e, xf=xf, scr=scr: e.activation(
                    scr, xf, AF.Square, bias=COLS[:, 1:2], accum_out=COLS[:, 2:3]))
                pr.add("vector", lambda e: e.tensor_scalar(COLS[:, 3:4], COLS[:, 2:3], 1.0 / D, EPS, ALU.mult, ALU.add))
                pr.add("scalar", lambda e: e.activation(COLS[:, 4:5], COLS[:, 3:4], AF.Sqrt))
                pr.add("vector", lambda e: e.reciprocal(COLS[:, 5:6], COLS[:, 4:5]))
                pr.add("vector", lambda e, xf=xf, o3=o3: e.tensor_scalar(
                    o3, xf, COLS[:, 1:2], COLS[:, 5:6], ALU.add, ALU.mult))
                for fc in range(16):
                    pr.add("tensor", lambda e, fc=fc, o3=o3: e.transpose(
                        PSF[6][:, 0:128], o3[:, fc * 128:(fc + 1) * 128], IDf[:, :]))
                    pr.add("vector", lambda e, fc=fc, ti=ti: e.tensor_scalar(
                        XN3F[:, fc * 512 + ti * 128: fc * 512 + ti * 128 + 128],
                        PSF[6][:, 0:128], L2S[:, fc:fc + 1], L2B[:, fc:fc + 1],
                        ALU.mult, ALU.add))
            pr.dma("sync", lambda e, tb=tb: [e.dma_start(
                out=XNFR[:, ic * 512:(ic + 1) * 512],
                in_=d_xnfm[ic, :, tb * 512:(tb + 1) * 512]) for ic in range(16)], n=16)

            def mm_dn(e):
                return [e.matmul(PSF[6][0:32, 0:512], WDT[:, ic * 32:(ic + 1) * 32],
                                 XNFR[:, ic * 512:(ic + 1) * 512],
                                 start=(ic == 0), stop=(ic == 15)) for ic in range(16)]
            pr.add("tensor", mm_dn, n=16)
            pr.add("vector", lambda e: e.tensor_copy(DWN[:, :], PSF[6][0:32, 0:512]))

            for fg in range(16):
                buf = fg % 2
                Wb = WSLB[:, buf * 8192: buf * 8192 + 8192]
                pr.dma("gpsimd", lambda e, fg=fg, Wb=Wb: [e.dma_start(
                    out=Wb[:, j * 2048:(j + 1) * 2048].rearrange("q (ic o) -> q ic o", o=128),
                    in_=w_f1T.rearrange("(ic q) o -> q ic o", q=128)
                        [:, :, (fg * 4 + j) * 128:(fg * 4 + j + 1) * 128])
                    for j in range(4)], n=4)
                for j in range(4):
                    def mm_f1(e, j=j, Wb=Wb):
                        return [e.matmul(PSF[j][:, 0:512],
                                         Wb[:, j * 2048 + ic * 128: j * 2048 + (ic + 1) * 128],
                                         XN3F[:, ic * 512:(ic + 1) * 512],
                                         start=(ic == 0), stop=(ic == 15)) for ic in range(16)]
                    pr.add("tensor", mm_f1, n=16)
                for j in range(4):
                    ffc = fg * 4 + j
                    pr.add("scalar", lambda e, j=j, ffc=ffc: e.activation(
                        G1[:, ffc * 512:(ffc + 1) * 512], PSF[j][:, 0:512], AF.Gelu_apprx_tanh))

            for osl in range(4):
                for sl8 in range(8):
                    buf2 = sl8 % 2
                    W2 = WSLB[:, buf2 * 8192: buf2 * 8192 + 4096]
                    pr.dma("gpsimd", lambda e, osl=osl, sl8=sl8, W2=W2: [e.dma_start(
                        out=W2[:, q * 512:(q + 1) * 512],
                        in_=w_f2T[(sl8 * 8 + q) * 128:(sl8 * 8 + q + 1) * 128,
                                  osl * 512:(osl + 1) * 512]) for q in range(8)], n=8)
                    for ti in range(4):
                        def mm_f2(e, ti=ti, sl8=sl8, W2=W2):
                            return [e.matmul(PSF[ti][:, 0:512],
                                             G1[:, (sl8 * 8 + q) * 512 + ti * 128:
                                                (sl8 * 8 + q) * 512 + ti * 128 + 128],
                                             W2[:, q * 512:(q + 1) * 512],
                                             start=(sl8 == 0 and q == 0),
                                             stop=(sl8 == 7 and q == 7)) for q in range(8)]
                        pr.add("tensor", mm_f2, n=8)
                for ti in range(4):
                    t = tb * 4 + ti
                    pr.add("vector", lambda e, t=t, ti=ti, osl=osl: e.tensor_tensor(
                        O2ALL[:, ti * 2048 + osl * 512: ti * 2048 + osl * 512 + 512],
                        PSF[ti][:, 0:512],
                        XFNT[:, t * 2048 + osl * 512: t * 2048 + osl * 512 + 512],
                        ALU.add))

            WUT = WSLB[0:32, 2048:4096]
            pr.dma("gpsimd", lambda e, WUT=WUT: e.dma_start(out=WUT, in_=w_uT))

            for ti in range(4):
                t = tb * 4 + ti
                xs = A2[:, 0:2048]
                sg = A2[:, 2048:4096]
                of = A2[:, 4096:6144]
                f1 = A2[:, 6144:6656]
                f2 = A2[:, 6656:7168]
                f3 = A2[:, 7168:7680]
                pr.dma("sync", lambda e, t=t: e.dma_start(
                    out=MC[:, 0:1], in_=d_m.rearrange("a (t o) -> a t o", o=1)[0, t * 128:(t + 1) * 128]))
                pr.dma("sync", lambda e, t=t: e.dma_start(
                    out=MC[:, 1:2], in_=d_m.rearrange("a (t o) -> a t o", o=1)[1, t * 128:(t + 1) * 128]))
                pr.dma("sync", lambda e, t=t: e.dma_start(out=STL[:, :], in_=d_s[t]))
                pr.dma("sync", lambda e, t=t, xs=xs: e.dma_start(
                    out=xs, in_=x_own[t * 128:(t + 1) * 128, :]))
                for osl in range(4):
                    co = osl * 512
                    pr.add("tensor", lambda e, ti=ti, osl=osl: e.matmul(
                        PSF[5][:, 0:512], DWN[:, ti * 128:(ti + 1) * 128],
                        WUT[:, osl * 512:(osl + 1) * 512], start=True, stop=True))
                    pr.add("vector", lambda e, t=t, co=co, f1=f1: e.tensor_tensor(
                        f1, XNT[:, t * 2048 + co: t * 2048 + co + 512],
                        BREP[:, co:co + 512], ALU.add))
                    pr.add("vector", lambda e, ti=ti, co=co, f1=f1, f2=f2: e.tensor_tensor(
                        f2, O2ALL[:, ti * 2048 + co: ti * 2048 + co + 512], f1, ALU.subtract))
                    pr.add("vector", lambda e, f1=f1, f2=f2, f3=f3: e.scalar_tensor_tensor(
                        f3, f2, MC[:, 1:2], f1, ALU.mult, ALU.add))
                    pr.add("vector", lambda e, co=co, f2=f2: e.tensor_tensor(
                        f2, PSF[5][:, 0:512], BREP[:, co:co + 512], ALU.subtract))
                    pr.add("vector", lambda e, f2=f2, f3=f3: e.scalar_tensor_tensor(
                        f3, f2, MC[:, 0:1], f3, ALU.mult, ALU.add))
                    pr.add("scalar", lambda e, co=co, f3=f3: e.activation(
                        GT[:, co:co + 512], f3, AF.Silu))
                    if debug:
                        pr.add("vector", lambda e, co=co, f3=f3: e.tensor_copy(
                            A2[:, 8192 + co:8192 + co + 512], f3))
                if debug:
                    pr.dma("sync", lambda e, t=t: e.dma_start(
                        out=dbg["dbg_ffn"][t], in_=A2[:, 8192:10240]))
                pr.add("vector", lambda e, sg=sg: e.tensor_tensor(sg, STL[:, :], GT[:, :], ALU.mult))
                pr.add("vector", lambda e, xs=xs, sg=sg, of=of: e.tensor_tensor(of, sg, xs, ALU.add))
                pr.dma("sync", lambda e, t=t, of=of: e.dma_start(
                    out=out[t * 128:(t + 1) * 128, :], in_=of))

        pr.add("sync", lambda e: e.nop())

        with nc.Block() as block:
            pr.realize(block, sem)

    return nc


def _prep_inputs(inputs):
    f32 = lambda a: np.ascontiguousarray(np.asarray(a, np.float32))
    x = f32(inputs["L0_attn_input_states"]).reshape(S, D)
    wip = f32(inputs["L0_attn_in_proj_w"])
    convw = f32(inputs["L0_attn_conv_w"]).reshape(CONV_DIM, 4)
    convb = f32(inputs["L0_attn_conv_b"])
    alog = f32(inputs["L0_attn_A_log"])
    dtb = f32(inputs["L0_attn_dt_bias"])
    dvec = f32(inputs["L0_attn_D"])
    nw = f32(inputs["L0_attn_norm_w"])
    wop = f32(inputs["L0_attn_out_proj_w"])
    sb_ = f32(inputs["L0_ffn_simple_bias"])
    wd = f32(inputs["L0_ffn_w_down"])
    wu = f32(inputs["L0_ffn_w_up"])
    wr = f32(inputs["L0_ffn_w_router"])
    l1s = f32(inputs["L0_ffn_ln1_scale"]); l1b = f32(inputs["L0_ffn_ln1_bias"])
    wq = f32(inputs["L0_ffn_wq"]); wk = f32(inputs["L0_ffn_wk"])
    wv = f32(inputs["L0_ffn_wv"]); wo = f32(inputs["L0_ffn_wo"])
    l2s = f32(inputs["L0_ffn_ln2_scale"]); l2b = f32(inputs["L0_ffn_ln2_bias"])
    wf1 = f32(inputs["L0_ffn_w_ff1"]); wf2 = f32(inputs["L0_ffn_w_ff2"])

    identf = np.eye(128, dtype=np.float32)
    mask_sle = (np.arange(128)[:, None] <= np.arange(128)[None, :]).astype(np.float32)
    amask = np.zeros((4, 128, 512), np.float32)
    for j in range(4):
        amask[j] = (np.arange(128)[:, None] + 128 * j <= np.arange(512)[None, :])
    onesf = np.ones((1, 128), np.float32)

    shared = dict(
        w_ipT=np.ascontiguousarray(wip.T),
        conv_w=convw, conv_b=convb, a_log=alog, dt_bias=dtb, d_vec=dvec,
        norm_w=nw, w_opT=np.ascontiguousarray(wop.T),
        sbias=sb_, w_dT=np.ascontiguousarray(wd.T), w_uT=np.ascontiguousarray(wu.T),
        w_rT=np.ascontiguousarray(wr.T),
        ln1_s=l1s, ln1_b=l1b, ln2_s=l2s, ln2_b=l2b,
        w_f1T=np.ascontiguousarray(wf1.T), w_f2T=np.ascontiguousarray(wf2.T),
        identf=identf, identb_src=identf, mask_sle=mask_sle, amask=amask,
        onesf=onesf,
    )
    woT = np.ascontiguousarray(wo.T)
    in_maps = []
    for c in range(NCORE):
        m = dict(shared)
        m["x_own"] = np.ascontiguousarray(x[c * TOK:(c + 1) * TOK])
        if c == 0:
            m["x_halo"] = np.zeros((3, D), np.float32)
        else:
            m["x_halo"] = np.ascontiguousarray(x[c * TOK - 3:c * TOK])
        hsl = slice(2 * c * DH, (2 * c + 2) * DH)
        m["w_qT"] = np.ascontiguousarray(wq[hsl].T)
        m["w_kT"] = np.ascontiguousarray(wk[hsl].T)
        m["w_vT"] = np.ascontiguousarray(wv[hsl].T)
        m["w_oTs"] = np.ascontiguousarray(woT[hsl])
        g = np.zeros((1, NCORE), np.float32)
        g[0, :c] = 1.0
        m["gates"] = g
        in_maps.append(m)
    return in_maps


def kernel(**inputs):
    if "nc" not in _CACHE:
        _CACHE["nc"] = build_program(debug=False)
    nc = _CACHE["nc"]
    in_maps = _prep_inputs(inputs)
    res = run_bass_kernel_spmd(nc, in_maps, core_ids=list(range(NCORE)))
    outp = np.concatenate([np.asarray(r["out"], np.float32) for r in res.results], axis=0)
    return outp.reshape(1, S, D)


# version 03
# speedup vs baseline: 1.0992x; 1.0992x over previous
import numpy as np
import ml_dtypes
import concourse.bass as bass
import concourse.mybir as mybir
from concourse.bass_utils import run_bass_kernel_spmd

F32 = mybir.dt.float32
BF16 = mybir.dt.bfloat16
AF = mybir.ActivationFunctionType
ALU = mybir.AluOpType
AXX = mybir.AxisListType.X

S, D = 8192, 2048
NCORE = 8
TOK = S // NCORE
NT = TOK // 128
CH = 128
NCH = TOK // CH
H, P, NS = 64, 64, 128
INTER, CONV_DIM, PROJ = 4096, 4352, 8512
NIC = 16
NOC = 67
HA, DH = 16, 128
FF = 8192
RNK = 32
EPS = 1e-5
ASHIFT = 40.0
ASCALE = float(1.0 / np.sqrt(DH))

_CACHE = {}


class Prog:

    LANES = ("sync", "gpsimd", "tensor", "vector", "scalar")

    def __init__(self):
        self.items = {e: [] for e in self.LANES}
        self.counts = {e: 0 for e in self.LANES}

    def mark(self):
        return dict(self.counts)

    def add(self, eng, fn, n=1, unit=1, pre=None, waits=None, self_wait=True):
        w = dict(waits) if waits is not None else self.mark()
        if self_wait:
            w[eng] = self.counts[eng]
        else:
            w.pop(eng, None)
        self.items[eng].append((w, fn, n, unit, pre))
        self.counts[eng] += n * unit

    def dma(self, eng, fn, n=1, pre=None, waits=None):
        self.add(eng, fn, n=n, unit=16, pre=pre, waits=waits)

    def async_op(self, eng, fn, done_sem, waits=None):
        w = dict(waits) if waits is not None else self.mark()
        w.pop(eng, None)
        self.items[eng].append((w, fn, 0, done_sem, None))

    def realize(self, block, sems):
        def mk(name):
            items = self.items[name]

            def body(e):
                reached = {l: 0 for l in self.LANES}
                for w, fn, n, unit, pre in items:
                    if pre is not None:
                        for psem, pval in pre:
                            e.wait_ge(psem, pval)
                    for lane, v in w.items():
                        if v > reached[lane]:
                            e.wait_ge(sems[lane], v)
                            reached[lane] = v
                    r = fn(e)
                    insts = r if isinstance(r, (list, tuple)) else [r]
                    if n == 0:
                        insts[-1].then_inc(unit, 1)
                    else:
                        assert len(insts) == n, f"{name}: got {len(insts)} want {n}"
                        for i in insts:
                            i.then_inc(sems[name], unit)
            return body
        block.sync(mk("sync"))
        block.gpsimd(mk("gpsimd"))
        block.tensor(mk("tensor"))
        block.vector(mk("vector"))
        block.scalar(mk("scalar"))


def build_program(debug=False):
    nc = bass.Bass("TRN2", target_bir_lowering=False)
    pr = Prog()

    def din(name, shape, dt=F32):
        return nc.dram_tensor(name, shape, dt, kind="ExternalInput").ap()

    def dint(name, shape, dt=F32):
        return nc.dram_tensor(name, shape, dt).ap()

    x_own = din("x_own", [TOK, D])
    x_halo = din("x_halo", [3, D])
    w_ipT = din("w_ipT", [D, PROJ])
    conv_w = din("conv_w", [CONV_DIM, 4])
    conv_b = din("conv_b", [CONV_DIM])
    a_log = din("a_log", [H])
    dt_bias = din("dt_bias", [H])
    d_vec = din("d_vec", [H])
    norm_w = din("norm_w", [INTER])
    w_opT = din("w_opT", [INTER, D])
    sbias = din("sbias", [D])
    w_dT = din("w_dT", [D, RNK])
    w_uT = din("w_uT", [RNK, D])
    w_rT = din("w_rT", [D, 3])
    ln1_s = din("ln1_s", [D])
    ln1_b = din("ln1_b", [D])
    w_qT = din("w_qT", [D, 2 * DH])
    w_kT = din("w_kT", [D, 2 * DH])
    w_vT = din("w_vT", [D, 2 * DH])
    w_oTs = din("w_oTs", [2 * DH, D])
    ln2_s = din("ln2_s", [D])
    ln2_b = din("ln2_b", [D])
    w_f1T = din("w_f1T", [D, FF])
    w_f2T = din("w_f2T", [FF, D])
    identf = din("identf", [128, 128])
    identb_src = din("identb_src", [128, 128])
    mask_sle = din("mask_sle", [128, 128])
    amask = din("amask", [4, 128, 512])
    gates = din("gates", [1, NCORE])
    onesf = din("onesf", [1, 128])
    out = nc.dram_tensor("out", [TOK, D], F32, kind="ExternalOutput").ap()

    g_xn2_in = dint("g_xn2_in", [D, TOK], BF16)
    g_xn2 = dint("g_xn2", [NCORE * D, TOK], BF16)
    g_st_in = dint("g_st_in", [1, NS * H * P], BF16)
    g_st = dint("g_st", [NCORE, NS * H * P], BF16)
    g_T_in = dint("g_T_in", [1, H])
    g_T = dint("g_T", [NCORE, H])
    rs_in = dint("rs_in", [S, D])
    rs_out = dint("rs_out", [TOK, D])

    d_xnfm = dint("d_xnfm", [NIC, 128, TOK], BF16)
    d_m = dint("d_m", [2, TOK])
    d_xt = dint("d_xt", [NCH, 128, H * P], BF16)
    d_sg = dint("d_sg", [32, 128, TOK], BF16)
    d_dres = dint("d_dres", [32, 128, TOK], BF16)
    d_Cfm = dint("d_Cfm", [NCH, 128, CH], BF16)
    d_gmt = dint("d_gmt", [NCH, 128, CH])
    d_acum = dint("d_acum", [NCH, H, CH])
    d_nacT = dint("d_nacT", [NCH, CH, H])
    d_sin = dint("d_sin", [NCH, 128, H * P], BF16)
    d_lam = dint("d_lam", [NCH, H])
    d_yn = dint("d_yn", [NCH, 128, INTER], BF16)
    d_s = dint("d_s", [NT, 128, D], BF16)

    dbg = {}
    if debug:
        def dbgout(name, shape, dt=F32):
            dbg[name] = nc.dram_tensor(name, shape, dt, kind="ExternalOutput").ap()
        dbgout("dbg_xn2", [D, TOK])
        dbgout("dbg_csl", [34, 128, 512])
        dbgout("dbg_dt", [64, 1024])
        dbgout("dbg_sin", [NCH, 128, H * P])
        dbgout("dbg_stot", [NCH, 128, H * P])
        dbgout("dbg_yg", [NCH, 128, H * P])
        dbgout("dbg_s", [NT, 128, D])
        dbgout("dbg_att", [2 * DH, S])
        dbgout("dbg_rs", [TOK, D])
        dbgout("dbg_ffn", [NT, 128, D])

    import contextlib
    stack = contextlib.ExitStack()

    def sb(name, shape, dt=F32):
        return stack.enter_context(nc.sbuf_tensor(name, shape, dt))

    def psx(name, shape, dt=F32):
        return stack.enter_context(nc.psum_tensor(name, shape, dt))

    with stack:
        sems = {l: nc.alloc_semaphore("lane_" + l) for l in Prog.LANES}
        cc1 = nc.alloc_semaphore("cc1")
        cc2 = nc.alloc_semaphore("cc2")
        cc3 = nc.alloc_semaphore("cc3")
        cc4 = nc.alloc_semaphore("cc4")

        IDf = sb("IDf", [128, 128], F32)
        IDb = sb("IDb", [128, 128], BF16)
        MSL = sb("MSL", [128, 128], F32)
        AMK = sb("AMK", [128, 4 * 512], BF16)
        ONESB = sb("ONESB", [1, 128], BF16)
        ONESCB = sb("ONESCB", [128, 1], BF16)
        ONESCF = sb("ONESCF", [128, 1], F32)
        ONES64 = sb("ONES64", [64, 128], F32)
        CONVW = sb("CONVW", [128, 34 * 4], F32)
        CONVB = sb("CONVB", [128, 34], F32)
        DCOL = sb("DCOL", [128, 32], F32)
        NWCOL = sb("NWCOL", [128, 32], F32)
        ACOL = sb("ACOL", [64, 1], F32)
        DTBC = sb("DTBC", [64, 1], F32)
        L1S = sb("L1S", [128, 16], F32)
        L1B = sb("L1B", [128, 16], F32)
        L2S = sb("L2S", [128, 16], F32)
        L2B = sb("L2B", [128, 16], F32)
        WRT = sb("WRT", [128, 48], F32)
        GATES = sb("GATES", [1, NCORE], F32)
        GREP = sb("GREP", [128, NCORE], F32)
        WDT = sb("WDT", [128, 16 * RNK], BF16)
        BREP = sb("BREP", [128, D], BF16)
        ROWS = sb("ROWS", [1, 1024], F32)
        ROWB = sb("ROWB", [1, 1024], BF16)
        COLS = sb("COLS", [128, 32], F32)
        SHFT = sb("SHFT", [128, 1], F32)
        SMF = sb("SMF", [128, 640], F32)

        PSF = [psx(f"PS{i}", [128, 512], F32) for i in range(7)]
        PS_HB = psx("PS_HB", [128, 1024], BF16)

        A1 = sb("A1", [128, 17408], BF16)
        A2 = sb("A2", [128, 16384], F32)
        A3 = sb("A3", [128, 16384], BF16)
        A4 = sb("A4", [128, 16384], BF16)
        A5 = sb("A5", [128, 8192], BF16)
        A7 = sb("A7", [128, 2048], BF16)
        A2b = A2.ap().bitcast(BF16)

        pr.dma("sync", lambda e: e.dma_start(out=IDf[:, :], in_=identf))
        pr.dma("gpsimd", lambda e: e.dma_start(out=IDb[:, :], in_=identb_src))
        pr.dma("sync", lambda e: e.dma_start(out=MSL[:, :], in_=mask_sle))
        pr.dma("gpsimd", lambda e: e.dma_start(
            out=AMK[:, :].rearrange("p (j t) -> p j t", t=512),
            in_=amask.rearrange("j p t -> p j t")))
        pr.dma("gpsimd", lambda e: e.dma_start(out=ONESB[:, :], in_=onesf))
        pr.dma("gpsimd", lambda e: e.dma_start(
            out=ONESCB[:, :], in_=onesf.rearrange("a (p o) -> (a p) o", o=1)))
        pr.dma("sync", lambda e: e.dma_start(
            out=ONESCF[:, :], in_=onesf.rearrange("a (p o) -> (a p) o", o=1)))
        pr.add("vector", lambda e: e.memset(ONES64[:, :], 1.0))
        pr.add("vector", lambda e: e.memset(SHFT[:, :], -ASHIFT))
        pr.dma("sync", lambda e: e.dma_start(
            out=CONVW[:, :].rearrange("p (fc k) -> p fc k", k=4),
            in_=conv_w.rearrange("(fc p) k -> p fc k", p=128)))
        pr.dma("sync", lambda e: e.dma_start(
            out=CONVB[:, :], in_=conv_b.rearrange("(fc p) -> p fc", p=128), allow_slow_non_contiguous=True))
        pr.dma("sync", lambda e: e.dma_start(
            out=DCOL[0:64, :],
            in_=d_vec.rearrange("(fc h2) -> fc h2", h2=2)[:, 0:1]
                     .rearrange("fc a -> a fc").broadcast_to((64, 32)), allow_slow_non_contiguous=True))
        pr.dma("sync", lambda e: e.dma_start(
            out=DCOL[64:128, :],
            in_=d_vec.rearrange("(fc h2) -> fc h2", h2=2)[:, 1:2]
                     .rearrange("fc a -> a fc").broadcast_to((64, 32)), allow_slow_non_contiguous=True))
        pr.dma("sync", lambda e: e.dma_start(
            out=NWCOL[:, :], in_=norm_w.rearrange("(fc q) -> q fc", q=128), allow_slow_non_contiguous=True))
        pr.dma("sync", lambda e: e.dma_start(out=ACOL[:, :], in_=a_log.rearrange("(h o) -> h o", o=1)))
        pr.dma("sync", lambda e: e.dma_start(out=DTBC[:, :], in_=dt_bias.rearrange("(h o) -> h o", o=1)))
        for t, src in ((L1S, ln1_s), (L1B, ln1_b), (L2S, ln2_s), (L2B, ln2_b)):
            pr.dma("sync", lambda e, t=t, src=src: e.dma_start(
                out=t[:, :], in_=src.rearrange("(fc q) -> q fc", q=128), allow_slow_non_contiguous=True))
        pr.dma("gpsimd", lambda e: e.dma_start(
            out=BREP[:, :], in_=sbias.rearrange("(o d) -> o d", o=1).broadcast_to((128, D))))
        pr.dma("sync", lambda e: e.dma_start(
            out=WRT[:, :].rearrange("q (ic r) -> q ic r", r=3),
            in_=w_rT.rearrange("(ic q) r -> q ic r", q=128)))
        pr.dma("sync", lambda e: e.dma_start(out=GATES[:, :], in_=gates))
        pr.dma("sync", lambda e: e.dma_start(out=GREP[:, :], in_=gates.broadcast_to((128, NCORE))))
        pr.dma("gpsimd", lambda e: e.dma_start(
            out=WDT[:, :].rearrange("q (ic r) -> q ic r", r=RNK),
            in_=w_dT.rearrange("(ic q) r -> q ic r", q=128)))
        pr.add("scalar", lambda e: e.activation(ACOL[:, 0:1], ACOL[:, 0:1], AF.Exp))
        pr.add("vector", lambda e: e.tensor_scalar_mul(ACOL[:, 0:1], ACOL[:, 0:1], -1.0))

        XNFM = A4
        for t in range(NT):
            xs = A2[:, 0:2048]
            xn = A2[:, 2048:4096]
            o1 = A2[:, 4096:6144]
            scr = A2[:, 6144:8192]
            xnf32 = A2[:, 8192:10240]
            xn2t = A7[:, 0:2048]
            pr.dma("sync", lambda e, t=t, xs=xs: e.dma_start(
                out=xs, in_=x_own[t * 128:(t + 1) * 128, :]))
            pr.add("vector", lambda e, xs=xs: e.reduce_sum(COLS[:, 0:1], xs, axis=AXX))
            pr.add("vector", lambda e: e.tensor_scalar_mul(COLS[:, 1:2], COLS[:, 0:1], -1.0 / D))
            pr.add("scalar", lambda e, xs=xs, scr=scr: e.activation(
                scr, xs, AF.Square, bias=COLS[:, 1:2], accum_out=COLS[:, 2:3]))
            pr.add("vector", lambda e: e.tensor_scalar(COLS[:, 3:4], COLS[:, 2:3], 1.0 / D, EPS, ALU.mult, ALU.add))
            pr.add("scalar", lambda e: e.activation(COLS[:, 4:5], COLS[:, 3:4], AF.Sqrt))
            pr.add("vector", lambda e: e.reciprocal(COLS[:, 5:6], COLS[:, 4:5]))
            pr.add("vector", lambda e, xs=xs, xn=xn: e.tensor_scalar(
                xn, xs, COLS[:, 1:2], COLS[:, 5:6], ALU.add, ALU.mult))
            pr.add("vector", lambda e, xn=xn: e.reduce_sum(COLS[:, 8:9], xn, axis=AXX))
            pr.add("vector", lambda e: e.tensor_scalar_mul(COLS[:, 9:10], COLS[:, 8:9], -1.0 / D))
            pr.add("scalar", lambda e, xn=xn, scr=scr: e.activation(
                scr, xn, AF.Square, bias=COLS[:, 9:10], accum_out=COLS[:, 10:11]))
            pr.add("vector", lambda e: e.tensor_scalar(COLS[:, 11:12], COLS[:, 10:11], 1.0 / D, EPS, ALU.mult, ALU.add))
            pr.add("scalar", lambda e: e.activation(COLS[:, 12:13], COLS[:, 11:12], AF.Sqrt))
            pr.add("vector", lambda e: e.reciprocal(COLS[:, 13:14], COLS[:, 12:13]))
            pr.add("vector", lambda e, xn=xn, o1=o1: e.tensor_scalar(
                o1, xn, COLS[:, 9:10], COLS[:, 13:14], ALU.add, ALU.mult))
            for fc in range(16):
                pr.add("tensor", lambda e, fc=fc, xn=xn: e.transpose(
                    PSF[6][:, 0:128], xn[:, fc * 128:(fc + 1) * 128], IDf[:, :]))
                pr.add("vector", lambda e, fc=fc, t=t: e.tensor_copy(
                    XNFM[:, fc * 1024 + t * 128: fc * 1024 + t * 128 + 128], PSF[6][:, 0:128]))
                pr.add("vector", lambda e, fc=fc, xnf32=xnf32: e.tensor_copy(
                    xnf32[:, fc * 128:(fc + 1) * 128], PSF[6][:, 0:128]))
                pr.add("tensor", lambda e, fc=fc, o1=o1: e.transpose(
                    PSF[6][:, 128:256], o1[:, fc * 128:(fc + 1) * 128], IDf[:, :]))
                pr.add("vector", lambda e, fc=fc, xn2t=xn2t: e.tensor_scalar(
                    xn2t[:, fc * 128:(fc + 1) * 128], PSF[6][:, 128:256],
                    L1S[:, fc:fc + 1], L1B[:, fc:fc + 1], ALU.mult, ALU.add))
            pr.dma("sync", lambda e, t=t: [e.dma_start(
                out=d_xnfm[fc, :, t * 128:(t + 1) * 128],
                in_=XNFM[:, fc * 1024 + t * 128: fc * 1024 + t * 128 + 128])
                for fc in range(16)], n=16)
            pr.dma("gpsimd", lambda e, t=t, xn2t=xn2t: [e.dma_start(
                out=g_xn2_in.rearrange("(fc q) t -> fc q t", q=128)[fc, :, t * 128:(t + 1) * 128],
                in_=xn2t[:, fc * 128:(fc + 1) * 128]) for fc in range(16)], n=16)

            def mm_rt(e, xnf32=xnf32):
                return [e.matmul(PSF[5][:, 0:3], xnf32[:, ic * 128:(ic + 1) * 128],
                                 WRT[:, ic * 3:(ic + 1) * 3],
                                 start=(ic == 0), stop=(ic == 15)) for ic in range(16)]
            pr.add("tensor", mm_rt, n=16)
            pr.add("vector", lambda e: e.tensor_copy(COLS[:, 24:27], PSF[5][:, 0:3]))
            pr.add("vector", lambda e: e.tensor_tensor(COLS[:, 16:17], COLS[:, 26:27], COLS[:, 24:25], ALU.is_gt))
            pr.add("vector", lambda e: e.tensor_tensor(COLS[:, 17:18], COLS[:, 26:27], COLS[:, 25:26], ALU.is_gt))
            pr.add("vector", lambda e: e.tensor_tensor(COLS[:, 18:19], COLS[:, 25:26], COLS[:, 24:25], ALU.is_gt))
            pr.add("vector", lambda e: e.tensor_tensor(COLS[:, 19:20], COLS[:, 16:17], COLS[:, 17:18], ALU.mult))
            pr.add("vector", lambda e: e.tensor_scalar(COLS[:, 20:21], COLS[:, 17:18], -1.0, 1.0, ALU.mult, ALU.add))
            pr.add("vector", lambda e: e.tensor_tensor(COLS[:, 21:22], COLS[:, 18:19], COLS[:, 20:21], ALU.mult))
            pr.dma("sync", lambda e, t=t: e.dma_start(
                out=d_m.rearrange("a (t o) -> a t o", o=1)[1, t * 128:(t + 1) * 128], in_=COLS[:, 19:20]))
            pr.dma("sync", lambda e, t=t: e.dma_start(
                out=d_m.rearrange("a (t o) -> a t o", o=1)[0, t * 128:(t + 1) * 128], in_=COLS[:, 21:22]))
        if debug:
            pr.dma("gpsimd", lambda e: e.dma_start(out=dbg["dbg_xn2"], in_=g_xn2_in))
        pr.async_op("gpsimd", lambda e: e.collective_compute(
            "AllGather", ALU.bypass, replica_groups=[list(range(NCORE))],
            ins=[g_xn2_in.opt()], outs=[g_xn2.opt()]), cc1)

        pr.dma("sync", lambda e: e.dma_start(out=A2[0:3, 0:2048], in_=x_halo))
        pr.add("vector", lambda e: e.reduce_sum(COLS[0:3, 0:1], A2[0:3, 0:2048], axis=AXX))
        pr.add("vector", lambda e: e.tensor_scalar_mul(COLS[0:3, 1:2], COLS[0:3, 0:1], -1.0 / D))
        pr.add("scalar", lambda e: e.activation(A2[0:3, 2048:4096], A2[0:3, 0:2048], AF.Square,
                                                bias=COLS[0:3, 1:2], accum_out=COLS[0:3, 2:3]))
        pr.add("vector", lambda e: e.tensor_scalar(COLS[0:3, 3:4], COLS[0:3, 2:3], 1.0 / D, EPS, ALU.mult, ALU.add))
        pr.add("scalar", lambda e: e.activation(COLS[0:3, 4:5], COLS[0:3, 3:4], AF.Sqrt))
        pr.add("vector", lambda e: e.reciprocal(COLS[0:3, 5:6], COLS[0:3, 4:5]))
        pr.add("vector", lambda e: e.tensor_scalar(A2[0:3, 0:2048], A2[0:3, 0:2048],
                                                   COLS[0:3, 1:2], COLS[0:3, 5:6], ALU.add, ALU.mult))
        XNH = sb("XNH", [128, 48], BF16)
        for ic in range(16):
            pr.add("tensor", lambda e, ic=ic: e.transpose(
                PSF[6][:, 0:3], A2[0:3, ic * 128:(ic + 1) * 128], IDf[0:3, 0:3]))
            pr.add("vector", lambda e, ic=ic: e.tensor_copy(XNH[:, ic * 3:ic * 3 + 3], PSF[6][:, 0:3]))

        CSL = A1
        IPO = A7
        HALO = sb("HALO", [128, 102], BF16)
        WSL = A3
        SILG = A2b[:, 24576:25600]
        DTX = A2[0:64, 6144:8192]
        ACM = sb("ACM", [64, 384], F32)
        TSEG = sb("TSEG", [64, 1], F32)
        XTI = A2b[:, 16384:20480]
        X2T = A2b[:, 20480:24576]
        KSM = sb("KSM", [128, 256], BF16)
        RUNA = A2b
        pr.add("vector", lambda e: e.memset(TSEG[:, :], 0.0))
        pr.add("vector", lambda e: e.memset(RUNA[:, 0:4096], 0.0))

        for tb in range(2):
            for og in range(17):
                ocs = list(range(og * 4, min(og * 4 + 4, NOC)))
                buf = og % 2
                Wb = WSL[:, buf * 8192: buf * 8192 + 8192]

                def ld_w(e, ocs=ocs, Wb=Wb):
                    ins = []
                    for j, oc in enumerate(ocs):
                        ow = 128 if oc < 66 else 64
                        ins.append(e.dma_start(
                            out=Wb[:, j * 2048: j * 2048 + 16 * ow]
                                .rearrange("q (ic o) -> q ic o", o=ow),
                            in_=w_ipT.rearrange("(ic q) o -> q ic o", q=128)[:, :, oc * 128:oc * 128 + ow]))
                    return ins
                pr.dma("gpsimd", ld_w, n=len(ocs))

                for j, oc in enumerate(ocs):
                    ow = 128 if oc < 66 else 64
                    bank = PSF[j]

                    def mm_ip(e, j=j, ow=ow, bank=bank, Wb=Wb, tb=tb):
                        return [e.matmul(bank[0:ow, 0:512],
                                         Wb[:, j * 2048 + ic * ow: j * 2048 + (ic + 1) * ow],
                                         XNFM[:, ic * 1024 + tb * 512: ic * 1024 + tb * 512 + 512],
                                         start=(ic == 0), stop=(ic == 15)) for ic in range(16)]
                    pr.add("tensor", mm_ip, n=16)
                    if tb == 0 and 32 <= oc <= 65:
                        def mm_halo(e, j=j, Wb=Wb):
                            return [e.matmul(PSF[4][0:128, 500 + j * 3:503 + j * 3],
                                             Wb[:, j * 2048 + ic * 128: j * 2048 + (ic + 1) * 128],
                                             XNH[:, ic * 3:(ic + 1) * 3],
                                             start=(ic == 0), stop=(ic == 15)) for ic in range(16)]
                        pr.add("tensor", mm_halo, n=16)
                        pr.add("vector", lambda e, j=j, oc=oc: e.tensor_copy(
                            HALO[:, (oc - 32) * 3:(oc - 32) * 3 + 3],
                            PSF[4][0:128, 500 + j * 3:503 + j * 3]))
                    if oc == 66:
                        pr.add("vector", lambda e, bank=bank: e.tensor_copy(DTX[:, 0:512], bank[0:64, 0:512]))
                    else:
                        pr.add("vector", lambda e, j=j, bank=bank: e.tensor_copy(
                            IPO[:, (j % 4) * 512:(j % 4) * 512 + 512], bank[0:128, 0:512]))
                for j, oc in enumerate(ocs):
                    src = IPO[:, (j % 4) * 512:(j % 4) * 512 + 512]
                    if oc < 32:
                        pr.add("scalar", lambda e, src=src: e.activation(SILG[:, 0:512], src, AF.Silu))
                        pr.dma("sync", lambda e, oc=oc, tb=tb: e.dma_start(
                            out=d_sg[oc, :, tb * 512:(tb + 1) * 512], in_=SILG[:, 0:512]))
                    elif oc < 66:
                        fc = oc - 32
                        acc = SMF[:, 0:512]
                        pr.add("vector", lambda e, src=src, fc=fc, acc=acc: e.tensor_scalar(
                            acc, src, CONVW[:, fc * 4 + 3:fc * 4 + 4], None, ALU.mult))
                        for k in (2, 1, 0):
                            sh = 3 - k
                            pr.add("vector", lambda e, src=src, fc=fc, k=k, sh=sh, acc=acc:
                                   e.scalar_tensor_tensor(
                                       acc[:, sh:512], src[:, 0:512 - sh],
                                       CONVW[:, fc * 4 + k:fc * 4 + k + 1],
                                       acc[:, sh:512], ALU.mult, ALU.add))
                            pr.add("vector", lambda e, fc=fc, k=k, sh=sh, acc=acc:
                                   e.scalar_tensor_tensor(
                                       acc[:, 0:sh], HALO[:, fc * 3 + 3 - sh:fc * 3 + 3],
                                       CONVW[:, fc * 4 + k:fc * 4 + k + 1],
                                       acc[:, 0:sh], ALU.mult, ALU.add))
                        pr.add("vector", lambda e, src=src, fc=fc: e.tensor_copy(
                            HALO[:, fc * 3:fc * 3 + 3], src[:, 509:512]))
                        pr.add("scalar", lambda e, fc=fc, acc=acc: e.activation(
                            CSL[:, fc * 512:fc * 512 + 512], acc, AF.Silu, bias=CONVB[:, fc:fc + 1]))
            for fc in range(32):
                pr.dma("sync", lambda e, fc=fc, tb=tb: e.dma_start(
                    out=SILG[:, 0:512], in_=d_sg[fc, :, tb * 512:(tb + 1) * 512]))
                pr.add("vector", lambda e, fc=fc: e.scalar_tensor_tensor(
                    SILG[:, 512:1024], CSL[:, fc * 512:fc * 512 + 512],
                    DCOL[:, fc:fc + 1], SILG[:, 0:512], ALU.mult, ALU.mult))
                pr.dma("sync", lambda e, fc=fc, tb=tb: e.dma_start(
                    out=d_dres[fc, :, tb * 512:(tb + 1) * 512], in_=SILG[:, 512:1024]))
            if debug and tb == 0:
                pr.dma("gpsimd", lambda e: [e.dma_start(
                    out=dbg["dbg_csl"][fc], in_=CSL[:, fc * 512:(fc + 1) * 512]) for fc in range(34)], n=34)

            pr.add("vector", lambda e: e.tensor_scalar(DTX[:, 512:1024], DTX[:, 0:512], DTBC[:, 0:1], None, ALU.add))
            pr.add("vector", lambda e: e.tensor_scalar(DTX[:, 1024:1536], DTX[:, 512:1024], -1.0, None, ALU.mult))
            pr.add("vector", lambda e: e.tensor_max(DTX[:, 1024:1536], DTX[:, 1024:1536], DTX[:, 512:1024]))
            pr.add("scalar", lambda e: e.activation(DTX[:, 1024:1536], DTX[:, 1024:1536], AF.Exp, scale=-1.0))
            pr.add("scalar", lambda e: e.activation(DTX[:, 1024:1536], DTX[:, 1024:1536], AF.Ln, bias=1.0))
            pr.add("vector", lambda e: e.tensor_scalar_max(DTX[:, 512:1024], DTX[:, 512:1024], 0.0))
            pr.add("vector", lambda e: e.tensor_add(DTX[:, 512:1024], DTX[:, 512:1024], DTX[:, 1024:1536]))
            pr.add("vector", lambda e: e.tensor_scalar(DTX[:, 1536:2048], DTX[:, 512:1024], ACOL[:, 0:1], None, ALU.mult))
            if debug and tb == 0:
                pr.dma("sync", lambda e: e.dma_start(out=dbg["dbg_dt"][:, 0:512], in_=DTX[:, 512:1024]))
                pr.dma("sync", lambda e: e.dma_start(out=dbg["dbg_dt"][:, 512:1024], in_=DTX[:, 1536:2048]))

            for ci in range(4):
                c = tb * 4 + ci
                cs = slice(ci * 128, ci * 128 + 128)
                pr.add("vector", lambda e, cs=cs: e.tensor_tensor_scan(
                    ACM[:, 0:128], ONES64[:, :], DTX[:, 1536:2048][:, cs], 0.0, ALU.mult, ALU.add))
                pr.add("vector", lambda e: e.tensor_add(TSEG[:, 0:1], TSEG[:, 0:1], ACM[:, 127:128]))
                pr.dma("sync", lambda e, c=c: e.dma_start(
                    out=d_lam.rearrange("c (h o) -> c h o", o=1)[c], in_=ACM[:, 127:128]))
                pr.dma("sync", lambda e, c=c: e.dma_start(out=d_acum[c], in_=ACM[:, 0:128]))
                pr.add("tensor", lambda e: e.transpose(PSF[6][:, 0:64], ACM[:, 0:128], IDf[0:64, 0:64]))
                pr.add("vector", lambda e: e.tensor_scalar_mul(SMF[:, 512:576], PSF[6][:, 0:64], -1.0))
                pr.dma("sync", lambda e, c=c: e.dma_start(out=d_nacT[c], in_=SMF[:, 512:576]))
                pr.add("vector", lambda e: e.tensor_scalar(
                    ACM[:, 128:256], ACM[:, 0:128], -1.0, ACM[:, 127:128], ALU.mult, ALU.add))
                pr.add("scalar", lambda e: e.activation(ACM[:, 128:256], ACM[:, 128:256], AF.Exp))
                pr.add("tensor", lambda e: e.transpose(PSF[6][:, 64:128], ACM[:, 128:256], IDf[0:64, 0:64]))
                pr.add("vector", lambda e: e.tensor_copy(KSM[:, 0:64], PSF[6][:, 64:128]))
                pr.dma("sync", lambda e, c=c: e.dma_start(out=ROWS[0:1, 0:64], in_=d_lam[c:c + 1, :]))
                pr.add("scalar", lambda e: e.activation(ROWS[0:1, 64:128], ROWS[0:1, 0:64], AF.Exp))
                pr.add("vector", lambda e: e.tensor_copy(ROWB[0:1, 0:64], ROWS[0:1, 64:128]))
                pr.add("tensor", lambda e, cs=cs: e.transpose(
                    PSF[6][:, 128:192], DTX[:, 512:1024][:, cs], IDf[0:64, 0:64]))
                pr.add("vector", lambda e: e.tensor_copy(KSM[:, 64:128], PSF[6][:, 128:192]))
                pr.dma("sync", lambda e, c=c, cs=cs: e.dma_start(
                    out=d_Cfm[c], in_=CSL[:, 33 * 512:34 * 512][:, cs]))
                pr.add("tensor", lambda e, cs=cs: e.transpose(
                    PS_HB[:, 0:128], CSL[:, 32 * 512:33 * 512][:, cs], IDb[:, :]))
                pr.add("vector", lambda e: e.tensor_copy(KSM[:, 128:256], PS_HB[:, 0:128]))
                pr.add("tensor", lambda e, cs=cs: e.matmul(
                    PSF[6][:, 192:320], CSL[:, 32 * 512:33 * 512][:, cs],
                    CSL[:, 33 * 512:34 * 512][:, cs], start=True, stop=True))
                pr.add("vector", lambda e: e.tensor_tensor(SMF[:, 0:128], PSF[6][:, 192:320], MSL[:, :], ALU.mult))
                pr.dma("sync", lambda e, c=c: e.dma_start(out=d_gmt[c], in_=SMF[:, 0:128]))
                for fc in range(32):
                    pr.add("tensor", lambda e, fc=fc, cs=cs: e.transpose(
                        PS_HB[:, 128:256], CSL[:, fc * 512:(fc + 1) * 512][:, cs], IDb[:, :]))
                    pr.add("vector", lambda e, fc=fc: e.tensor_tensor(
                        XTI[:, fc * 128:(fc + 1) * 128].rearrange("p (h w) -> p h w", h=2),
                        PS_HB[:, 128:256].rearrange("p (h w) -> p h w", h=2),
                        KSM[:, 64 + 2 * fc:64 + 2 * fc + 2].rearrange("p (h o) -> p h o", o=1)
                        .broadcast_to((128, 2, 64)), ALU.mult))
                pr.dma("sync", lambda e, c=c: e.dma_start(out=d_xt[c], in_=XTI[:, :]))
                pr.add("vector", lambda e: e.tensor_tensor(
                    X2T[:, :].rearrange("p (h w) -> p h w", h=64),
                    XTI[:, :].rearrange("p (h w) -> p h w", h=64),
                    KSM[:, 0:64].rearrange("p (h o) -> p h o", o=1).broadcast_to((128, 64, 64)),
                    ALU.mult))
                for jj in range(8):
                    pr.add("tensor", lambda e, jj=jj: e.matmul(
                        PSF[jj % 2][:, 0:512], KSM[:, 128:256],
                        X2T[:, jj * 512:(jj + 1) * 512], start=True, stop=True))
                    pr.add("vector", lambda e, jj=jj: e.tensor_copy(
                        RUNA[:, 8192 + jj * 512:8192 + (jj + 1) * 512], PSF[jj % 2][:, 0:512]))
                pr.dma("sync", lambda e, c=c: e.dma_start(out=d_sin[c], in_=RUNA[:, 0:4096]))
                for jj in range(8):
                    pr.add("tensor", lambda e, jj=jj: e.matmul(
                        PSF[2 + jj % 2][:, 0:512], ONESB[0:1, :],
                        ROWB[0:1, jj * 8:jj * 8 + 8].rearrange("a (h o) -> a h o", o=1)
                        .broadcast_to((1, 8, 64)), start=True, stop=True))
                    pr.add("vector", lambda e, jj=jj: e.tensor_copy(
                        RUNA[:, 4096 + jj * 512:4096 + (jj + 1) * 512], PSF[2 + jj % 2][:, 0:512]))
                pr.add("vector", lambda e: e.tensor_tensor(
                    RUNA[:, 0:4096], RUNA[:, 0:4096], RUNA[:, 4096:8192], ALU.mult))
                pr.add("vector", lambda e: e.tensor_tensor(
                    RUNA[:, 0:4096], RUNA[:, 0:4096], RUNA[:, 8192:12288], ALU.add))

        pr.dma("sync", lambda e: e.dma_start(
            out=g_st_in.rearrange("a (n hp) -> (a n) hp", n=128), in_=RUNA[:, 0:4096]))
        pr.dma("sync", lambda e: e.dma_start(
            out=g_T_in.rearrange("a (h o) -> (a h) o", o=1), in_=TSEG[:, 0:1]))
        pr.async_op("gpsimd", lambda e: e.collective_compute(
            "AllGather", ALU.bypass, replica_groups=[list(range(NCORE))],
            ins=[g_st_in.opt()], outs=[g_st.opt()]), cc2)
        pr.async_op("gpsimd", lambda e: e.collective_compute(
            "AllGather", ALU.bypass, replica_groups=[list(range(NCORE))],
            ins=[g_T_in.opt()], outs=[g_T.opt()]), cc3)
        if debug:
            pr.dma("gpsimd", lambda e: [e.dma_start(out=dbg["dbg_sin"][c], in_=d_sin[c])
                                      for c in range(NCH)], n=NCH)

        QKH = A2b
        VT = A3
        WQ3 = A4
        X2S = A1
        OT2 = A5
        OT1 = A1
        PT = A7[:, 0:1024]
        OEV = A7[:, 1024:2048]
        pr.dma("gpsimd", lambda e: [e.dma_start(
            out=WQ3[:, i * 4096:(i + 1) * 4096].rearrange("q (ic o) -> q ic o", o=256),
            in_=w.rearrange("(ic q) o -> q ic o", q=128))
            for i, w in enumerate((w_qT, w_kT, w_vT))], n=3, pre=[(cc1, 1)])

        for qb in range(16):
            src_core = qb // 2
            tloc = (qb % 2) * 512
            pr.dma("sync", lambda e, src_core=src_core, tloc=tloc: [e.dma_start(
                out=X2S[:, ic * 512:(ic + 1) * 512],
                in_=g_xn2[src_core * D + ic * 128: src_core * D + (ic + 1) * 128, tloc:tloc + 512])
                for ic in range(16)], n=16)
            for wi in range(2):
                for hc in range(2):
                    def mm_qk(e, wi=wi, hc=hc):
                        return [e.matmul(PSF[wi * 2 + hc][:, 0:512],
                                         WQ3[:, wi * 4096 + ic * 256 + hc * 128:
                                             wi * 4096 + ic * 256 + hc * 128 + 128],
                                         X2S[:, ic * 512:(ic + 1) * 512],
                                         start=(ic == 0), stop=(ic == 15)) for ic in range(16)]
                    pr.add("tensor", mm_qk, n=16)
                    off = wi * 16384 + hc * 8192 + qb * 512
                    pr.add("vector", lambda e, wi=wi, hc=hc, off=off: e.tensor_copy(
                        QKH[:, off:off + 512], PSF[wi * 2 + hc][:, 0:512]))
            for t4 in range(4):
                def mm_v(e, t4=t4):
                    return [e.matmul(PSF[4 + t4 % 2][:, 0:256],
                                     X2S[:, ic * 512 + t4 * 128: ic * 512 + t4 * 128 + 128],
                                     WQ3[:, 2 * 4096 + ic * 256: 2 * 4096 + (ic + 1) * 256],
                                     start=(ic == 0), stop=(ic == 15)) for ic in range(16)]
                pr.add("tensor", mm_v, n=16)
                kb = qb * 4 + t4
                pr.add("vector", lambda e, kb=kb, t4=t4: e.tensor_copy(
                    VT[:, kb * 256:kb * 256 + 256], PSF[4 + t4 % 2][:, 0:256]))

        for h in range(2):
            OST = OT1[:, 8192:16384] if h == 1 else OT2[:, 0:8192]
            for qb in range(16):
                nkb = 4 * qb + 4
                for kb in range(nkb):
                    sbank = PSF[kb % 2]
                    pr.add("tensor", lambda e, h=h, qb=qb, kb=kb, sbank=sbank: e.matmul(
                        sbank[:, 0:512],
                        QKH[:, 16384 + h * 8192 + kb * 128: 16384 + h * 8192 + kb * 128 + 128],
                        QKH[:, h * 8192 + qb * 512: h * 8192 + qb * 512 + 512],
                        start=True, stop=True))
                    pt = PT[:, (kb % 2) * 512:(kb % 2) * 512 + 512]
                    pr.add("scalar", lambda e, pt=pt, sbank=sbank: e.activation(
                        pt, sbank[:, 0:512], AF.Exp, bias=SHFT[:, 0:1], scale=ASCALE))
                    j = kb - 4 * qb
                    if j >= 0:
                        pr.add("vector", lambda e, pt=pt, j=j: e.tensor_tensor(
                            pt, pt, AMK[:, j * 512:(j + 1) * 512], ALU.mult))
                    pr.add("tensor", lambda e, h=h, kb=kb, pt=pt, f=(kb == 0), l=(kb == nkb - 1):
                           e.matmul(PSF[2][:, 0:512],
                                    VT[:, kb * 256 + h * 128: kb * 256 + h * 128 + 128],
                                    pt, start=f, stop=l))
                    pr.add("tensor", lambda e, pt=pt, f=(kb == 0), l=(kb == nkb - 1):
                           e.matmul(PSF[3][0:1, 0:512], ONESCB[:, 0:1], pt, start=f, stop=l))
                pr.add("vector", lambda e: e.reciprocal(ROWS[0:1, 128:640], PSF[3][0:1, 0:512]))
                pr.add("vector", lambda e: e.tensor_copy(ROWB[0:1, 128:640], ROWS[0:1, 128:640]))
                pr.add("tensor", lambda e: e.matmul(PSF[4][:, 0:512], ONESB[0:1, :],
                                                    ROWB[0:1, 128:640], start=True, stop=True))
                pr.add("vector", lambda e: e.tensor_copy(OEV[:, 0:512], PSF[4][:, 0:512]))
                pr.add("vector", lambda e, OST=OST, qb=qb: e.tensor_tensor(
                    OST[:, qb * 512:(qb + 1) * 512], PSF[2][:, 0:512], OEV[:, 0:512], ALU.mult))
        if debug:
            pr.dma("gpsimd", lambda e: e.dma_start(out=dbg["dbg_att"][0:128, :], in_=OT2[:, 0:8192]))
            pr.dma("gpsimd", lambda e: e.dma_start(out=dbg["dbg_att"][128:256, :], in_=OT1[:, 8192:16384]))

        WOB = WQ3[:, 12288:16384]
        RSE = SMF[:, 0:512]
        for osl in range(4):
            pr.dma("gpsimd", lambda e, osl=osl: [e.dma_start(
                out=WOB[:, hc * 2048 + 0:hc * 2048 + 512],
                in_=w_oTs[hc * 128:(hc + 1) * 128, osl * 512:(osl + 1) * 512])
                for hc in range(2)], n=2)
            for tt in range(64):
                def mm_wo(e, osl=osl, tt=tt):
                    ins = []
                    for hc in range(2):
                        lhs = (OT2[:, 0:8192] if hc == 0 else OT1[:, 8192:16384])
                        ins.append(e.matmul(
                            PSF[4 + tt % 2][:, 0:512],
                            lhs[:, tt * 128:(tt + 1) * 128],
                            WOB[:, hc * 2048:hc * 2048 + 512],
                            start=(hc == 0), stop=(hc == 1)))
                    return ins
                pr.add("tensor", mm_wo, n=2)
                pr.add("vector", lambda e, tt=tt: e.tensor_copy(RSE[:, 0:512], PSF[4 + tt % 2][:, 0:512]))
                pr.dma("sync", lambda e, osl=osl, tt=tt: e.dma_start(
                    out=rs_in[tt * 128:(tt + 1) * 128, osl * 512:(osl + 1) * 512],
                    in_=RSE[:, 0:512]))
        pr.async_op("gpsimd", lambda e: e.collective_compute(
            "ReduceScatter", ALU.add, replica_groups=[list(range(NCORE))],
            ins=[rs_in.opt()], outs=[rs_out.opt()]), cc4)

        PREF = A3
        pr.dma("sync", lambda e: e.dma_start(
            out=ROWS[0:1, 0:512], in_=g_T.rearrange("(o c) h -> o (c h)", o=1)),
            pre=[(cc2, 1), (cc3, 1)])
        pr.add("vector", lambda e: e.memset(PREF[:, 0:4096], 0.0))
        for k in range(NCORE - 1):
            pr.dma("sync", lambda e, k=k: e.dma_start(
                out=PREF[:, 4096:8192], in_=g_st[k].rearrange("(n hp) -> n hp", n=128)))
            pr.add("vector", lambda e, k=k: e.tensor_scalar(
                ROWS[0:1, 640:704], ROWS[0:1, k * 64:(k + 1) * 64],
                GATES[0:1, k:k + 1], None, ALU.mult))
            pr.add("scalar", lambda e: e.activation(ROWS[0:1, 704:768], ROWS[0:1, 640:704], AF.Exp))
            pr.add("vector", lambda e: e.tensor_copy(ROWB[0:1, 640:704], ROWS[0:1, 704:768]))
            for jj in range(8):
                pr.add("tensor", lambda e, jj=jj: e.matmul(
                    PSF[jj % 2][:, 0:512], ONESB[0:1, :],
                    ROWB[0:1, 640 + jj * 8:640 + jj * 8 + 8].rearrange("a (h o) -> a h o", o=1)
                    .broadcast_to((1, 8, 64)), start=True, stop=True))
                pr.add("vector", lambda e, jj=jj: e.tensor_copy(
                    PREF[:, 8192 + jj * 512:8192 + (jj + 1) * 512], PSF[jj % 2][:, 0:512]))
            pr.add("vector", lambda e: e.tensor_tensor(
                PREF[:, 0:4096], PREF[:, 0:4096], PREF[:, 8192:12288], ALU.mult))
            pr.add("vector", lambda e, k=k: e.scalar_tensor_tensor(
                PREF[:, 0:4096], PREF[:, 4096:8192], GREP[:, k:k + 1],
                PREF[:, 0:4096], ALU.mult, ALU.add))

        pr.dma("sync", lambda e: e.dma_start(out=ROWS[0:1, 0:512], in_=d_lam.rearrange("(o c) h -> o (c h)", o=1)))
        pr.add("vector", lambda e: e.memset(ROWS[0:1, 512:576], 0.0))

        BCA = A2
        MTC = A5
        STOT = A2b[:, 16384:20480]
        SINB = A2b[:, 20480:24576]
        NACT = sb("NACT", [128, 64], F32)
        GMTC = sb("GMTC", [128, 128], F32)
        CFMC = sb("CFMC", [128, 128], BF16)
        XTC = A4
        SGD = A4[:, 8192:9216]

        for c in range(NCH):
            pr.dma("sync", lambda e, c=c: e.dma_start(out=SINB[:, :], in_=d_sin[c]))
            pr.add("scalar", lambda e: e.activation(ROWS[0:1, 576:640], ROWS[0:1, 512:576], AF.Exp))
            pr.add("vector", lambda e: e.tensor_copy(ROWB[0:1, 576:640], ROWS[0:1, 576:640]))
            for jj in range(8):
                pr.add("tensor", lambda e, jj=jj: e.matmul(
                    PSF[jj % 2][:, 0:512], ONESB[0:1, :],
                    ROWB[0:1, 576 + jj * 8:576 + jj * 8 + 8].rearrange("a (h o) -> a h o", o=1)
                    .broadcast_to((1, 8, 64)), start=True, stop=True))
                pr.add("vector", lambda e, jj=jj: e.tensor_tensor(
                    STOT[:, jj * 512:(jj + 1) * 512], PSF[jj % 2][:, 0:512],
                    PREF[:, jj * 512:(jj + 1) * 512], ALU.mult))
            pr.add("vector", lambda e: e.tensor_tensor(STOT[:, :], STOT[:, :], SINB[:, :], ALU.add))
            pr.add("vector", lambda e, c=c: e.tensor_add(
                ROWS[0:1, 512:576], ROWS[0:1, 512:576], ROWS[0:1, c * 64:(c + 1) * 64]))
            if debug:
                pr.dma("gpsimd", lambda e, c=c: e.dma_start(out=dbg["dbg_stot"][c], in_=STOT[:, :]))

            pr.dma("sync", lambda e, c=c: e.dma_start(out=NACT[:, :], in_=d_nacT[c]))
            pr.dma("sync", lambda e, c=c: e.dma_start(out=GMTC[:, :], in_=d_gmt[c]))
            pr.dma("sync", lambda e, c=c: e.dma_start(out=CFMC[:, :], in_=d_Cfm[c]))
            pr.dma("sync", lambda e, c=c: e.dma_start(out=XTC[:, 0:4096], in_=d_xt[c]))

            for half in range(2):
                pr.dma("gpsimd", lambda e, c=c, half=half: e.dma_start(
                    out=BCA[:, 0:4096],
                    in_=d_acum[c].rearrange("(o h) l -> o (h l)", o=1)[0:1, half * 4096:(half + 1) * 4096]
                    .broadcast_to((128, 4096))))
                for hh in range(32):
                    h = half * 32 + hh
                    pr.add("vector", lambda e, hh=hh, h=h: e.tensor_scalar(
                        BCA[:, 4096 + hh * 128:4096 + (hh + 1) * 128],
                        BCA[:, hh * 128:(hh + 1) * 128],
                        NACT[:, h:h + 1], 0.0, ALU.add, ALU.min))
                pr.add("scalar", lambda e: e.activation(BCA[:, 4096:8192], BCA[:, 4096:8192], AF.Exp))
                pr.add("scalar", lambda e: e.activation(BCA[:, 0:4096], BCA[:, 0:4096], AF.Exp))
                pr.add("vector", lambda e: e.tensor_tensor(
                    MTC[:, 0:4096].rearrange("p (h l) -> p h l", l=128),
                    BCA[:, 4096:8192].rearrange("p (h l) -> p h l", l=128),
                    GMTC[:, :].rearrange("p (o l) -> p o l", o=1).broadcast_to((128, 32, 128)),
                    ALU.mult))
                pr.add("vector", lambda e: e.tensor_tensor(
                    MTC[:, 4096:8192].rearrange("p (h l) -> p h l", l=128),
                    BCA[:, 0:4096].rearrange("p (h l) -> p h l", l=128),
                    CFMC[:, :].rearrange("p (o l) -> p o l", o=1).broadcast_to((128, 32, 128)),
                    ALU.mult))
                for rnd in range(4):
                    for pair in range(4):
                        h0 = half * 32 + rnd * 8 + pair * 2
                        bank = PSF[pair]
                        for i in range(2):
                            h = h0 + i
                            hh = h - half * 32
                            tp = (0, 64) if i == 1 else None
                            oap = bank[64:128, 0:128] if i == 1 else bank[0:64, 0:128]
                            pr.add("tensor", lambda e, h=h, hh=hh, tp=tp, oap=oap: [
                                e.matmul(oap, XTC[:, h * 64:(h + 1) * 64],
                                         MTC[:, hh * 128:(hh + 1) * 128],
                                         start=True, stop=False, tile_position=tp),
                                e.matmul(oap, STOT[:, h * 64:(h + 1) * 64],
                                         MTC[:, 4096 + hh * 128:4096 + (hh + 1) * 128],
                                         start=False, stop=True, tile_position=tp)], n=2)
                    grp = half * 4 + rnd
                    pr.dma("sync", lambda e, grp=grp, c=c: [e.dma_start(
                        out=SGD[:, q * 128:(q + 1) * 128],
                        in_=d_sg[grp * 4 + q, :, c * 128:(c + 1) * 128]) for q in range(4)], n=4)
                    for pair in range(4):
                        fcg = grp * 4 + pair
                        pr.add("vector", lambda e, pair=pair, fcg=fcg: e.tensor_tensor(
                            BCA.ap().bitcast(BF16)[:, 24576 + fcg * 128:24576 + (fcg + 1) * 128]
                            if False else
                            BCA[:, 12288 + fcg * 128:12288 + (fcg + 1) * 128],
                            PSF[pair][:, 0:128], SGD[:, pair * 128:(pair + 1) * 128], ALU.mult))
                    pr.dma("sync", lambda e, grp=grp, c=c: [e.dma_start(
                        out=SGD[:, 512 + q * 128:512 + (q + 1) * 128],
                        in_=d_dres[grp * 4 + q, :, c * 128:(c + 1) * 128]) for q in range(4)], n=4)
                    for pair in range(4):
                        fcg = grp * 4 + pair
                        pr.add("vector", lambda e, pair=pair, fcg=fcg: e.tensor_tensor(
                            BCA[:, 12288 + fcg * 128:12288 + (fcg + 1) * 128],
                            BCA[:, 12288 + fcg * 128:12288 + (fcg + 1) * 128],
                            SGD[:, 512 + pair * 128:512 + (pair + 1) * 128], ALU.add))
            if debug:
                pr.add("vector", lambda e: e.tensor_copy(XTC[:, 4096:8192], BCA[:, 12288:16384]))
                pr.dma("gpsimd", lambda e, c=c: e.dma_start(out=dbg["dbg_yg"][c], in_=XTC[:, 4096:8192]))

            pr.add("scalar", lambda e: e.activation(BCA[:, 0:4096], BCA[:, 12288:16384], AF.Square))

            def rms_mm(e):
                return [e.matmul(PSF[4][0:1, 0:128], ONESCF[:, 0:1],
                                 BCA[:, fcg * 128:(fcg + 1) * 128],
                                 start=(fcg == 0), stop=(fcg == 31)) for fcg in range(32)]
            pr.add("tensor", rms_mm, n=32)
            pr.add("vector", lambda e: e.tensor_scalar(
                ROWS[0:1, 640:768], PSF[4][0:1, 0:128], 1.0 / INTER, EPS, ALU.mult, ALU.add))
            pr.add("scalar", lambda e: e.activation(ROWS[0:1, 640:768], ROWS[0:1, 640:768], AF.Sqrt))
            pr.add("vector", lambda e: e.reciprocal(ROWS[0:1, 768:896], ROWS[0:1, 640:768]))
            pr.add("vector", lambda e: e.tensor_copy(ROWB[0:1, 768:896], ROWS[0:1, 768:896]))
            pr.add("tensor", lambda e: e.matmul(PSF[5][:, 0:128], ONESB[0:1, :],
                                                ROWB[0:1, 768:896], start=True, stop=True))
            pr.add("vector", lambda e: e.tensor_copy(SMF[:, 0:128], PSF[5][:, 0:128]))
            pr.add("vector", lambda e: e.tensor_tensor(
                BCA[:, 12288:16384].rearrange("p (g l) -> p g l", l=128),
                BCA[:, 12288:16384].rearrange("p (g l) -> p g l", l=128),
                SMF[:, 0:128].rearrange("p (o l) -> p o l", o=1).broadcast_to((128, 32, 128)),
                ALU.mult))
            pr.add("vector", lambda e: e.tensor_tensor(
                XTC[:, 4096:8192].rearrange("p (g l) -> p g l", l=128),
                BCA[:, 12288:16384].rearrange("p (g l) -> p g l", l=128),
                NWCOL[:, :].rearrange("p (g o) -> p g o", o=1).broadcast_to((128, 32, 128)),
                ALU.mult))
            pr.dma("sync", lambda e, c=c: e.dma_start(out=d_yn[c], in_=XTC[:, 4096:8192]))

        WOPX = A3
        YNL = A2b[:, 0:8192]
        STM = A4[:, 0:512]
        for osl in range(4):
            pr.dma("gpsimd", lambda e, osl=osl: [e.dma_start(
                out=WOPX[:, fcg * 512:(fcg + 1) * 512],
                in_=w_opT[fcg * 128:(fcg + 1) * 128, osl * 512:(osl + 1) * 512])
                for fcg in range(32)], n=32)
            for c in range(NCH):
                pr.dma("gpsimd", lambda e, c=c: e.dma_start(
                    out=YNL[:, (c % 2) * 4096:(c % 2) * 4096 + 4096], in_=d_yn[c]))

                def mm_op(e, c=c):
                    return [e.matmul(PSF[4 + c % 2][:, 0:512],
                                     YNL[:, (c % 2) * 4096 + fcg * 128:(c % 2) * 4096 + (fcg + 1) * 128],
                                     WOPX[:, fcg * 512:(fcg + 1) * 512],
                                     start=(fcg == 0), stop=(fcg == 31)) for fcg in range(32)]
                pr.add("tensor", mm_op, n=32)
                pr.add("vector", lambda e, c=c: e.tensor_copy(STM[:, 0:512], PSF[4 + c % 2][:, 0:512]))
                pr.dma("sync", lambda e, c=c, osl=osl: e.dma_start(
                    out=d_s[c, :, osl * 512:(osl + 1) * 512], in_=STM[:, 0:512]))
        if debug:
            pr.dma("gpsimd", lambda e: [e.dma_start(out=dbg["dbg_s"][c], in_=d_s[c])
                                      for c in range(NT)], n=NT)

        XFNT = A3
        XNT = A1
        XN3F = A5
        G1 = A2b
        WSLB = A4
        XNFR = A2b[:, 0:16384]
        DWN = sb("DWN", [32, 512], BF16)
        O2ALL = A5
        MC = sb("MC", [128, 2], F32)
        GT = A7[:, 0:2048]
        STL = A4[:, 0:2048]

        for t in range(NT):
            xs = A2[:, 0:2048]
            xn = A2[:, 2048:4096]
            rst = A2[:, 4096:6144]
            scr = A2[:, 6144:8192]
            pr.dma("sync", lambda e, t=t, xs=xs: e.dma_start(
                out=xs, in_=x_own[t * 128:(t + 1) * 128, :]))
            pr.dma("sync", lambda e, t=t, rst=rst: e.dma_start(
                out=rst, in_=rs_out[t * 128:(t + 1) * 128, :]),
                pre=[(cc4, 1)] if t == 0 else None)
            pr.add("vector", lambda e, xs=xs: e.reduce_sum(COLS[:, 0:1], xs, axis=AXX))
            pr.add("vector", lambda e: e.tensor_scalar_mul(COLS[:, 1:2], COLS[:, 0:1], -1.0 / D))
            pr.add("scalar", lambda e, xs=xs, scr=scr: e.activation(
                scr, xs, AF.Square, bias=COLS[:, 1:2], accum_out=COLS[:, 2:3]))
            pr.add("vector", lambda e: e.tensor_scalar(COLS[:, 3:4], COLS[:, 2:3], 1.0 / D, EPS, ALU.mult, ALU.add))
            pr.add("scalar", lambda e: e.activation(COLS[:, 4:5], COLS[:, 3:4], AF.Sqrt))
            pr.add("vector", lambda e: e.reciprocal(COLS[:, 5:6], COLS[:, 4:5]))
            pr.add("vector", lambda e, xs=xs, xn=xn: e.tensor_scalar(
                xn, xs, COLS[:, 1:2], COLS[:, 5:6], ALU.add, ALU.mult))
            pr.add("vector", lambda e, t=t, xn=xn: e.tensor_copy(
                XNT[:, t * 2048:(t + 1) * 2048], xn))
            pr.add("vector", lambda e, t=t, xn=xn, rst=rst: e.tensor_tensor(
                XFNT[:, t * 2048:(t + 1) * 2048], xn, rst, ALU.add))
        if debug:
            pr.dma("sync", lambda e: e.dma_start(out=dbg["dbg_rs"], in_=rs_out))

        for tb in range(2):
            for ti in range(4):
                t = tb * 4 + ti
                xf = A2[:, 0:2048]
                o3 = A2[:, 2048:4096]
                scr = A2[:, 6144:8192]
                pr.add("vector", lambda e, t=t, xf=xf: e.tensor_copy(
                    xf, XFNT[:, t * 2048:(t + 1) * 2048]))
                pr.add("vector", lambda e, xf=xf: e.reduce_sum(COLS[:, 0:1], xf, axis=AXX))
                pr.add("vector", lambda e: e.tensor_scalar_mul(COLS[:, 1:2], COLS[:, 0:1], -1.0 / D))
                pr.add("scalar", lambda e, xf=xf, scr=scr: e.activation(
                    scr, xf, AF.Square, bias=COLS[:, 1:2], accum_out=COLS[:, 2:3]))
                pr.add("vector", lambda e: e.tensor_scalar(COLS[:, 3:4], COLS[:, 2:3], 1.0 / D, EPS, ALU.mult, ALU.add))
                pr.add("scalar", lambda e: e.activation(COLS[:, 4:5], COLS[:, 3:4], AF.Sqrt))
                pr.add("vector", lambda e: e.reciprocal(COLS[:, 5:6], COLS[:, 4:5]))
                pr.add("vector", lambda e, xf=xf, o3=o3: e.tensor_scalar(
                    o3, xf, COLS[:, 1:2], COLS[:, 5:6], ALU.add, ALU.mult))
                for fc in range(16):
                    pr.add("tensor", lambda e, fc=fc, o3=o3: e.transpose(
                        PSF[6][:, 0:128], o3[:, fc * 128:(fc + 1) * 128], IDf[:, :]))
                    pr.add("vector", lambda e, fc=fc, ti=ti: e.tensor_scalar(
                        XN3F[:, fc * 512 + ti * 128: fc * 512 + ti * 128 + 128],
                        PSF[6][:, 0:128], L2S[:, fc:fc + 1], L2B[:, fc:fc + 1],
                        ALU.mult, ALU.add))
            pr.dma("sync", lambda e, tb=tb: [e.dma_start(
                out=XNFR[:, ic * 512:(ic + 1) * 512],
                in_=d_xnfm[ic, :, tb * 512:(tb + 1) * 512]) for ic in range(16)], n=16)

            def mm_dn(e):
                return [e.matmul(PSF[6][0:32, 0:512], WDT[:, ic * 32:(ic + 1) * 32],
                                 XNFR[:, ic * 512:(ic + 1) * 512],
                                 start=(ic == 0), stop=(ic == 15)) for ic in range(16)]
            pr.add("tensor", mm_dn, n=16)
            pr.add("vector", lambda e: e.tensor_copy(DWN[:, :], PSF[6][0:32, 0:512]))

            for fg in range(16):
                buf = fg % 2
                Wb = WSLB[:, buf * 8192: buf * 8192 + 8192]
                pr.dma("gpsimd", lambda e, fg=fg, Wb=Wb: [e.dma_start(
                    out=Wb[:, j * 2048:(j + 1) * 2048].rearrange("q (ic o) -> q ic o", o=128),
                    in_=w_f1T.rearrange("(ic q) o -> q ic o", q=128)
                        [:, :, (fg * 4 + j) * 128:(fg * 4 + j + 1) * 128])
                    for j in range(4)], n=4)
                for j in range(4):
                    def mm_f1(e, j=j, Wb=Wb):
                        return [e.matmul(PSF[j][:, 0:512],
                                         Wb[:, j * 2048 + ic * 128: j * 2048 + (ic + 1) * 128],
                                         XN3F[:, ic * 512:(ic + 1) * 512],
                                         start=(ic == 0), stop=(ic == 15)) for ic in range(16)]
                    pr.add("tensor", mm_f1, n=16)
                for j in range(4):
                    ffc = fg * 4 + j
                    pr.add("scalar", lambda e, j=j, ffc=ffc: e.activation(
                        G1[:, ffc * 512:(ffc + 1) * 512], PSF[j][:, 0:512], AF.Gelu_apprx_tanh))

            for osl in range(4):
                for sl8 in range(8):
                    buf2 = sl8 % 2
                    W2 = WSLB[:, buf2 * 8192: buf2 * 8192 + 4096]
                    pr.dma("gpsimd", lambda e, osl=osl, sl8=sl8, W2=W2: [e.dma_start(
                        out=W2[:, q * 512:(q + 1) * 512],
                        in_=w_f2T[(sl8 * 8 + q) * 128:(sl8 * 8 + q + 1) * 128,
                                  osl * 512:(osl + 1) * 512]) for q in range(8)], n=8)
                    for ti in range(4):
                        def mm_f2(e, ti=ti, sl8=sl8, W2=W2):
                            return [e.matmul(PSF[ti][:, 0:512],
                                             G1[:, (sl8 * 8 + q) * 512 + ti * 128:
                                                (sl8 * 8 + q) * 512 + ti * 128 + 128],
                                             W2[:, q * 512:(q + 1) * 512],
                                             start=(sl8 == 0 and q == 0),
                                             stop=(sl8 == 7 and q == 7)) for q in range(8)]
                        pr.add("tensor", mm_f2, n=8)
                for ti in range(4):
                    t = tb * 4 + ti
                    pr.add("vector", lambda e, t=t, ti=ti, osl=osl: e.tensor_tensor(
                        O2ALL[:, ti * 2048 + osl * 512: ti * 2048 + osl * 512 + 512],
                        PSF[ti][:, 0:512],
                        XFNT[:, t * 2048 + osl * 512: t * 2048 + osl * 512 + 512],
                        ALU.add))

            WUT = WSLB[0:32, 2048:4096]
            pr.dma("gpsimd", lambda e, WUT=WUT: e.dma_start(out=WUT, in_=w_uT))

            for ti in range(4):
                t = tb * 4 + ti
                xs = A2[:, 0:2048]
                sg = A2[:, 2048:4096]
                of = A2[:, 4096:6144]
                f1 = A2[:, 6144:6656]
                f2 = A2[:, 6656:7168]
                f3 = A2[:, 7168:7680]
                pr.dma("sync", lambda e, t=t: e.dma_start(
                    out=MC[:, 0:1], in_=d_m.rearrange("a (t o) -> a t o", o=1)[0, t * 128:(t + 1) * 128]))
                pr.dma("sync", lambda e, t=t: e.dma_start(
                    out=MC[:, 1:2], in_=d_m.rearrange("a (t o) -> a t o", o=1)[1, t * 128:(t + 1) * 128]))
                pr.dma("sync", lambda e, t=t: e.dma_start(out=STL[:, :], in_=d_s[t]))
                pr.dma("sync", lambda e, t=t, xs=xs: e.dma_start(
                    out=xs, in_=x_own[t * 128:(t + 1) * 128, :]))
                for osl in range(4):
                    co = osl * 512
                    pr.add("tensor", lambda e, ti=ti, osl=osl: e.matmul(
                        PSF[5][:, 0:512], DWN[:, ti * 128:(ti + 1) * 128],
                        WUT[:, osl * 512:(osl + 1) * 512], start=True, stop=True))
                    pr.add("vector", lambda e, t=t, co=co, f1=f1: e.tensor_tensor(
                        f1, XNT[:, t * 2048 + co: t * 2048 + co + 512],
                        BREP[:, co:co + 512], ALU.add))
                    pr.add("vector", lambda e, ti=ti, co=co, f1=f1, f2=f2: e.tensor_tensor(
                        f2, O2ALL[:, ti * 2048 + co: ti * 2048 + co + 512], f1, ALU.subtract))
                    pr.add("vector", lambda e, f1=f1, f2=f2, f3=f3: e.scalar_tensor_tensor(
                        f3, f2, MC[:, 1:2], f1, ALU.mult, ALU.add))
                    pr.add("vector", lambda e, co=co, f2=f2: e.tensor_tensor(
                        f2, PSF[5][:, 0:512], BREP[:, co:co + 512], ALU.subtract))
                    pr.add("vector", lambda e, f2=f2, f3=f3: e.scalar_tensor_tensor(
                        f3, f2, MC[:, 0:1], f3, ALU.mult, ALU.add))
                    pr.add("scalar", lambda e, co=co, f3=f3: e.activation(
                        GT[:, co:co + 512], f3, AF.Silu))
                    if debug:
                        pr.add("vector", lambda e, co=co, f3=f3: e.tensor_copy(
                            A2[:, 8192 + co:8192 + co + 512], f3))
                if debug:
                    pr.dma("sync", lambda e, t=t: e.dma_start(
                        out=dbg["dbg_ffn"][t], in_=A2[:, 8192:10240]))
                pr.add("vector", lambda e, sg=sg: e.tensor_tensor(sg, STL[:, :], GT[:, :], ALU.mult))
                pr.add("vector", lambda e, xs=xs, sg=sg, of=of: e.tensor_tensor(of, sg, xs, ALU.add))
                pr.dma("sync", lambda e, t=t, of=of: e.dma_start(
                    out=out[t * 128:(t + 1) * 128, :], in_=of))

        pr.add("sync", lambda e: e.nop())

        with nc.Block() as block:
            pr.realize(block, sems)

    return nc


def _prep_inputs(inputs):
    f32 = lambda a: np.ascontiguousarray(np.asarray(a, np.float32))
    x = f32(inputs["L0_attn_input_states"]).reshape(S, D)
    wip = f32(inputs["L0_attn_in_proj_w"])
    convw = f32(inputs["L0_attn_conv_w"]).reshape(CONV_DIM, 4)
    convb = f32(inputs["L0_attn_conv_b"])
    alog = f32(inputs["L0_attn_A_log"])
    dtb = f32(inputs["L0_attn_dt_bias"])
    dvec = f32(inputs["L0_attn_D"])
    nw = f32(inputs["L0_attn_norm_w"])
    wop = f32(inputs["L0_attn_out_proj_w"])
    sb_ = f32(inputs["L0_ffn_simple_bias"])
    wd = f32(inputs["L0_ffn_w_down"])
    wu = f32(inputs["L0_ffn_w_up"])
    wr = f32(inputs["L0_ffn_w_router"])
    l1s = f32(inputs["L0_ffn_ln1_scale"]); l1b = f32(inputs["L0_ffn_ln1_bias"])
    wq = f32(inputs["L0_ffn_wq"]); wk = f32(inputs["L0_ffn_wk"])
    wv = f32(inputs["L0_ffn_wv"]); wo = f32(inputs["L0_ffn_wo"])
    l2s = f32(inputs["L0_ffn_ln2_scale"]); l2b = f32(inputs["L0_ffn_ln2_bias"])
    wf1 = f32(inputs["L0_ffn_w_ff1"]); wf2 = f32(inputs["L0_ffn_w_ff2"])

    identf = np.eye(128, dtype=np.float32)
    mask_sle = (np.arange(128)[:, None] <= np.arange(128)[None, :]).astype(np.float32)
    amask = np.zeros((4, 128, 512), np.float32)
    for j in range(4):
        amask[j] = (np.arange(128)[:, None] + 128 * j <= np.arange(512)[None, :])
    onesf = np.ones((1, 128), np.float32)

    shared = dict(
        w_ipT=np.ascontiguousarray(wip.T),
        conv_w=convw, conv_b=convb, a_log=alog, dt_bias=dtb, d_vec=dvec,
        norm_w=nw, w_opT=np.ascontiguousarray(wop.T),
        sbias=sb_, w_dT=np.ascontiguousarray(wd.T), w_uT=np.ascontiguousarray(wu.T),
        w_rT=np.ascontiguousarray(wr.T),
        ln1_s=l1s, ln1_b=l1b, ln2_s=l2s, ln2_b=l2b,
        w_f1T=np.ascontiguousarray(wf1.T), w_f2T=np.ascontiguousarray(wf2.T),
        identf=identf, identb_src=identf, mask_sle=mask_sle, amask=amask,
        onesf=onesf,
    )
    woT = np.ascontiguousarray(wo.T)
    in_maps = []
    for c in range(NCORE):
        m = dict(shared)
        m["x_own"] = np.ascontiguousarray(x[c * TOK:(c + 1) * TOK])
        if c == 0:
            m["x_halo"] = np.zeros((3, D), np.float32)
        else:
            m["x_halo"] = np.ascontiguousarray(x[c * TOK - 3:c * TOK])
        hsl = slice(2 * c * DH, (2 * c + 2) * DH)
        m["w_qT"] = np.ascontiguousarray(wq[hsl].T)
        m["w_kT"] = np.ascontiguousarray(wk[hsl].T)
        m["w_vT"] = np.ascontiguousarray(wv[hsl].T)
        m["w_oTs"] = np.ascontiguousarray(woT[hsl])
        g = np.zeros((1, NCORE), np.float32)
        g[0, :c] = 1.0
        m["gates"] = g
        in_maps.append(m)
    return in_maps


def kernel(**inputs):
    if "nc" not in _CACHE:
        _CACHE["nc"] = build_program(debug=False)
    nc = _CACHE["nc"]
    in_maps = _prep_inputs(inputs)
    res = run_bass_kernel_spmd(nc, in_maps, core_ids=list(range(NCORE)))
    outp = np.concatenate([np.asarray(r["out"], np.float32) for r in res.results], axis=0)
    return outp.reshape(1, S, D)


# version 05
# speedup vs baseline: 1.1497x; 1.0459x over previous
import numpy as np
import ml_dtypes
import concourse.bass as bass
import concourse.mybir as mybir
from concourse.bass_utils import run_bass_kernel_spmd

F32 = mybir.dt.float32
BF16 = mybir.dt.bfloat16
AF = mybir.ActivationFunctionType
ALU = mybir.AluOpType
AXX = mybir.AxisListType.X

S, D = 8192, 2048
NCORE = 8
TOK = S // NCORE
NT = TOK // 128
CH = 128
NCH = TOK // CH
H, P, NS = 64, 64, 128
INTER, CONV_DIM, PROJ = 4096, 4352, 8512
NIC = 16
NOC = 67
HA, DH = 16, 128
FF = 8192
RNK = 32
EPS = 1e-5
ASHIFT = 40.0
ASCALE = float(1.0 / np.sqrt(DH))

_CACHE = {}


class Prog:

    LANES = ("sync", "gpsimd", "tensor", "vector", "scalar")
    DMA_LANES = ("sync", "gpsimd")

    def __init__(self):
        self.items = {e: [] for e in self.LANES}
        self.counts = {e: 0 for e in self.LANES}
        self.chain_head = None

    def mark(self):
        return dict(self.counts)

    def add(self, eng, fn, n=1, unit=1, pre=None, waits=None, sem=None):
        chained = waits is None and sem is None
        if sem is not None:
            w = dict(waits) if waits else {}
            w.pop(eng, None)
            self.items[eng].append((w, fn, n, unit, pre, sem))
            return
        SELFW = ("sync", "gpsimd", "vector", "scalar")
        if waits is None:
            w = {}
            if self.chain_head is not None:
                cl, cv = self.chain_head
                if cl != eng:
                    w[cl] = cv
                elif eng in SELFW:
                    w[eng] = self.counts[eng]
            if eng in SELFW and eng not in w:
                w[eng] = self.counts[eng]
        else:
            w = dict(waits)
            w.pop(eng, None)
            if eng in SELFW:
                w[eng] = self.counts[eng]
        self.items[eng].append((w, fn, n, unit, pre, None))
        self.counts[eng] += n * unit
        if chained:
            self.chain_head = (eng, self.counts[eng])

    def dma(self, eng, fn, n=1, pre=None, waits=None, sem=None):
        self.add(eng, fn, n=n, unit=16, pre=pre, waits=waits, sem=sem)

    def async_op(self, eng, fn, done_sem, waits=None):
        w = dict(waits) if waits is not None else self.mark()
        w.pop(eng, None)
        self.items[eng].append((w, fn, 0, done_sem, None, None))

    def realize(self, block, sems):
        def mk(name):
            items = self.items[name]

            def body(e):
                reached = {l: 0 for l in self.LANES}
                for w, fn, n, unit, pre, sem in items:
                    if pre is not None:
                        for psem, pval in pre:
                            e.wait_ge(psem, pval)
                    for lane, v in w.items():
                        if v > reached.get(lane, 0):
                            e.wait_ge(sems[lane], v)
                            reached[lane] = v
                    r = fn(e)
                    insts = r if isinstance(r, (list, tuple)) else [r]
                    if n == 0:
                        insts[-1].then_inc(unit, 1)
                    elif sem is not None:
                        assert len(insts) == n
                        for i in insts:
                            i.then_inc(sem, unit)
                    else:
                        assert len(insts) == n, f"{name}: got {len(insts)} want {n}"
                        for i in insts:
                            i.then_inc(sems[name], unit)
            return body
        block.sync(mk("sync"))
        block.gpsimd(mk("gpsimd"))
        block.tensor(mk("tensor"))
        block.vector(mk("vector"))
        block.scalar(mk("scalar"))


def build_program(debug=False):
    nc = bass.Bass("TRN2", target_bir_lowering=False)
    pr = Prog()

    def din(name, shape, dt=F32):
        return nc.dram_tensor(name, shape, dt, kind="ExternalInput").ap()

    def dint(name, shape, dt=F32):
        return nc.dram_tensor(name, shape, dt).ap()

    x_own = din("x_own", [TOK, D])
    x_halo = din("x_halo", [3, D])
    w_ipT = din("w_ipT", [D, PROJ])
    conv_w = din("conv_w", [CONV_DIM, 4])
    conv_b = din("conv_b", [CONV_DIM])
    a_log = din("a_log", [H])
    dt_bias = din("dt_bias", [H])
    d_vec = din("d_vec", [H])
    norm_w = din("norm_w", [INTER])
    w_opT = din("w_opT", [INTER, D])
    sbias = din("sbias", [D])
    w_dT = din("w_dT", [D, RNK])
    w_uT = din("w_uT", [RNK, D])
    w_rT = din("w_rT", [D, 3])
    ln1_s = din("ln1_s", [D])
    ln1_b = din("ln1_b", [D])
    w_qT = din("w_qT", [D, 2 * DH])
    w_kT = din("w_kT", [D, 2 * DH])
    w_vT = din("w_vT", [D, 2 * DH])
    w_oTs = din("w_oTs", [2 * DH, D])
    ln2_s = din("ln2_s", [D])
    ln2_b = din("ln2_b", [D])
    w_f1T = din("w_f1T", [D, FF])
    w_f2T = din("w_f2T", [FF, D])
    identf = din("identf", [128, 128])
    identb_src = din("identb_src", [128, 128])
    mask_sle = din("mask_sle", [128, 128])
    amask = din("amask", [4, 128, 512])
    gates = din("gates", [1, NCORE])
    onesf = din("onesf", [1, 128])
    out = nc.dram_tensor("out", [TOK, D], F32, kind="ExternalOutput").ap()

    g_xn2_in = dint("g_xn2_in", [D, TOK], BF16)
    g_xn2 = dint("g_xn2", [NCORE * D, TOK], BF16)
    g_st_in = dint("g_st_in", [1, NS * H * P], BF16)
    g_st = dint("g_st", [NCORE, NS * H * P], BF16)
    g_T_in = dint("g_T_in", [1, H])
    g_T = dint("g_T", [NCORE, H])
    rs_in = dint("rs_in", [S, D])
    rs_out = dint("rs_out", [TOK, D])

    d_xnfm = dint("d_xnfm", [NIC, 128, TOK], BF16)
    d_m = dint("d_m", [2, TOK])
    d_xt = dint("d_xt", [NCH, 128, H * P], BF16)
    d_sg = dint("d_sg", [32, 128, TOK], BF16)
    d_dres = dint("d_dres", [32, 128, TOK], BF16)
    d_Cfm = dint("d_Cfm", [NCH, 128, CH], BF16)
    d_gmt = dint("d_gmt", [NCH, 128, CH])
    d_acum = dint("d_acum", [NCH, H, CH])
    d_nacT = dint("d_nacT", [NCH, CH, H])
    d_sin = dint("d_sin", [NCH, 128, H * P], BF16)
    d_lam = dint("d_lam", [NCH, H])
    d_yn = dint("d_yn", [NCH, 128, INTER], BF16)
    d_s = dint("d_s", [NT, 128, D], BF16)

    dbg = {}
    if debug:
        def dbgout(name, shape, dt=F32):
            dbg[name] = nc.dram_tensor(name, shape, dt, kind="ExternalOutput").ap()
        dbgout("dbg_xn2", [D, TOK])
        dbgout("dbg_csl", [34, 128, 512])
        dbgout("dbg_dt", [64, 1024])
        dbgout("dbg_sin", [NCH, 128, H * P])
        dbgout("dbg_stot", [NCH, 128, H * P])
        dbgout("dbg_yg", [NCH, 128, H * P])
        dbgout("dbg_s", [NT, 128, D])
        dbgout("dbg_att", [2 * DH, S])
        dbgout("dbg_rs", [TOK, D])
        dbgout("dbg_ffn", [NT, 128, D])

    import contextlib
    stack = contextlib.ExitStack()

    def sb(name, shape, dt=F32):
        return stack.enter_context(nc.sbuf_tensor(name, shape, dt))

    def psx(name, shape, dt=F32):
        return stack.enter_context(nc.psum_tensor(name, shape, dt))

    with stack:
        sems = {l: nc.alloc_semaphore("lane_" + l) for l in Prog.LANES}
        cc1 = nc.alloc_semaphore("cc1")
        cc2 = nc.alloc_semaphore("cc2")
        cc3 = nc.alloc_semaphore("cc3")
        cc4 = nc.alloc_semaphore("cc4")

        IDf = sb("IDf", [128, 128], F32)
        IDb = sb("IDb", [128, 128], BF16)
        MSL = sb("MSL", [128, 128], F32)
        AMK = sb("AMK", [128, 4 * 512], BF16)
        ONESB = sb("ONESB", [1, 128], BF16)
        ONESCB = sb("ONESCB", [128, 1], BF16)
        ONESCF = sb("ONESCF", [128, 1], F32)
        ONES64 = sb("ONES64", [64, 128], F32)
        CONVW = sb("CONVW", [128, 34 * 4], F32)
        CONVB = sb("CONVB", [128, 34], F32)
        DCOL = sb("DCOL", [128, 32], F32)
        NWCOL = sb("NWCOL", [128, 32], F32)
        ACOL = sb("ACOL", [64, 1], F32)
        DTBC = sb("DTBC", [64, 1], F32)
        L1S = sb("L1S", [128, 16], F32)
        L1B = sb("L1B", [128, 16], F32)
        L2S = sb("L2S", [128, 16], F32)
        L2B = sb("L2B", [128, 16], F32)
        WRT = sb("WRT", [128, 48], F32)
        GATES = sb("GATES", [1, NCORE], F32)
        GREP = sb("GREP", [128, NCORE], F32)
        WDT = sb("WDT", [128, 16 * RNK], BF16)
        BREP = sb("BREP", [128, D], BF16)
        ROWS = sb("ROWS", [1, 1024], F32)
        ROWB = sb("ROWB", [1, 1024], BF16)
        COLS = sb("COLS", [128, 32], F32)
        SHFT = sb("SHFT", [128, 1], F32)
        SMF = sb("SMF", [128, 640], F32)

        PSF = [psx(f"PS{i}", [128, 512], F32) for i in range(7)]
        PS_HB = psx("PS_HB", [128, 1024], BF16)

        A1 = sb("A1", [128, 17408], BF16)
        A2 = sb("A2", [128, 16384], F32)
        A3 = sb("A3", [128, 16384], BF16)
        A4 = sb("A4", [128, 16384], BF16)
        A5 = sb("A5", [128, 8192], BF16)
        A7 = sb("A7", [128, 2048], BF16)
        A2b = A2.ap().bitcast(BF16)

        pr.dma("sync", lambda e: e.dma_start(out=IDf[:, :], in_=identf))
        pr.dma("gpsimd", lambda e: e.dma_start(out=IDb[:, :], in_=identb_src))
        pr.dma("sync", lambda e: e.dma_start(out=MSL[:, :], in_=mask_sle))
        pr.dma("gpsimd", lambda e: e.dma_start(
            out=AMK[:, :].rearrange("p (j t) -> p j t", t=512),
            in_=amask.rearrange("j p t -> p j t")))
        pr.dma("gpsimd", lambda e: e.dma_start(out=ONESB[:, :], in_=onesf))
        pr.dma("gpsimd", lambda e: e.dma_start(
            out=ONESCB[:, :], in_=onesf.rearrange("a (p o) -> (a p) o", o=1)))
        pr.dma("sync", lambda e: e.dma_start(
            out=ONESCF[:, :], in_=onesf.rearrange("a (p o) -> (a p) o", o=1)))
        pr.add("vector", lambda e: e.memset(ONES64[:, :], 1.0))
        pr.add("vector", lambda e: e.memset(SHFT[:, :], -ASHIFT))
        pr.dma("sync", lambda e: e.dma_start(
            out=CONVW[:, :].rearrange("p (fc k) -> p fc k", k=4),
            in_=conv_w.rearrange("(fc p) k -> p fc k", p=128)))
        pr.dma("sync", lambda e: e.dma_start(
            out=CONVB[:, :], in_=conv_b.rearrange("(fc p) -> p fc", p=128), allow_slow_non_contiguous=True))
        pr.dma("sync", lambda e: e.dma_start(
            out=DCOL[0:64, :],
            in_=d_vec.rearrange("(fc h2) -> fc h2", h2=2)[:, 0:1]
                     .rearrange("fc a -> a fc").broadcast_to((64, 32)), allow_slow_non_contiguous=True))
        pr.dma("sync", lambda e: e.dma_start(
            out=DCOL[64:128, :],
            in_=d_vec.rearrange("(fc h2) -> fc h2", h2=2)[:, 1:2]
                     .rearrange("fc a -> a fc").broadcast_to((64, 32)), allow_slow_non_contiguous=True))
        pr.dma("sync", lambda e: e.dma_start(
            out=NWCOL[:, :], in_=norm_w.rearrange("(fc q) -> q fc", q=128), allow_slow_non_contiguous=True))
        pr.dma("sync", lambda e: e.dma_start(out=ACOL[:, :], in_=a_log.rearrange("(h o) -> h o", o=1)))
        pr.dma("sync", lambda e: e.dma_start(out=DTBC[:, :], in_=dt_bias.rearrange("(h o) -> h o", o=1)))
        for t, src in ((L1S, ln1_s), (L1B, ln1_b), (L2S, ln2_s), (L2B, ln2_b)):
            pr.dma("sync", lambda e, t=t, src=src: e.dma_start(
                out=t[:, :], in_=src.rearrange("(fc q) -> q fc", q=128), allow_slow_non_contiguous=True))
        pr.dma("gpsimd", lambda e: e.dma_start(
            out=BREP[:, :], in_=sbias.rearrange("(o d) -> o d", o=1).broadcast_to((128, D))))
        pr.dma("sync", lambda e: e.dma_start(
            out=WRT[:, :].rearrange("q (ic r) -> q ic r", r=3),
            in_=w_rT.rearrange("(ic q) r -> q ic r", q=128)))
        pr.dma("sync", lambda e: e.dma_start(out=GATES[:, :], in_=gates))
        pr.dma("sync", lambda e: e.dma_start(out=GREP[:, :], in_=gates.broadcast_to((128, NCORE))))
        pr.dma("gpsimd", lambda e: e.dma_start(
            out=WDT[:, :].rearrange("q (ic r) -> q ic r", r=RNK),
            in_=w_dT.rearrange("(ic q) r -> q ic r", q=128)))
        pr.add("scalar", lambda e: e.activation(ACOL[:, 0:1], ACOL[:, 0:1], AF.Exp))
        pr.add("vector", lambda e: e.tensor_scalar_mul(ACOL[:, 0:1], ACOL[:, 0:1], -1.0))

        XNFM = A4
        for t in range(NT):
            xs = A2[:, 0:2048]
            xn = A2[:, 2048:4096]
            o1 = A2[:, 4096:6144]
            scr = A2[:, 6144:8192]
            xnf32 = A2[:, 8192:10240]
            xn2t = A7[:, 0:2048]
            pr.dma("sync", lambda e, t=t, xs=xs: e.dma_start(
                out=xs, in_=x_own[t * 128:(t + 1) * 128, :]))
            pr.add("vector", lambda e, xs=xs: e.reduce_sum(COLS[:, 0:1], xs, axis=AXX))
            pr.add("vector", lambda e: e.tensor_scalar_mul(COLS[:, 1:2], COLS[:, 0:1], -1.0 / D))
            pr.add("scalar", lambda e, xs=xs, scr=scr: e.activation(
                scr, xs, AF.Square, bias=COLS[:, 1:2], accum_out=COLS[:, 2:3]))
            pr.add("vector", lambda e: e.tensor_scalar(COLS[:, 3:4], COLS[:, 2:3], 1.0 / D, EPS, ALU.mult, ALU.add))
            pr.add("scalar", lambda e: e.activation(COLS[:, 4:5], COLS[:, 3:4], AF.Sqrt))
            pr.add("vector", lambda e: e.reciprocal(COLS[:, 5:6], COLS[:, 4:5]))
            pr.add("vector", lambda e, xs=xs, xn=xn: e.tensor_scalar(
                xn, xs, COLS[:, 1:2], COLS[:, 5:6], ALU.add, ALU.mult))
            pr.add("vector", lambda e, xn=xn: e.reduce_sum(COLS[:, 8:9], xn, axis=AXX))
            pr.add("vector", lambda e: e.tensor_scalar_mul(COLS[:, 9:10], COLS[:, 8:9], -1.0 / D))
            pr.add("scalar", lambda e, xn=xn, scr=scr: e.activation(
                scr, xn, AF.Square, bias=COLS[:, 9:10], accum_out=COLS[:, 10:11]))
            pr.add("vector", lambda e: e.tensor_scalar(COLS[:, 11:12], COLS[:, 10:11], 1.0 / D, EPS, ALU.mult, ALU.add))
            pr.add("scalar", lambda e: e.activation(COLS[:, 12:13], COLS[:, 11:12], AF.Sqrt))
            pr.add("vector", lambda e: e.reciprocal(COLS[:, 13:14], COLS[:, 12:13]))
            pr.add("vector", lambda e, xn=xn, o1=o1: e.tensor_scalar(
                o1, xn, COLS[:, 9:10], COLS[:, 13:14], ALU.add, ALU.mult))
            for fc in range(16):
                pr.add("tensor", lambda e, fc=fc, xn=xn: e.transpose(
                    PSF[6][:, 0:128], xn[:, fc * 128:(fc + 1) * 128], IDf[:, :]))
                pr.add("vector", lambda e, fc=fc, t=t: e.tensor_copy(
                    XNFM[:, fc * 1024 + t * 128: fc * 1024 + t * 128 + 128], PSF[6][:, 0:128]))
                pr.add("vector", lambda e, fc=fc, xnf32=xnf32: e.tensor_copy(
                    xnf32[:, fc * 128:(fc + 1) * 128], PSF[6][:, 0:128]))
                pr.add("tensor", lambda e, fc=fc, o1=o1: e.transpose(
                    PSF[6][:, 128:256], o1[:, fc * 128:(fc + 1) * 128], IDf[:, :]))
                pr.add("vector", lambda e, fc=fc, xn2t=xn2t: e.tensor_scalar(
                    xn2t[:, fc * 128:(fc + 1) * 128], PSF[6][:, 128:256],
                    L1S[:, fc:fc + 1], L1B[:, fc:fc + 1], ALU.mult, ALU.add))
            pr.dma("sync", lambda e, t=t: [e.dma_start(
                out=d_xnfm[fc, :, t * 128:(t + 1) * 128],
                in_=XNFM[:, fc * 1024 + t * 128: fc * 1024 + t * 128 + 128])
                for fc in range(16)], n=16)
            pr.dma("gpsimd", lambda e, t=t, xn2t=xn2t: [e.dma_start(
                out=g_xn2_in.rearrange("(fc q) t -> fc q t", q=128)[fc, :, t * 128:(t + 1) * 128],
                in_=xn2t[:, fc * 128:(fc + 1) * 128]) for fc in range(16)], n=16)

            def mm_rt(e, xnf32=xnf32):
                return [e.matmul(PSF[5][:, 0:3], xnf32[:, ic * 128:(ic + 1) * 128],
                                 WRT[:, ic * 3:(ic + 1) * 3],
                                 start=(ic == 0), stop=(ic == 15)) for ic in range(16)]
            pr.add("tensor", mm_rt, n=16)
            pr.add("vector", lambda e: e.tensor_copy(COLS[:, 24:27], PSF[5][:, 0:3]))
            pr.add("vector", lambda e: e.tensor_tensor(COLS[:, 16:17], COLS[:, 26:27], COLS[:, 24:25], ALU.is_gt))
            pr.add("vector", lambda e: e.tensor_tensor(COLS[:, 17:18], COLS[:, 26:27], COLS[:, 25:26], ALU.is_gt))
            pr.add("vector", lambda e: e.tensor_tensor(COLS[:, 18:19], COLS[:, 25:26], COLS[:, 24:25], ALU.is_gt))
            pr.add("vector", lambda e: e.tensor_tensor(COLS[:, 19:20], COLS[:, 16:17], COLS[:, 17:18], ALU.mult))
            pr.add("vector", lambda e: e.tensor_scalar(COLS[:, 20:21], COLS[:, 17:18], -1.0, 1.0, ALU.mult, ALU.add))
            pr.add("vector", lambda e: e.tensor_tensor(COLS[:, 21:22], COLS[:, 18:19], COLS[:, 20:21], ALU.mult))
            pr.dma("sync", lambda e, t=t: e.dma_start(
                out=d_m.rearrange("a (t o) -> a t o", o=1)[1, t * 128:(t + 1) * 128], in_=COLS[:, 19:20]))
            pr.dma("sync", lambda e, t=t: e.dma_start(
                out=d_m.rearrange("a (t o) -> a t o", o=1)[0, t * 128:(t + 1) * 128], in_=COLS[:, 21:22]))
        if debug:
            pr.dma("gpsimd", lambda e: e.dma_start(out=dbg["dbg_xn2"], in_=g_xn2_in))
        pr.async_op("gpsimd", lambda e: e.collective_compute(
            "AllGather", ALU.bypass, replica_groups=[list(range(NCORE))],
            ins=[g_xn2_in.opt()], outs=[g_xn2.opt()]), cc1)

        pr.dma("sync", lambda e: e.dma_start(out=A2[0:3, 0:2048], in_=x_halo))
        pr.add("vector", lambda e: e.reduce_sum(COLS[0:3, 0:1], A2[0:3, 0:2048], axis=AXX))
        pr.add("vector", lambda e: e.tensor_scalar_mul(COLS[0:3, 1:2], COLS[0:3, 0:1], -1.0 / D))
        pr.add("scalar", lambda e: e.activation(A2[0:3, 2048:4096], A2[0:3, 0:2048], AF.Square,
                                                bias=COLS[0:3, 1:2], accum_out=COLS[0:3, 2:3]))
        pr.add("vector", lambda e: e.tensor_scalar(COLS[0:3, 3:4], COLS[0:3, 2:3], 1.0 / D, EPS, ALU.mult, ALU.add))
        pr.add("scalar", lambda e: e.activation(COLS[0:3, 4:5], COLS[0:3, 3:4], AF.Sqrt))
        pr.add("vector", lambda e: e.reciprocal(COLS[0:3, 5:6], COLS[0:3, 4:5]))
        pr.add("vector", lambda e: e.tensor_scalar(A2[0:3, 0:2048], A2[0:3, 0:2048],
                                                   COLS[0:3, 1:2], COLS[0:3, 5:6], ALU.add, ALU.mult))
        XNH = sb("XNH", [128, 48], BF16)
        for ic in range(16):
            pr.add("tensor", lambda e, ic=ic: e.transpose(
                PSF[6][:, 0:3], A2[0:3, ic * 128:(ic + 1) * 128], IDf[0:3, 0:3]))
            pr.add("vector", lambda e, ic=ic: e.tensor_copy(XNH[:, ic * 3:ic * 3 + 3], PSF[6][:, 0:3]))

        CSL = A1
        IPO = A7
        HALO = sb("HALO", [128, 102], BF16)
        WSL = A3
        SILG = A2b[:, 24576:25600]
        DTX = A2[0:64, 6144:8192]
        ACM = sb("ACM", [64, 384], F32)
        TSEG = sb("TSEG", [64, 1], F32)
        XTI = A2b[:, 16384:20480]
        X2T = A2b[:, 20480:24576]
        KSM = sb("KSM", [128, 256], BF16)
        RUNA = A2b
        pr.add("vector", lambda e: e.memset(TSEG[:, :], 0.0))
        pr.add("vector", lambda e: e.memset(RUNA[:, 0:4096], 0.0))

        for tb in range(2):
            for og in range(17):
                ocs = list(range(og * 4, min(og * 4 + 4, NOC)))
                buf = og % 2
                Wb = WSL[:, buf * 8192: buf * 8192 + 8192]

                def ld_w(e, ocs=ocs, Wb=Wb):
                    ins = []
                    for j, oc in enumerate(ocs):
                        ow = 128 if oc < 66 else 64
                        ins.append(e.dma_start(
                            out=Wb[:, j * 2048: j * 2048 + 16 * ow]
                                .rearrange("q (ic o) -> q ic o", o=ow),
                            in_=w_ipT.rearrange("(ic q) o -> q ic o", q=128)[:, :, oc * 128:oc * 128 + ow]))
                    return ins
                pr.dma("gpsimd", ld_w, n=len(ocs))

                for j, oc in enumerate(ocs):
                    ow = 128 if oc < 66 else 64
                    bank = PSF[j]

                    def mm_ip(e, j=j, ow=ow, bank=bank, Wb=Wb, tb=tb):
                        return [e.matmul(bank[0:ow, 0:512],
                                         Wb[:, j * 2048 + ic * ow: j * 2048 + (ic + 1) * ow],
                                         XNFM[:, ic * 1024 + tb * 512: ic * 1024 + tb * 512 + 512],
                                         start=(ic == 0), stop=(ic == 15)) for ic in range(16)]
                    pr.add("tensor", mm_ip, n=16)
                    if tb == 0 and 32 <= oc <= 65:
                        def mm_halo(e, j=j, Wb=Wb):
                            return [e.matmul(PSF[4][0:128, 500 + j * 3:503 + j * 3],
                                             Wb[:, j * 2048 + ic * 128: j * 2048 + (ic + 1) * 128],
                                             XNH[:, ic * 3:(ic + 1) * 3],
                                             start=(ic == 0), stop=(ic == 15)) for ic in range(16)]
                        pr.add("tensor", mm_halo, n=16)
                        pr.add("vector", lambda e, j=j, oc=oc: e.tensor_copy(
                            HALO[:, (oc - 32) * 3:(oc - 32) * 3 + 3],
                            PSF[4][0:128, 500 + j * 3:503 + j * 3]))
                    if oc == 66:
                        pr.add("vector", lambda e, bank=bank: e.tensor_copy(DTX[:, 0:512], bank[0:64, 0:512]))
                    else:
                        pr.add("vector", lambda e, j=j, bank=bank: e.tensor_copy(
                            IPO[:, (j % 4) * 512:(j % 4) * 512 + 512], bank[0:128, 0:512]))
                for j, oc in enumerate(ocs):
                    src = IPO[:, (j % 4) * 512:(j % 4) * 512 + 512]
                    if oc < 32:
                        pr.add("scalar", lambda e, src=src: e.activation(SILG[:, 0:512], src, AF.Silu))
                        pr.dma("sync", lambda e, oc=oc, tb=tb: e.dma_start(
                            out=d_sg[oc, :, tb * 512:(tb + 1) * 512], in_=SILG[:, 0:512]))
                    elif oc < 66:
                        fc = oc - 32
                        acc = SMF[:, 0:512]
                        pr.add("vector", lambda e, src=src, fc=fc, acc=acc: e.tensor_scalar(
                            acc, src, CONVW[:, fc * 4 + 3:fc * 4 + 4], None, ALU.mult))
                        for k in (2, 1, 0):
                            sh = 3 - k
                            pr.add("vector", lambda e, src=src, fc=fc, k=k, sh=sh, acc=acc:
                                   e.scalar_tensor_tensor(
                                       acc[:, sh:512], src[:, 0:512 - sh],
                                       CONVW[:, fc * 4 + k:fc * 4 + k + 1],
                                       acc[:, sh:512], ALU.mult, ALU.add))
                            pr.add("vector", lambda e, fc=fc, k=k, sh=sh, acc=acc:
                                   e.scalar_tensor_tensor(
                                       acc[:, 0:sh], HALO[:, fc * 3 + 3 - sh:fc * 3 + 3],
                                       CONVW[:, fc * 4 + k:fc * 4 + k + 1],
                                       acc[:, 0:sh], ALU.mult, ALU.add))
                        pr.add("vector", lambda e, src=src, fc=fc: e.tensor_copy(
                            HALO[:, fc * 3:fc * 3 + 3], src[:, 509:512]))
                        pr.add("scalar", lambda e, fc=fc, acc=acc: e.activation(
                            CSL[:, fc * 512:fc * 512 + 512], acc, AF.Silu, bias=CONVB[:, fc:fc + 1]))
            for fc in range(32):
                pr.dma("sync", lambda e, fc=fc, tb=tb: e.dma_start(
                    out=SILG[:, 0:512], in_=d_sg[fc, :, tb * 512:(tb + 1) * 512]))
                pr.add("vector", lambda e, fc=fc: e.scalar_tensor_tensor(
                    SILG[:, 512:1024], CSL[:, fc * 512:fc * 512 + 512],
                    DCOL[:, fc:fc + 1], SILG[:, 0:512], ALU.mult, ALU.mult))
                pr.dma("sync", lambda e, fc=fc, tb=tb: e.dma_start(
                    out=d_dres[fc, :, tb * 512:(tb + 1) * 512], in_=SILG[:, 512:1024]))
            if debug and tb == 0:
                pr.dma("gpsimd", lambda e: [e.dma_start(
                    out=dbg["dbg_csl"][fc], in_=CSL[:, fc * 512:(fc + 1) * 512]) for fc in range(34)], n=34)

            pr.add("vector", lambda e: e.tensor_scalar(DTX[:, 512:1024], DTX[:, 0:512], DTBC[:, 0:1], None, ALU.add))
            pr.add("vector", lambda e: e.tensor_scalar(DTX[:, 1024:1536], DTX[:, 512:1024], -1.0, None, ALU.mult))
            pr.add("vector", lambda e: e.tensor_max(DTX[:, 1024:1536], DTX[:, 1024:1536], DTX[:, 512:1024]))
            pr.add("scalar", lambda e: e.activation(DTX[:, 1024:1536], DTX[:, 1024:1536], AF.Exp, scale=-1.0))
            pr.add("scalar", lambda e: e.activation(DTX[:, 1024:1536], DTX[:, 1024:1536], AF.Ln, bias=1.0))
            pr.add("vector", lambda e: e.tensor_scalar_max(DTX[:, 512:1024], DTX[:, 512:1024], 0.0))
            pr.add("vector", lambda e: e.tensor_add(DTX[:, 512:1024], DTX[:, 512:1024], DTX[:, 1024:1536]))
            pr.add("vector", lambda e: e.tensor_scalar(DTX[:, 1536:2048], DTX[:, 512:1024], ACOL[:, 0:1], None, ALU.mult))
            if debug and tb == 0:
                pr.dma("sync", lambda e: e.dma_start(out=dbg["dbg_dt"][:, 0:512], in_=DTX[:, 512:1024]))
                pr.dma("sync", lambda e: e.dma_start(out=dbg["dbg_dt"][:, 512:1024], in_=DTX[:, 1536:2048]))

            for ci in range(4):
                c = tb * 4 + ci
                cs = slice(ci * 128, ci * 128 + 128)
                pr.add("vector", lambda e, cs=cs: e.tensor_tensor_scan(
                    ACM[:, 0:128], ONES64[:, :], DTX[:, 1536:2048][:, cs], 0.0, ALU.mult, ALU.add))
                pr.add("vector", lambda e: e.tensor_add(TSEG[:, 0:1], TSEG[:, 0:1], ACM[:, 127:128]))
                pr.dma("sync", lambda e, c=c: e.dma_start(
                    out=d_lam.rearrange("c (h o) -> c h o", o=1)[c], in_=ACM[:, 127:128]))
                pr.dma("sync", lambda e, c=c: e.dma_start(out=d_acum[c], in_=ACM[:, 0:128]))
                pr.add("tensor", lambda e: e.transpose(PSF[6][:, 0:64], ACM[:, 0:128], IDf[0:64, 0:64]))
                pr.add("vector", lambda e: e.tensor_scalar_mul(SMF[:, 512:576], PSF[6][:, 0:64], -1.0))
                pr.dma("sync", lambda e, c=c: e.dma_start(out=d_nacT[c], in_=SMF[:, 512:576]))
                pr.add("vector", lambda e: e.tensor_scalar(
                    ACM[:, 128:256], ACM[:, 0:128], -1.0, ACM[:, 127:128], ALU.mult, ALU.add))
                pr.add("scalar", lambda e: e.activation(ACM[:, 128:256], ACM[:, 128:256], AF.Exp))
                pr.add("tensor", lambda e: e.transpose(PSF[6][:, 64:128], ACM[:, 128:256], IDf[0:64, 0:64]))
                pr.add("vector", lambda e: e.tensor_copy(KSM[:, 0:64], PSF[6][:, 64:128]))
                pr.dma("sync", lambda e, c=c: e.dma_start(out=ROWS[0:1, 0:64], in_=d_lam[c:c + 1, :]))
                pr.add("scalar", lambda e: e.activation(ROWS[0:1, 64:128], ROWS[0:1, 0:64], AF.Exp))
                pr.add("vector", lambda e: e.tensor_copy(ROWB[0:1, 0:64], ROWS[0:1, 64:128]))
                pr.add("tensor", lambda e, cs=cs: e.transpose(
                    PSF[6][:, 128:192], DTX[:, 512:1024][:, cs], IDf[0:64, 0:64]))
                pr.add("vector", lambda e: e.tensor_copy(KSM[:, 64:128], PSF[6][:, 128:192]))
                pr.dma("sync", lambda e, c=c, cs=cs: e.dma_start(
                    out=d_Cfm[c], in_=CSL[:, 33 * 512:34 * 512][:, cs]))
                pr.add("tensor", lambda e, cs=cs: e.transpose(
                    PS_HB[:, 0:128], CSL[:, 32 * 512:33 * 512][:, cs], IDb[:, :]))
                pr.add("vector", lambda e: e.tensor_copy(KSM[:, 128:256], PS_HB[:, 0:128]))
                pr.add("tensor", lambda e, cs=cs: e.matmul(
                    PSF[6][:, 192:320], CSL[:, 32 * 512:33 * 512][:, cs],
                    CSL[:, 33 * 512:34 * 512][:, cs], start=True, stop=True))
                pr.add("vector", lambda e: e.tensor_tensor(SMF[:, 0:128], PSF[6][:, 192:320], MSL[:, :], ALU.mult))
                pr.dma("sync", lambda e, c=c: e.dma_start(out=d_gmt[c], in_=SMF[:, 0:128]))
                for fc in range(32):
                    pr.add("tensor", lambda e, fc=fc, cs=cs: e.transpose(
                        PS_HB[:, 128:256], CSL[:, fc * 512:(fc + 1) * 512][:, cs], IDb[:, :]))
                    pr.add("vector", lambda e, fc=fc: e.tensor_tensor(
                        XTI[:, fc * 128:(fc + 1) * 128].rearrange("p (h w) -> p h w", h=2),
                        PS_HB[:, 128:256].rearrange("p (h w) -> p h w", h=2),
                        KSM[:, 64 + 2 * fc:64 + 2 * fc + 2].rearrange("p (h o) -> p h o", o=1)
                        .broadcast_to((128, 2, 64)), ALU.mult))
                pr.dma("sync", lambda e, c=c: e.dma_start(out=d_xt[c], in_=XTI[:, :]))
                pr.add("vector", lambda e: e.tensor_tensor(
                    X2T[:, :].rearrange("p (h w) -> p h w", h=64),
                    XTI[:, :].rearrange("p (h w) -> p h w", h=64),
                    KSM[:, 0:64].rearrange("p (h o) -> p h o", o=1).broadcast_to((128, 64, 64)),
                    ALU.mult))
                for jj in range(8):
                    pr.add("tensor", lambda e, jj=jj: e.matmul(
                        PSF[jj % 2][:, 0:512], KSM[:, 128:256],
                        X2T[:, jj * 512:(jj + 1) * 512], start=True, stop=True))
                    pr.add("vector", lambda e, jj=jj: e.tensor_copy(
                        RUNA[:, 8192 + jj * 512:8192 + (jj + 1) * 512], PSF[jj % 2][:, 0:512]))
                pr.dma("sync", lambda e, c=c: e.dma_start(out=d_sin[c], in_=RUNA[:, 0:4096]))
                for jj in range(8):
                    pr.add("tensor", lambda e, jj=jj: e.matmul(
                        PSF[2 + jj % 2][:, 0:512], ONESB[0:1, :],
                        ROWB[0:1, jj * 8:jj * 8 + 8].rearrange("a (h o) -> a h o", o=1)
                        .broadcast_to((1, 8, 64)), start=True, stop=True))
                    pr.add("vector", lambda e, jj=jj: e.tensor_copy(
                        RUNA[:, 4096 + jj * 512:4096 + (jj + 1) * 512], PSF[2 + jj % 2][:, 0:512]))
                pr.add("vector", lambda e: e.tensor_tensor(
                    RUNA[:, 0:4096], RUNA[:, 0:4096], RUNA[:, 4096:8192], ALU.mult))
                pr.add("vector", lambda e: e.tensor_tensor(
                    RUNA[:, 0:4096], RUNA[:, 0:4096], RUNA[:, 8192:12288], ALU.add))

        pr.dma("sync", lambda e: e.dma_start(
            out=g_st_in.rearrange("a (n hp) -> (a n) hp", n=128), in_=RUNA[:, 0:4096]))
        pr.dma("sync", lambda e: e.dma_start(
            out=g_T_in.rearrange("a (h o) -> (a h) o", o=1), in_=TSEG[:, 0:1]))
        pr.async_op("gpsimd", lambda e: e.collective_compute(
            "AllGather", ALU.bypass, replica_groups=[list(range(NCORE))],
            ins=[g_st_in.opt()], outs=[g_st.opt()]), cc2)
        pr.async_op("gpsimd", lambda e: e.collective_compute(
            "AllGather", ALU.bypass, replica_groups=[list(range(NCORE))],
            ins=[g_T_in.opt()], outs=[g_T.opt()]), cc3)
        if debug:
            pr.dma("gpsimd", lambda e: [e.dma_start(out=dbg["dbg_sin"][c], in_=d_sin[c])
                                      for c in range(NCH)], n=NCH)

        QKH = A2b
        VT = A3
        WQ3 = A4
        X2S = A1
        OT2 = A5
        OT1 = A1
        PT = A7[:, 0:1024]
        OEV = A7[:, 1024:2048]
        pr.dma("gpsimd", lambda e: [e.dma_start(
            out=WQ3[:, i * 4096:(i + 1) * 4096].rearrange("q (ic o) -> q ic o", o=256),
            in_=w.rearrange("(ic q) o -> q ic o", q=128))
            for i, w in enumerate((w_qT, w_kT, w_vT))], n=3, pre=[(cc1, 1)])

        for qb in range(16):
            src_core = qb // 2
            tloc = (qb % 2) * 512
            pr.dma("sync", lambda e, src_core=src_core, tloc=tloc: [e.dma_start(
                out=X2S[:, ic * 512:(ic + 1) * 512],
                in_=g_xn2[src_core * D + ic * 128: src_core * D + (ic + 1) * 128, tloc:tloc + 512])
                for ic in range(16)], n=16)
            for wi in range(2):
                for hc in range(2):
                    def mm_qk(e, wi=wi, hc=hc):
                        return [e.matmul(PSF[wi * 2 + hc][:, 0:512],
                                         WQ3[:, wi * 4096 + ic * 256 + hc * 128:
                                             wi * 4096 + ic * 256 + hc * 128 + 128],
                                         X2S[:, ic * 512:(ic + 1) * 512],
                                         start=(ic == 0), stop=(ic == 15)) for ic in range(16)]
                    pr.add("tensor", mm_qk, n=16)
                    off = wi * 16384 + hc * 8192 + qb * 512
                    pr.add("vector", lambda e, wi=wi, hc=hc, off=off: e.tensor_copy(
                        QKH[:, off:off + 512], PSF[wi * 2 + hc][:, 0:512]))
            for t4 in range(4):
                def mm_v(e, t4=t4):
                    return [e.matmul(PSF[4 + t4 % 2][:, 0:256],
                                     X2S[:, ic * 512 + t4 * 128: ic * 512 + t4 * 128 + 128],
                                     WQ3[:, 2 * 4096 + ic * 256: 2 * 4096 + (ic + 1) * 256],
                                     start=(ic == 0), stop=(ic == 15)) for ic in range(16)]
                pr.add("tensor", mm_v, n=16)
                kb = qb * 4 + t4
                pr.add("vector", lambda e, kb=kb, t4=t4: e.tensor_copy(
                    VT[:, kb * 256:kb * 256 + 256], PSF[4 + t4 % 2][:, 0:256]))

        for h in range(2):
            OST = OT1[:, 8192:16384] if h == 1 else OT2[:, 0:8192]
            for qb in range(16):
                nkb = 4 * qb + 4
                for kb in range(nkb):
                    sbank = PSF[kb % 2]
                    pr.add("tensor", lambda e, h=h, qb=qb, kb=kb, sbank=sbank: e.matmul(
                        sbank[:, 0:512],
                        QKH[:, 16384 + h * 8192 + kb * 128: 16384 + h * 8192 + kb * 128 + 128],
                        QKH[:, h * 8192 + qb * 512: h * 8192 + qb * 512 + 512],
                        start=True, stop=True))
                    pt = PT[:, (kb % 2) * 512:(kb % 2) * 512 + 512]
                    pr.add("scalar", lambda e, pt=pt, sbank=sbank: e.activation(
                        pt, sbank[:, 0:512], AF.Exp, bias=SHFT[:, 0:1], scale=ASCALE))
                    j = kb - 4 * qb
                    if j >= 0:
                        pr.add("vector", lambda e, pt=pt, j=j: e.tensor_tensor(
                            pt, pt, AMK[:, j * 512:(j + 1) * 512], ALU.mult))
                    pr.add("tensor", lambda e, h=h, kb=kb, pt=pt, f=(kb == 0), l=(kb == nkb - 1):
                           e.matmul(PSF[2][:, 0:512],
                                    VT[:, kb * 256 + h * 128: kb * 256 + h * 128 + 128],
                                    pt, start=f, stop=l))
                    pr.add("tensor", lambda e, pt=pt, f=(kb == 0), l=(kb == nkb - 1):
                           e.matmul(PSF[3][0:1, 0:512], ONESCB[:, 0:1], pt, start=f, stop=l))
                pr.add("vector", lambda e: e.reciprocal(ROWS[0:1, 128:640], PSF[3][0:1, 0:512]))
                pr.add("vector", lambda e: e.tensor_copy(ROWB[0:1, 128:640], ROWS[0:1, 128:640]))
                pr.add("tensor", lambda e: e.matmul(PSF[4][:, 0:512], ONESB[0:1, :],
                                                    ROWB[0:1, 128:640], start=True, stop=True))
                pr.add("vector", lambda e: e.tensor_copy(OEV[:, 0:512], PSF[4][:, 0:512]))
                pr.add("vector", lambda e, OST=OST, qb=qb: e.tensor_tensor(
                    OST[:, qb * 512:(qb + 1) * 512], PSF[2][:, 0:512], OEV[:, 0:512], ALU.mult))
        if debug:
            pr.dma("gpsimd", lambda e: e.dma_start(out=dbg["dbg_att"][0:128, :], in_=OT2[:, 0:8192]))
            pr.dma("gpsimd", lambda e: e.dma_start(out=dbg["dbg_att"][128:256, :], in_=OT1[:, 8192:16384]))

        WOB = WQ3[:, 12288:16384]
        RSE = SMF[:, 0:512]
        for osl in range(4):
            pr.dma("gpsimd", lambda e, osl=osl: [e.dma_start(
                out=WOB[:, hc * 2048 + 0:hc * 2048 + 512],
                in_=w_oTs[hc * 128:(hc + 1) * 128, osl * 512:(osl + 1) * 512])
                for hc in range(2)], n=2)
            for tt in range(64):
                def mm_wo(e, osl=osl, tt=tt):
                    ins = []
                    for hc in range(2):
                        lhs = (OT2[:, 0:8192] if hc == 0 else OT1[:, 8192:16384])
                        ins.append(e.matmul(
                            PSF[4 + tt % 2][:, 0:512],
                            lhs[:, tt * 128:(tt + 1) * 128],
                            WOB[:, hc * 2048:hc * 2048 + 512],
                            start=(hc == 0), stop=(hc == 1)))
                    return ins
                pr.add("tensor", mm_wo, n=2)
                pr.add("vector", lambda e, tt=tt: e.tensor_copy(RSE[:, 0:512], PSF[4 + tt % 2][:, 0:512]))
                pr.dma("sync", lambda e, osl=osl, tt=tt: e.dma_start(
                    out=rs_in[tt * 128:(tt + 1) * 128, osl * 512:(osl + 1) * 512],
                    in_=RSE[:, 0:512]))
        pr.async_op("gpsimd", lambda e: e.collective_compute(
            "ReduceScatter", ALU.add, replica_groups=[list(range(NCORE))],
            ins=[rs_in.opt()], outs=[rs_out.opt()]), cc4)

        PREF = A3
        pr.dma("sync", lambda e: e.dma_start(
            out=ROWS[0:1, 0:512], in_=g_T.rearrange("(o c) h -> o (c h)", o=1)),
            pre=[(cc2, 1), (cc3, 1)])
        pr.add("vector", lambda e: e.memset(PREF[:, 0:4096], 0.0))
        for k in range(NCORE - 1):
            pr.dma("sync", lambda e, k=k: e.dma_start(
                out=PREF[:, 4096:8192], in_=g_st[k].rearrange("(n hp) -> n hp", n=128)))
            pr.add("vector", lambda e, k=k: e.tensor_scalar(
                ROWS[0:1, 640:704], ROWS[0:1, k * 64:(k + 1) * 64],
                GATES[0:1, k:k + 1], None, ALU.mult))
            pr.add("scalar", lambda e: e.activation(ROWS[0:1, 704:768], ROWS[0:1, 640:704], AF.Exp))
            pr.add("vector", lambda e: e.tensor_copy(ROWB[0:1, 640:704], ROWS[0:1, 704:768]))
            for jj in range(8):
                pr.add("tensor", lambda e, jj=jj: e.matmul(
                    PSF[jj % 2][:, 0:512], ONESB[0:1, :],
                    ROWB[0:1, 640 + jj * 8:640 + jj * 8 + 8].rearrange("a (h o) -> a h o", o=1)
                    .broadcast_to((1, 8, 64)), start=True, stop=True))
                pr.add("vector", lambda e, jj=jj: e.tensor_copy(
                    PREF[:, 8192 + jj * 512:8192 + (jj + 1) * 512], PSF[jj % 2][:, 0:512]))
            pr.add("vector", lambda e: e.tensor_tensor(
                PREF[:, 0:4096], PREF[:, 0:4096], PREF[:, 8192:12288], ALU.mult))
            pr.add("vector", lambda e, k=k: e.scalar_tensor_tensor(
                PREF[:, 0:4096], PREF[:, 4096:8192], GREP[:, k:k + 1],
                PREF[:, 0:4096], ALU.mult, ALU.add))

        pr.dma("sync", lambda e: e.dma_start(out=ROWS[0:1, 0:512], in_=d_lam.rearrange("(o c) h -> o (c h)", o=1)))
        pr.add("vector", lambda e: e.memset(ROWS[0:1, 512:576], 0.0))

        BCA = A2
        MTC = A5
        STOT = A2b[:, 16384:20480]
        SINB = A2b[:, 20480:24576]
        NACT = sb("NACT", [128, 64], F32)
        GMTC = sb("GMTC", [128, 128], F32)
        CFMC = sb("CFMC", [128, 128], BF16)
        XTC = A4
        SGD = A4[:, 8192:9216]

        for c in range(NCH):
            pr.dma("sync", lambda e, c=c: e.dma_start(out=SINB[:, :], in_=d_sin[c]))
            pr.add("scalar", lambda e: e.activation(ROWS[0:1, 576:640], ROWS[0:1, 512:576], AF.Exp))
            pr.add("vector", lambda e: e.tensor_copy(ROWB[0:1, 576:640], ROWS[0:1, 576:640]))
            for jj in range(8):
                pr.add("tensor", lambda e, jj=jj: e.matmul(
                    PSF[jj % 2][:, 0:512], ONESB[0:1, :],
                    ROWB[0:1, 576 + jj * 8:576 + jj * 8 + 8].rearrange("a (h o) -> a h o", o=1)
                    .broadcast_to((1, 8, 64)), start=True, stop=True))
                pr.add("vector", lambda e, jj=jj: e.tensor_tensor(
                    STOT[:, jj * 512:(jj + 1) * 512], PSF[jj % 2][:, 0:512],
                    PREF[:, jj * 512:(jj + 1) * 512], ALU.mult))
            pr.add("vector", lambda e: e.tensor_tensor(STOT[:, :], STOT[:, :], SINB[:, :], ALU.add))
            pr.add("vector", lambda e, c=c: e.tensor_add(
                ROWS[0:1, 512:576], ROWS[0:1, 512:576], ROWS[0:1, c * 64:(c + 1) * 64]))
            if debug:
                pr.dma("gpsimd", lambda e, c=c: e.dma_start(out=dbg["dbg_stot"][c], in_=STOT[:, :]))

            pr.dma("sync", lambda e, c=c: e.dma_start(out=NACT[:, :], in_=d_nacT[c]))
            pr.dma("sync", lambda e, c=c: e.dma_start(out=GMTC[:, :], in_=d_gmt[c]))
            pr.dma("sync", lambda e, c=c: e.dma_start(out=CFMC[:, :], in_=d_Cfm[c]))
            pr.dma("sync", lambda e, c=c: e.dma_start(out=XTC[:, 0:4096], in_=d_xt[c]))

            for half in range(2):
                pr.dma("gpsimd", lambda e, c=c, half=half: e.dma_start(
                    out=BCA[:, 0:4096],
                    in_=d_acum[c].rearrange("(o h) l -> o (h l)", o=1)[0:1, half * 4096:(half + 1) * 4096]
                    .broadcast_to((128, 4096))))
                for hh in range(32):
                    h = half * 32 + hh
                    pr.add("vector", lambda e, hh=hh, h=h: e.tensor_scalar(
                        BCA[:, 4096 + hh * 128:4096 + (hh + 1) * 128],
                        BCA[:, hh * 128:(hh + 1) * 128],
                        NACT[:, h:h + 1], 0.0, ALU.add, ALU.min))
                pr.add("scalar", lambda e: e.activation(BCA[:, 4096:8192], BCA[:, 4096:8192], AF.Exp))
                pr.add("scalar", lambda e: e.activation(BCA[:, 0:4096], BCA[:, 0:4096], AF.Exp))
                pr.add("vector", lambda e: e.tensor_tensor(
                    MTC[:, 0:4096].rearrange("p (h l) -> p h l", l=128),
                    BCA[:, 4096:8192].rearrange("p (h l) -> p h l", l=128),
                    GMTC[:, :].rearrange("p (o l) -> p o l", o=1).broadcast_to((128, 32, 128)),
                    ALU.mult))
                pr.add("vector", lambda e: e.tensor_tensor(
                    MTC[:, 4096:8192].rearrange("p (h l) -> p h l", l=128),
                    BCA[:, 0:4096].rearrange("p (h l) -> p h l", l=128),
                    CFMC[:, :].rearrange("p (o l) -> p o l", o=1).broadcast_to((128, 32, 128)),
                    ALU.mult))
                for rnd in range(4):
                    for pair in range(4):
                        h0 = half * 32 + rnd * 8 + pair * 2
                        bank = PSF[pair]
                        for i in range(2):
                            h = h0 + i
                            hh = h - half * 32
                            tp = (0, 64) if i == 1 else None
                            oap = bank[64:128, 0:128] if i == 1 else bank[0:64, 0:128]
                            pr.add("tensor", lambda e, h=h, hh=hh, tp=tp, oap=oap: [
                                e.matmul(oap, XTC[:, h * 64:(h + 1) * 64],
                                         MTC[:, hh * 128:(hh + 1) * 128],
                                         start=True, stop=False, tile_position=tp),
                                e.matmul(oap, STOT[:, h * 64:(h + 1) * 64],
                                         MTC[:, 4096 + hh * 128:4096 + (hh + 1) * 128],
                                         start=False, stop=True, tile_position=tp)], n=2)
                    grp = half * 4 + rnd
                    pr.dma("sync", lambda e, grp=grp, c=c: [e.dma_start(
                        out=SGD[:, q * 128:(q + 1) * 128],
                        in_=d_sg[grp * 4 + q, :, c * 128:(c + 1) * 128]) for q in range(4)], n=4)
                    for pair in range(4):
                        fcg = grp * 4 + pair
                        pr.add("vector", lambda e, pair=pair, fcg=fcg: e.tensor_tensor(
                            BCA.ap().bitcast(BF16)[:, 24576 + fcg * 128:24576 + (fcg + 1) * 128]
                            if False else
                            BCA[:, 12288 + fcg * 128:12288 + (fcg + 1) * 128],
                            PSF[pair][:, 0:128], SGD[:, pair * 128:(pair + 1) * 128], ALU.mult))
                    pr.dma("sync", lambda e, grp=grp, c=c: [e.dma_start(
                        out=SGD[:, 512 + q * 128:512 + (q + 1) * 128],
                        in_=d_dres[grp * 4 + q, :, c * 128:(c + 1) * 128]) for q in range(4)], n=4)
                    for pair in range(4):
                        fcg = grp * 4 + pair
                        pr.add("vector", lambda e, pair=pair, fcg=fcg: e.tensor_tensor(
                            BCA[:, 12288 + fcg * 128:12288 + (fcg + 1) * 128],
                            BCA[:, 12288 + fcg * 128:12288 + (fcg + 1) * 128],
                            SGD[:, 512 + pair * 128:512 + (pair + 1) * 128], ALU.add))
            if debug:
                pr.add("vector", lambda e: e.tensor_copy(XTC[:, 4096:8192], BCA[:, 12288:16384]))
                pr.dma("gpsimd", lambda e, c=c: e.dma_start(out=dbg["dbg_yg"][c], in_=XTC[:, 4096:8192]))

            pr.add("scalar", lambda e: e.activation(BCA[:, 0:4096], BCA[:, 12288:16384], AF.Square))

            def rms_mm(e):
                return [e.matmul(PSF[4][0:1, 0:128], ONESCF[:, 0:1],
                                 BCA[:, fcg * 128:(fcg + 1) * 128],
                                 start=(fcg == 0), stop=(fcg == 31)) for fcg in range(32)]
            pr.add("tensor", rms_mm, n=32)
            pr.add("vector", lambda e: e.tensor_scalar(
                ROWS[0:1, 640:768], PSF[4][0:1, 0:128], 1.0 / INTER, EPS, ALU.mult, ALU.add))
            pr.add("scalar", lambda e: e.activation(ROWS[0:1, 640:768], ROWS[0:1, 640:768], AF.Sqrt))
            pr.add("vector", lambda e: e.reciprocal(ROWS[0:1, 768:896], ROWS[0:1, 640:768]))
            pr.add("vector", lambda e: e.tensor_copy(ROWB[0:1, 768:896], ROWS[0:1, 768:896]))
            pr.add("tensor", lambda e: e.matmul(PSF[5][:, 0:128], ONESB[0:1, :],
                                                ROWB[0:1, 768:896], start=True, stop=True))
            pr.add("vector", lambda e: e.tensor_copy(SMF[:, 0:128], PSF[5][:, 0:128]))
            pr.add("vector", lambda e: e.tensor_tensor(
                BCA[:, 12288:16384].rearrange("p (g l) -> p g l", l=128),
                BCA[:, 12288:16384].rearrange("p (g l) -> p g l", l=128),
                SMF[:, 0:128].rearrange("p (o l) -> p o l", o=1).broadcast_to((128, 32, 128)),
                ALU.mult))
            pr.add("vector", lambda e: e.tensor_tensor(
                XTC[:, 4096:8192].rearrange("p (g l) -> p g l", l=128),
                BCA[:, 12288:16384].rearrange("p (g l) -> p g l", l=128),
                NWCOL[:, :].rearrange("p (g o) -> p g o", o=1).broadcast_to((128, 32, 128)),
                ALU.mult))
            pr.dma("sync", lambda e, c=c: e.dma_start(out=d_yn[c], in_=XTC[:, 4096:8192]))

        WOPX = A3
        YNL = A2b[:, 0:8192]
        STM = A4[:, 0:512]
        for osl in range(4):
            pr.dma("gpsimd", lambda e, osl=osl: [e.dma_start(
                out=WOPX[:, fcg * 512:(fcg + 1) * 512],
                in_=w_opT[fcg * 128:(fcg + 1) * 128, osl * 512:(osl + 1) * 512])
                for fcg in range(32)], n=32)
            for c in range(NCH):
                pr.dma("gpsimd", lambda e, c=c: e.dma_start(
                    out=YNL[:, (c % 2) * 4096:(c % 2) * 4096 + 4096], in_=d_yn[c]))

                def mm_op(e, c=c):
                    return [e.matmul(PSF[4 + c % 2][:, 0:512],
                                     YNL[:, (c % 2) * 4096 + fcg * 128:(c % 2) * 4096 + (fcg + 1) * 128],
                                     WOPX[:, fcg * 512:(fcg + 1) * 512],
                                     start=(fcg == 0), stop=(fcg == 31)) for fcg in range(32)]
                pr.add("tensor", mm_op, n=32)
                pr.add("vector", lambda e, c=c: e.tensor_copy(STM[:, 0:512], PSF[4 + c % 2][:, 0:512]))
                pr.dma("sync", lambda e, c=c, osl=osl: e.dma_start(
                    out=d_s[c, :, osl * 512:(osl + 1) * 512], in_=STM[:, 0:512]))
        if debug:
            pr.dma("gpsimd", lambda e: [e.dma_start(out=dbg["dbg_s"][c], in_=d_s[c])
                                      for c in range(NT)], n=NT)

        XFNT = A3
        XNT = A1
        XN3F = A5
        G1 = A2b
        WSLB = A4
        XNFR = A2b[:, 0:16384]
        DWN = sb("DWN", [32, 512], BF16)
        O2ALL = A5
        MC = sb("MC", [128, 2], F32)
        GT = A7[:, 0:2048]
        STL = A4[:, 0:2048]

        for t in range(NT):
            xs = A2[:, 0:2048]
            xn = A2[:, 2048:4096]
            rst = A2[:, 4096:6144]
            scr = A2[:, 6144:8192]
            pr.dma("sync", lambda e, t=t, xs=xs: e.dma_start(
                out=xs, in_=x_own[t * 128:(t + 1) * 128, :]))
            pr.dma("sync", lambda e, t=t, rst=rst: e.dma_start(
                out=rst, in_=rs_out[t * 128:(t + 1) * 128, :]),
                pre=[(cc4, 1)] if t == 0 else None)
            pr.add("vector", lambda e, xs=xs: e.reduce_sum(COLS[:, 0:1], xs, axis=AXX))
            pr.add("vector", lambda e: e.tensor_scalar_mul(COLS[:, 1:2], COLS[:, 0:1], -1.0 / D))
            pr.add("scalar", lambda e, xs=xs, scr=scr: e.activation(
                scr, xs, AF.Square, bias=COLS[:, 1:2], accum_out=COLS[:, 2:3]))
            pr.add("vector", lambda e: e.tensor_scalar(COLS[:, 3:4], COLS[:, 2:3], 1.0 / D, EPS, ALU.mult, ALU.add))
            pr.add("scalar", lambda e: e.activation(COLS[:, 4:5], COLS[:, 3:4], AF.Sqrt))
            pr.add("vector", lambda e: e.reciprocal(COLS[:, 5:6], COLS[:, 4:5]))
            pr.add("vector", lambda e, xs=xs, xn=xn: e.tensor_scalar(
                xn, xs, COLS[:, 1:2], COLS[:, 5:6], ALU.add, ALU.mult))
            pr.add("vector", lambda e, t=t, xn=xn: e.tensor_copy(
                XNT[:, t * 2048:(t + 1) * 2048], xn))
            pr.add("vector", lambda e, t=t, xn=xn, rst=rst: e.tensor_tensor(
                XFNT[:, t * 2048:(t + 1) * 2048], xn, rst, ALU.add))
        if debug:
            pr.dma("sync", lambda e: e.dma_start(out=dbg["dbg_rs"], in_=rs_out))

        for tb in range(2):
            for ti in range(4):
                t = tb * 4 + ti
                xf = A2[:, 0:2048]
                o3 = A2[:, 2048:4096]
                scr = A2[:, 6144:8192]
                pr.add("vector", lambda e, t=t, xf=xf: e.tensor_copy(
                    xf, XFNT[:, t * 2048:(t + 1) * 2048]))
                pr.add("vector", lambda e, xf=xf: e.reduce_sum(COLS[:, 0:1], xf, axis=AXX))
                pr.add("vector", lambda e: e.tensor_scalar_mul(COLS[:, 1:2], COLS[:, 0:1], -1.0 / D))
                pr.add("scalar", lambda e, xf=xf, scr=scr: e.activation(
                    scr, xf, AF.Square, bias=COLS[:, 1:2], accum_out=COLS[:, 2:3]))
                pr.add("vector", lambda e: e.tensor_scalar(COLS[:, 3:4], COLS[:, 2:3], 1.0 / D, EPS, ALU.mult, ALU.add))
                pr.add("scalar", lambda e: e.activation(COLS[:, 4:5], COLS[:, 3:4], AF.Sqrt))
                pr.add("vector", lambda e: e.reciprocal(COLS[:, 5:6], COLS[:, 4:5]))
                pr.add("vector", lambda e, xf=xf, o3=o3: e.tensor_scalar(
                    o3, xf, COLS[:, 1:2], COLS[:, 5:6], ALU.add, ALU.mult))
                for fc in range(16):
                    pr.add("tensor", lambda e, fc=fc, o3=o3: e.transpose(
                        PSF[6][:, 0:128], o3[:, fc * 128:(fc + 1) * 128], IDf[:, :]))
                    pr.add("vector", lambda e, fc=fc, ti=ti: e.tensor_scalar(
                        XN3F[:, fc * 512 + ti * 128: fc * 512 + ti * 128 + 128],
                        PSF[6][:, 0:128], L2S[:, fc:fc + 1], L2B[:, fc:fc + 1],
                        ALU.mult, ALU.add))
            pr.dma("sync", lambda e, tb=tb: [e.dma_start(
                out=XNFR[:, ic * 512:(ic + 1) * 512],
                in_=d_xnfm[ic, :, tb * 512:(tb + 1) * 512]) for ic in range(16)], n=16)

            def mm_dn(e):
                return [e.matmul(PSF[6][0:32, 0:512], WDT[:, ic * 32:(ic + 1) * 32],
                                 XNFR[:, ic * 512:(ic + 1) * 512],
                                 start=(ic == 0), stop=(ic == 15)) for ic in range(16)]
            pr.add("tensor", mm_dn, n=16)
            pr.add("vector", lambda e: e.tensor_copy(DWN[:, :], PSF[6][0:32, 0:512]))

            for fg in range(16):
                buf = fg % 2
                Wb = WSLB[:, buf * 8192: buf * 8192 + 8192]
                pr.dma("gpsimd", lambda e, fg=fg, Wb=Wb: [e.dma_start(
                    out=Wb[:, j * 2048:(j + 1) * 2048].rearrange("q (ic o) -> q ic o", o=128),
                    in_=w_f1T.rearrange("(ic q) o -> q ic o", q=128)
                        [:, :, (fg * 4 + j) * 128:(fg * 4 + j + 1) * 128])
                    for j in range(4)], n=4)
                for j in range(4):
                    def mm_f1(e, j=j, Wb=Wb):
                        return [e.matmul(PSF[j][:, 0:512],
                                         Wb[:, j * 2048 + ic * 128: j * 2048 + (ic + 1) * 128],
                                         XN3F[:, ic * 512:(ic + 1) * 512],
                                         start=(ic == 0), stop=(ic == 15)) for ic in range(16)]
                    pr.add("tensor", mm_f1, n=16)
                for j in range(4):
                    ffc = fg * 4 + j
                    pr.add("scalar", lambda e, j=j, ffc=ffc: e.activation(
                        G1[:, ffc * 512:(ffc + 1) * 512], PSF[j][:, 0:512], AF.Gelu_apprx_tanh))

            for osl in range(4):
                for sl8 in range(8):
                    buf2 = sl8 % 2
                    W2 = WSLB[:, buf2 * 8192: buf2 * 8192 + 4096]
                    pr.dma("gpsimd", lambda e, osl=osl, sl8=sl8, W2=W2: [e.dma_start(
                        out=W2[:, q * 512:(q + 1) * 512],
                        in_=w_f2T[(sl8 * 8 + q) * 128:(sl8 * 8 + q + 1) * 128,
                                  osl * 512:(osl + 1) * 512]) for q in range(8)], n=8)
                    for ti in range(4):
                        def mm_f2(e, ti=ti, sl8=sl8, W2=W2):
                            return [e.matmul(PSF[ti][:, 0:512],
                                             G1[:, (sl8 * 8 + q) * 512 + ti * 128:
                                                (sl8 * 8 + q) * 512 + ti * 128 + 128],
                                             W2[:, q * 512:(q + 1) * 512],
                                             start=(sl8 == 0 and q == 0),
                                             stop=(sl8 == 7 and q == 7)) for q in range(8)]
                        pr.add("tensor", mm_f2, n=8)
                for ti in range(4):
                    t = tb * 4 + ti
                    pr.add("vector", lambda e, t=t, ti=ti, osl=osl: e.tensor_tensor(
                        O2ALL[:, ti * 2048 + osl * 512: ti * 2048 + osl * 512 + 512],
                        PSF[ti][:, 0:512],
                        XFNT[:, t * 2048 + osl * 512: t * 2048 + osl * 512 + 512],
                        ALU.add))

            WUT = WSLB[0:32, 2048:4096]
            pr.dma("gpsimd", lambda e, WUT=WUT: e.dma_start(out=WUT, in_=w_uT))

            for ti in range(4):
                t = tb * 4 + ti
                xs = A2[:, 0:2048]
                sg = A2[:, 2048:4096]
                of = A2[:, 4096:6144]
                f1 = A2[:, 6144:6656]
                f2 = A2[:, 6656:7168]
                f3 = A2[:, 7168:7680]
                pr.dma("sync", lambda e, t=t: e.dma_start(
                    out=MC[:, 0:1], in_=d_m.rearrange("a (t o) -> a t o", o=1)[0, t * 128:(t + 1) * 128]))
                pr.dma("sync", lambda e, t=t: e.dma_start(
                    out=MC[:, 1:2], in_=d_m.rearrange("a (t o) -> a t o", o=1)[1, t * 128:(t + 1) * 128]))
                pr.dma("sync", lambda e, t=t: e.dma_start(out=STL[:, :], in_=d_s[t]))
                pr.dma("sync", lambda e, t=t, xs=xs: e.dma_start(
                    out=xs, in_=x_own[t * 128:(t + 1) * 128, :]))
                for osl in range(4):
                    co = osl * 512
                    pr.add("tensor", lambda e, ti=ti, osl=osl: e.matmul(
                        PSF[5][:, 0:512], DWN[:, ti * 128:(ti + 1) * 128],
                        WUT[:, osl * 512:(osl + 1) * 512], start=True, stop=True))
                    pr.add("vector", lambda e, t=t, co=co, f1=f1: e.tensor_tensor(
                        f1, XNT[:, t * 2048 + co: t * 2048 + co + 512],
                        BREP[:, co:co + 512], ALU.add))
                    pr.add("vector", lambda e, ti=ti, co=co, f1=f1, f2=f2: e.tensor_tensor(
                        f2, O2ALL[:, ti * 2048 + co: ti * 2048 + co + 512], f1, ALU.subtract))
                    pr.add("vector", lambda e, f1=f1, f2=f2, f3=f3: e.scalar_tensor_tensor(
                        f3, f2, MC[:, 1:2], f1, ALU.mult, ALU.add))
                    pr.add("vector", lambda e, co=co, f2=f2: e.tensor_tensor(
                        f2, PSF[5][:, 0:512], BREP[:, co:co + 512], ALU.subtract))
                    pr.add("vector", lambda e, f2=f2, f3=f3: e.scalar_tensor_tensor(
                        f3, f2, MC[:, 0:1], f3, ALU.mult, ALU.add))
                    pr.add("scalar", lambda e, co=co, f3=f3: e.activation(
                        GT[:, co:co + 512], f3, AF.Silu))
                    if debug:
                        pr.add("vector", lambda e, co=co, f3=f3: e.tensor_copy(
                            A2[:, 8192 + co:8192 + co + 512], f3))
                if debug:
                    pr.dma("sync", lambda e, t=t: e.dma_start(
                        out=dbg["dbg_ffn"][t], in_=A2[:, 8192:10240]))
                pr.add("vector", lambda e, sg=sg: e.tensor_tensor(sg, STL[:, :], GT[:, :], ALU.mult))
                pr.add("vector", lambda e, xs=xs, sg=sg, of=of: e.tensor_tensor(of, sg, xs, ALU.add))
                pr.dma("sync", lambda e, t=t, of=of: e.dma_start(
                    out=out[t * 128:(t + 1) * 128, :], in_=of))

        pr.add("sync", lambda e: e.nop())

        with nc.Block() as block:
            pr.realize(block, sems)

    return nc


def _prep_inputs(inputs):
    f32 = lambda a: np.ascontiguousarray(np.asarray(a, np.float32))
    x = f32(inputs["L0_attn_input_states"]).reshape(S, D)
    wip = f32(inputs["L0_attn_in_proj_w"])
    convw = f32(inputs["L0_attn_conv_w"]).reshape(CONV_DIM, 4)
    convb = f32(inputs["L0_attn_conv_b"])
    alog = f32(inputs["L0_attn_A_log"])
    dtb = f32(inputs["L0_attn_dt_bias"])
    dvec = f32(inputs["L0_attn_D"])
    nw = f32(inputs["L0_attn_norm_w"])
    wop = f32(inputs["L0_attn_out_proj_w"])
    sb_ = f32(inputs["L0_ffn_simple_bias"])
    wd = f32(inputs["L0_ffn_w_down"])
    wu = f32(inputs["L0_ffn_w_up"])
    wr = f32(inputs["L0_ffn_w_router"])
    l1s = f32(inputs["L0_ffn_ln1_scale"]); l1b = f32(inputs["L0_ffn_ln1_bias"])
    wq = f32(inputs["L0_ffn_wq"]); wk = f32(inputs["L0_ffn_wk"])
    wv = f32(inputs["L0_ffn_wv"]); wo = f32(inputs["L0_ffn_wo"])
    l2s = f32(inputs["L0_ffn_ln2_scale"]); l2b = f32(inputs["L0_ffn_ln2_bias"])
    wf1 = f32(inputs["L0_ffn_w_ff1"]); wf2 = f32(inputs["L0_ffn_w_ff2"])

    identf = np.eye(128, dtype=np.float32)
    mask_sle = (np.arange(128)[:, None] <= np.arange(128)[None, :]).astype(np.float32)
    amask = np.zeros((4, 128, 512), np.float32)
    for j in range(4):
        amask[j] = (np.arange(128)[:, None] + 128 * j <= np.arange(512)[None, :])
    onesf = np.ones((1, 128), np.float32)

    shared = dict(
        w_ipT=np.ascontiguousarray(wip.T),
        conv_w=convw, conv_b=convb, a_log=alog, dt_bias=dtb, d_vec=dvec,
        norm_w=nw, w_opT=np.ascontiguousarray(wop.T),
        sbias=sb_, w_dT=np.ascontiguousarray(wd.T), w_uT=np.ascontiguousarray(wu.T),
        w_rT=np.ascontiguousarray(wr.T),
        ln1_s=l1s, ln1_b=l1b, ln2_s=l2s, ln2_b=l2b,
        w_f1T=np.ascontiguousarray(wf1.T), w_f2T=np.ascontiguousarray(wf2.T),
        identf=identf, identb_src=identf, mask_sle=mask_sle, amask=amask,
        onesf=onesf,
    )
    woT = np.ascontiguousarray(wo.T)
    in_maps = []
    for c in range(NCORE):
        m = dict(shared)
        m["x_own"] = np.ascontiguousarray(x[c * TOK:(c + 1) * TOK])
        if c == 0:
            m["x_halo"] = np.zeros((3, D), np.float32)
        else:
            m["x_halo"] = np.ascontiguousarray(x[c * TOK - 3:c * TOK])
        hsl = slice(2 * c * DH, (2 * c + 2) * DH)
        m["w_qT"] = np.ascontiguousarray(wq[hsl].T)
        m["w_kT"] = np.ascontiguousarray(wk[hsl].T)
        m["w_vT"] = np.ascontiguousarray(wv[hsl].T)
        m["w_oTs"] = np.ascontiguousarray(woT[hsl])
        g = np.zeros((1, NCORE), np.float32)
        g[0, :c] = 1.0
        m["gates"] = g
        in_maps.append(m)
    return in_maps


def kernel(**inputs):
    if "nc" not in _CACHE:
        _CACHE["nc"] = build_program(debug=False)
    nc = _CACHE["nc"]
    in_maps = _prep_inputs(inputs)
    res = run_bass_kernel_spmd(nc, in_maps, core_ids=list(range(NCORE)))
    outp = np.concatenate([np.asarray(r["out"], np.float32) for r in res.results], axis=0)
    return outp.reshape(1, S, D)


# version 18
# speedup vs baseline: 1.9514x; 1.6974x over previous
import numpy as np
import ml_dtypes
import concourse.bass as bass
import concourse.mybir as mybir
from concourse.bass_utils import run_bass_kernel_spmd

F32 = mybir.dt.float32
BF16 = mybir.dt.bfloat16
AF = mybir.ActivationFunctionType
ALU = mybir.AluOpType
AXX = mybir.AxisListType.X

S, D = 8192, 2048
NCORE = 8
TOK = S // NCORE
NT = TOK // 128
CH = 128
NCH = TOK // CH
H, P, NS = 64, 64, 128
INTER, CONV_DIM, PROJ = 4096, 4352, 8512
NIC = 16
NOC = 67
HA, DH = 16, 128
FF = 8192
RNK = 32
EPS = 1e-5
ASHIFT = 40.0
ASCALE = float(1.0 / np.sqrt(DH))

_CACHE = {}


class Prog:

    LANES = ("sync", "gpsimd", "tensor", "vector", "scalar")
    DMA_LANES = ("sync", "gpsimd")

    def __init__(self):
        self.items = {e: [] for e in self.LANES}
        self.counts = {e: 0 for e in self.LANES}
        self.chain_head = None
        self.strict = False

    def mark(self):
        return dict(self.counts)

    def add(self, eng, fn, n=1, unit=1, pre=None, waits=None, sem=None, rejoin=False):
        if self.strict and waits is not None:
            waits = dict(self.mark(), **{})
            rejoin = sem is None
        chained = (waits is None and sem is None) or rejoin
        if sem is not None:
            w = dict(waits) if waits else {}
            w.pop(eng, None)
            self.items[eng].append((w, fn, n, unit, pre, sem))
            return
        SELFW = ("sync", "gpsimd", "vector", "scalar")
        if waits is None:
            w = {}
            if self.chain_head is not None:
                cl, cv = self.chain_head
                if cl != eng:
                    w[cl] = cv
                elif eng in SELFW:
                    w[eng] = self.counts[eng]
            if eng in SELFW and eng not in w:
                w[eng] = self.counts[eng]
        else:
            w = dict(waits)
            w.pop(eng, None)
            if eng in SELFW:
                w[eng] = self.counts[eng]
        self.items[eng].append((w, fn, n, unit, pre, None))
        self.counts[eng] += n * unit
        if chained:
            self.chain_head = (eng, self.counts[eng])

    def dma(self, eng, fn, n=1, pre=None, waits=None, sem=None):
        self.add(eng, fn, n=n, unit=16, pre=pre, waits=waits, sem=sem)

    def store(self, eng, fn, n=1):
        self.add(eng, fn, n=n, unit=16, waits=self.mark())

    def barrier(self):
        self.add("vector", lambda e: e.engine_nop(), waits=self.mark(), rejoin=True)

    def async_op(self, eng, fn, done_sem, waits=None):
        w = dict(waits) if waits is not None else self.mark()
        w.pop(eng, None)
        self.items[eng].append((w, fn, 0, done_sem, None, None))

    def realize(self, block, sems):
        def mk(name):
            items = self.items[name]

            def body(e):
                reached = {l: 0 for l in self.LANES}
                for w, fn, n, unit, pre, sem in items:
                    if pre is not None:
                        for psem, pval in pre:
                            e.wait_ge(psem, pval)
                    for lane, v in w.items():
                        if v > reached.get(lane, 0):
                            e.wait_ge(sems[lane], v)
                            reached[lane] = v
                    r = fn(e)
                    insts = r if isinstance(r, (list, tuple)) else [r]
                    if n == 0:
                        insts[-1].then_inc(unit, 1)
                    elif sem is not None:
                        assert len(insts) == n
                        for i in insts:
                            i.then_inc(sem, unit)
                    else:
                        assert len(insts) == n, f"{name}: got {len(insts)} want {n}"
                        for i in insts:
                            i.then_inc(sems[name], unit)
            return body
        block.sync(mk("sync"))
        block.gpsimd(mk("gpsimd"))
        block.tensor(mk("tensor"))
        block.vector(mk("vector"))
        block.scalar(mk("scalar"))


class PF:

    def __init__(self, nc, name):
        self.sems = [nc.alloc_semaphore(name + "_0"), nc.alloc_semaphore(name + "_1")]
        self.counts = [0, 0]

    def load(self, pr, eng, fn, n, slot, waits=None):
        p = slot % 2
        pr.dma(eng, fn, n=n, waits=waits if waits is not None else {}, sem=self.sems[p])
        self.counts[p] += n
        return (self.sems[p], 16 * self.counts[p])


def build_program(debug=False, relax=("p0", "ip", "dres", "qkv", "att", "wo", "p4", "op", "p5a", "p5b", "p5d")):
    nc = bass.Bass("TRN2", target_bir_lowering=False)
    pr = Prog()

    def din(name, shape, dt=F32):
        return nc.dram_tensor(name, shape, dt, kind="ExternalInput").ap()

    def dint(name, shape, dt=F32):
        return nc.dram_tensor(name, shape, dt).ap()

    x_own = din("x_own", [TOK, D])
    x_halo = din("x_halo", [3, D])
    w_ipT = din("w_ipT", [D, PROJ], BF16)
    conv_w = din("conv_w", [CONV_DIM, 4])
    conv_b = din("conv_b", [CONV_DIM])
    a_log = din("a_log", [H])
    dt_bias = din("dt_bias", [H])
    d_vec = din("d_vec", [H])
    norm_w = din("norm_w", [INTER])
    w_opT = din("w_opT", [INTER, D], BF16)
    sbias = din("sbias", [D])
    w_dT = din("w_dT", [D, RNK], BF16)
    w_uT = din("w_uT", [RNK, D], BF16)
    w_rT = din("w_rT", [D, 3])
    ln1_s = din("ln1_s", [D])
    ln1_b = din("ln1_b", [D])
    w_qT = din("w_qT", [D, 2 * DH], BF16)
    w_kT = din("w_kT", [D, 2 * DH], BF16)
    w_vT = din("w_vT", [D, 2 * DH], BF16)
    w_oTs = din("w_oTs", [2 * DH, D], BF16)
    ln2_s = din("ln2_s", [D])
    ln2_b = din("ln2_b", [D])
    w_f1T = din("w_f1T", [D, FF], BF16)
    w_f2T = din("w_f2T", [FF, D], BF16)
    identf = din("identf", [128, 128])
    identb_src = din("identb_src", [128, 128])
    mask_sle = din("mask_sle", [128, 128])
    amask = din("amask", [4, 128, 512])
    gates = din("gates", [1, NCORE])
    onesf = din("onesf", [1, 128])
    out = nc.dram_tensor("out", [TOK, D], F32, kind="ExternalOutput").ap()

    g_xn2_in = dint("g_xn2_in", [D, TOK], BF16)
    g_xn2 = dint("g_xn2", [NCORE * D, TOK], BF16)
    g_st_in = dint("g_st_in", [1, NS * H * P], BF16)
    g_st = dint("g_st", [NCORE, NS * H * P], BF16)
    g_T_in = dint("g_T_in", [1, H])
    g_T = dint("g_T", [NCORE, H])
    rs_in = dint("rs_in", [S, D], BF16)
    rs_out = dint("rs_out", [TOK, D], BF16)

    d_xnfm = dint("d_xnfm", [NIC, 128, TOK], BF16)
    d_m = dint("d_m", [2, TOK])
    d_xt = dint("d_xt", [NCH, 128, H * P], BF16)
    d_sg = dint("d_sg", [32, 128, TOK], BF16)
    d_dres = dint("d_dres", [32, 128, TOK], BF16)
    d_Cfm = dint("d_Cfm", [NCH, 128, CH], BF16)
    d_gmt = dint("d_gmt", [NCH, 128, CH])
    d_acum = dint("d_acum", [NCH, H, CH])
    d_nacT = dint("d_nacT", [NCH, CH, H])
    d_sin = dint("d_sin", [NCH, 128, H * P], BF16)
    d_lam = dint("d_lam", [NCH, H])
    d_yn = dint("d_yn", [NCH, 128, INTER], BF16)
    d_s = dint("d_s", [NT, 128, D], BF16)

    dbg = {}
    if debug:
        def dbgout(name, shape, dt=F32):
            dbg[name] = nc.dram_tensor(name, shape, dt, kind="ExternalOutput").ap()
        dbgout("dbg_xn2", [D, TOK])
        dbgout("dbg_csl", [34, 128, 512])
        dbgout("dbg_dt", [64, 1024])
        dbgout("dbg_sin", [NCH, 128, H * P])
        dbgout("dbg_stot", [NCH, 128, H * P])
        dbgout("dbg_yg", [NCH, 128, H * P])
        dbgout("dbg_s", [NT, 128, D])
        dbgout("dbg_att", [2 * DH, S])
        dbgout("dbg_rs", [TOK, D])
        dbgout("dbg_ffn", [NT, 128, D])

    import contextlib
    stack = contextlib.ExitStack()

    def sb(name, shape, dt=F32):
        return stack.enter_context(nc.sbuf_tensor(name, shape, dt))

    def psx(name, shape, dt=F32):
        return stack.enter_context(nc.psum_tensor(name, shape, dt))

    with stack:
        sems = {l: nc.alloc_semaphore("lane_" + l) for l in Prog.LANES}
        cc1 = nc.alloc_semaphore("cc1")
        cc2 = nc.alloc_semaphore("cc2")
        cc3 = nc.alloc_semaphore("cc3")
        cc4 = nc.alloc_semaphore("cc4")

        IDf = sb("IDf", [128, 128], F32)
        IDb = sb("IDb", [128, 128], BF16)
        MSL = sb("MSL", [128, 128], F32)
        AMK = sb("AMK", [128, 4 * 512], BF16)
        ONESB = sb("ONESB", [1, 128], BF16)
        ONESCB = sb("ONESCB", [128, 1], BF16)
        ONESCF = sb("ONESCF", [128, 1], F32)
        ONES64 = sb("ONES64", [64, 128], F32)
        CONVW = sb("CONVW", [128, 34 * 4], F32)
        CONVB = sb("CONVB", [128, 34], F32)
        DCOL = sb("DCOL", [128, 32], F32)
        NWCOL = sb("NWCOL", [128, 32], F32)
        ACOL = sb("ACOL", [64, 1], F32)
        DTBC = sb("DTBC", [64, 1], F32)
        L1S = sb("L1S", [128, 16], F32)
        L1B = sb("L1B", [128, 16], F32)
        L2S = sb("L2S", [128, 16], F32)
        L2B = sb("L2B", [128, 16], F32)
        WRT = sb("WRT", [128, 48], F32)
        GATES = sb("GATES", [1, NCORE], F32)
        GREP = sb("GREP", [128, NCORE], F32)
        WDT = sb("WDT", [128, 16 * RNK], BF16)
        BREP = sb("BREP", [128, D], BF16)
        ROWS = sb("ROWS", [1, 1024], F32)
        ROWB = sb("ROWB", [1, 1024], BF16)
        COLS = sb("COLS", [128, 32], F32)
        SHFT = sb("SHFT", [128, 1], F32)
        SMF = sb("SMF", [128, 640], F32)

        PS01 = psx("PS01", [128, 1024], F32)
        PSF = [PS01[:, 0:512], PS01[:, 512:1024]] + [psx(f"PS{i}", [128, 512], F32) for i in range(2, 7)]
        PS_HB = psx("PS_HB", [128, 1024], BF16)

        A1 = sb("A1", [128, 17408], BF16)
        A2 = sb("A2", [128, 16384], F32)
        A3 = sb("A3", [128, 16384], BF16)
        A4 = sb("A4", [128, 16384], BF16)
        A5 = sb("A5", [128, 8192], BF16)
        A7 = sb("A7", [128, 2048], BF16)
        A2b = A2.ap().bitcast(BF16)

        pr.dma("sync", lambda e: e.dma_start(out=IDf[:, :], in_=identf))
        pr.dma("gpsimd", lambda e: e.dma_start(out=IDb[:, :], in_=identb_src))
        pr.dma("sync", lambda e: e.dma_start(out=MSL[:, :], in_=mask_sle))
        pr.dma("gpsimd", lambda e: e.dma_start(
            out=AMK[:, :].rearrange("p (j t) -> p j t", t=512),
            in_=amask.rearrange("j p t -> p j t")))
        pr.dma("gpsimd", lambda e: e.dma_start(out=ONESB[:, :], in_=onesf))
        pr.dma("gpsimd", lambda e: e.dma_start(
            out=ONESCB[:, :], in_=onesf.rearrange("a (p o) -> (a p) o", o=1)))
        pr.dma("sync", lambda e: e.dma_start(
            out=ONESCF[:, :], in_=onesf.rearrange("a (p o) -> (a p) o", o=1)))
        pr.add("vector", lambda e: e.memset(ONES64[:, :], 1.0))
        pr.add("vector", lambda e: e.memset(SHFT[:, :], -ASHIFT))
        pr.dma("sync", lambda e: e.dma_start(
            out=CONVW[:, :].rearrange("p (fc k) -> p fc k", k=4),
            in_=conv_w.rearrange("(fc p) k -> p fc k", p=128)))
        pr.dma("sync", lambda e: e.dma_start(
            out=CONVB[:, :], in_=conv_b.rearrange("(fc p) -> p fc", p=128), allow_slow_non_contiguous=True))
        pr.dma("sync", lambda e: e.dma_start(
            out=DCOL[0:64, :],
            in_=d_vec.rearrange("(fc h2) -> fc h2", h2=2)[:, 0:1]
                     .rearrange("fc a -> a fc").broadcast_to((64, 32)), allow_slow_non_contiguous=True))
        pr.dma("sync", lambda e: e.dma_start(
            out=DCOL[64:128, :],
            in_=d_vec.rearrange("(fc h2) -> fc h2", h2=2)[:, 1:2]
                     .rearrange("fc a -> a fc").broadcast_to((64, 32)), allow_slow_non_contiguous=True))
        pr.dma("sync", lambda e: e.dma_start(
            out=NWCOL[:, :], in_=norm_w.rearrange("(fc q) -> q fc", q=128), allow_slow_non_contiguous=True))
        pr.dma("sync", lambda e: e.dma_start(out=ACOL[:, :], in_=a_log.rearrange("(h o) -> h o", o=1)))
        pr.dma("sync", lambda e: e.dma_start(out=DTBC[:, :], in_=dt_bias.rearrange("(h o) -> h o", o=1)))
        for t, src in ((L1S, ln1_s), (L1B, ln1_b), (L2S, ln2_s), (L2B, ln2_b)):
            pr.dma("sync", lambda e, t=t, src=src: e.dma_start(
                out=t[:, :], in_=src.rearrange("(fc q) -> q fc", q=128), allow_slow_non_contiguous=True))
        pr.dma("gpsimd", lambda e: e.dma_start(
            out=BREP[:, :], in_=sbias.rearrange("(o d) -> o d", o=1).broadcast_to((128, D))))
        pr.dma("sync", lambda e: e.dma_start(
            out=WRT[:, :].rearrange("q (ic r) -> q ic r", r=3),
            in_=w_rT.rearrange("(ic q) r -> q ic r", q=128)))
        pr.dma("sync", lambda e: e.dma_start(out=GATES[:, :], in_=gates))
        pr.dma("sync", lambda e: e.dma_start(out=GREP[:, :], in_=gates.broadcast_to((128, NCORE))))
        pr.dma("gpsimd", lambda e: e.dma_start(
            out=WDT[:, :].rearrange("q (ic r) -> q ic r", r=RNK),
            in_=w_dT.rearrange("(ic q) r -> q ic r", q=128)))
        pr.add("scalar", lambda e: e.activation(ACOL[:, 0:1], ACOL[:, 0:1], AF.Exp))
        pr.add("vector", lambda e: e.tensor_scalar_mul(ACOL[:, 0:1], ACOL[:, 0:1], -1.0))

        pr.strict = "p0" not in relax
        XNFM = A4
        p0_pf = PF(nc, "p0_pf")
        p0base = pr.mark()
        ln_done = {}
        lnv_done = {}
        tp_done = {}
        rt_done = {}
        xn2_store = {}
        for t in range(NT):
            par = t % 2
            xs = A2[:, par * 8192: par * 8192 + 2048]
            xn = A2[:, par * 8192 + 2048: par * 8192 + 4096]
            o1 = A2[:, par * 8192 + 4096: par * 8192 + 6144]
            xnf32 = A2[:, par * 8192 + 6144: par * 8192 + 8192]
            scr = xnf32
            xn2t = A7[:, par * 1024:(par + 1) * 1024].bitcast(F32) if False else None
            xn2t = A5[:, par * 2048:(par + 1) * 2048]
            CL = COLS[:, par * 16:(par + 1) * 16]
            wld = dict(p0base)
            if t - 2 in tp_done:
                wld["tensor"] = tp_done[t - 2]
                wld["scalar"] = ln_done[t - 2]
                wld["vector"] = lnv_done[t - 2]
            tok = p0_pf.load(pr, "sync", lambda e, t=t, xs=xs: e.dma_start(
                out=xs, in_=x_own[t * 128:(t + 1) * 128, :]), n=1, slot=t, waits=wld)
            wln = dict(p0base)
            if t - 2 in rt_done:
                wln["tensor"] = rt_done[t - 2]
            if t - 2 in xn2_store:
                wln["gpsimd"] = xn2_store[t - 2]
            pr.add("vector", lambda e, xs=xs, CL=CL: e.reduce_sum(CL[:, 0:1], xs, axis=AXX),
                   waits=wln, pre=[tok])
            pr.add("vector", lambda e, CL=CL: e.tensor_scalar_mul(CL[:, 1:2], CL[:, 0:1], -1.0 / D), waits=wln)
            lnv = dict(wln, vector=pr.counts["vector"])
            pr.add("scalar", lambda e, xs=xs, scr=scr, CL=CL: e.activation(
                scr, xs, AF.Square, bias=CL[:, 1:2], accum_out=CL[:, 2:3]), waits=lnv)
            lns = dict(wln, scalar=pr.counts["scalar"])
            pr.add("vector", lambda e, CL=CL: e.tensor_scalar(CL[:, 3:4], CL[:, 2:3], 1.0 / D, EPS, ALU.mult, ALU.add), waits=lns)
            lnv = dict(wln, vector=pr.counts["vector"])
            pr.add("scalar", lambda e, CL=CL: e.activation(CL[:, 4:5], CL[:, 3:4], AF.Sqrt), waits=lnv)
            lns = dict(wln, scalar=pr.counts["scalar"])
            pr.add("vector", lambda e, CL=CL: e.reciprocal(CL[:, 5:6], CL[:, 4:5]), waits=lns)
            pr.add("vector", lambda e, xs=xs, xn=xn, CL=CL: e.tensor_scalar(
                xn, xs, CL[:, 1:2], CL[:, 5:6], ALU.add, ALU.mult), waits=wln)
            pr.add("vector", lambda e, xn=xn, CL=CL: e.reduce_sum(CL[:, 8:9], xn, axis=AXX), waits=wln)
            pr.add("vector", lambda e, CL=CL: e.tensor_scalar_mul(CL[:, 9:10], CL[:, 8:9], -1.0 / D), waits=wln)
            lnv = dict(wln, vector=pr.counts["vector"])
            pr.add("scalar", lambda e, xn=xn, scr=scr, CL=CL: e.activation(
                scr, xn, AF.Square, bias=CL[:, 9:10], accum_out=CL[:, 10:11]), waits=lnv)
            lns = dict(wln, scalar=pr.counts["scalar"])
            pr.add("vector", lambda e, CL=CL: e.tensor_scalar(CL[:, 11:12], CL[:, 10:11], 1.0 / D, EPS, ALU.mult, ALU.add), waits=lns)
            lnv = dict(wln, vector=pr.counts["vector"])
            pr.add("scalar", lambda e, CL=CL: e.activation(CL[:, 12:13], CL[:, 11:12], AF.Sqrt), waits=lnv)
            lns = dict(wln, scalar=pr.counts["scalar"])
            pr.add("vector", lambda e, CL=CL: e.reciprocal(CL[:, 13:14], CL[:, 12:13]), waits=lns)
            pr.add("vector", lambda e, xn=xn, o1=o1, CL=CL: e.tensor_scalar(
                o1, xn, CL[:, 9:10], CL[:, 13:14], ALU.add, ALU.mult), waits=wln)
            ln_done[t] = pr.counts["scalar"]
            lnvd = pr.counts["vector"]
            lnv_done[t] = lnvd

            vrnd = {}
            for rnd in range(8):
                kind = rnd % 2
                fcs = [(rnd // 2) * 4 + q for q in range(4)]
                wt = dict(p0base, vector=lnvd)
                if rnd - 1 in vrnd:
                    wt["vector"] = vrnd[rnd - 1]
                if t - 1 in rt_done and rnd == 0:
                    wt["tensor"] = rt_done[t - 1]
                srcbuf = xn if kind == 0 else o1
                pr.add("tensor", lambda e, fcs=fcs, srcbuf=srcbuf: [e.transpose(
                    PSF[q][:, 0:128], srcbuf[:, fc * 128:(fc + 1) * 128], IDf[:, :])
                    for q, fc in enumerate(fcs)], n=4, waits=wt)
                tm = pr.counts["tensor"]
                if kind == 0:
                    pr.add("vector", lambda e, fcs=fcs, t=t: [e.tensor_copy(
                        XNFM[:, fc * 1024 + t * 128: fc * 1024 + t * 128 + 128], PSF[q][:, 0:128])
                        for q, fc in enumerate(fcs)], n=4, waits=dict(p0base, tensor=tm))
                    pr.add("vector", lambda e, fcs=fcs, xnf32=xnf32: [e.tensor_copy(
                        xnf32[:, fc * 128:(fc + 1) * 128], PSF[q][:, 0:128])
                        for q, fc in enumerate(fcs)], n=4, waits=dict(p0base))
                else:
                    wev1 = dict(p0base, tensor=tm)
                    if t - 2 in xn2_store:
                        wev1["gpsimd"] = xn2_store[t - 2]
                    pr.add("vector", lambda e, fcs=fcs, xn2t=xn2t: [e.tensor_scalar(
                        xn2t[:, fc * 128:(fc + 1) * 128], PSF[q][:, 0:128],
                        L1S[:, fc:fc + 1], L1B[:, fc:fc + 1], ALU.mult, ALU.add)
                        for q, fc in enumerate(fcs)], n=4, waits=wev1)
                vrnd[rnd] = pr.counts["vector"]
            tp_done[t] = pr.counts["tensor"]
            vald = pr.counts["vector"]

            pr.store("sync", lambda e, t=t: [e.dma_start(
                out=d_xnfm[fc, :, t * 128:(t + 1) * 128],
                in_=XNFM[:, fc * 1024 + t * 128: fc * 1024 + t * 128 + 128])
                for fc in range(16)], n=16)
            pr.store("gpsimd", lambda e, t=t, xn2t=xn2t: [e.dma_start(
                out=g_xn2_in.rearrange("(fc q) t -> fc q t", q=128)[fc, :, t * 128:(t + 1) * 128],
                in_=xn2t[:, fc * 128:(fc + 1) * 128]) for fc in range(16)], n=16)
            xn2_store[t] = pr.counts["gpsimd"]

            def mm_rt(e, xnf32=xnf32):
                return [e.matmul(PSF[5][:, 0:3], xnf32[:, ic * 128:(ic + 1) * 128],
                                 WRT[:, ic * 3:(ic + 1) * 3],
                                 start=(ic == 0), stop=(ic == 15)) for ic in range(16)]
            wrt_ = dict(p0base, vector=vald)
            if (t - 1, "v") in rt_done:
                wrt_["vector"] = max(vald, rt_done[(t - 1, "v")])
            pr.add("tensor", mm_rt, n=16, waits=wrt_)
            rtm = pr.counts["tensor"]
            CLr = CL

            pr.add("vector", lambda e, CLr=CLr: e.tensor_copy(CLr[:, 8:11], PSF[5][:, 0:3]),
                   waits=dict(p0base, tensor=rtm))
            pr.add("vector", lambda e, CLr=CLr: [
                e.tensor_tensor(CLr[:, 11:12], CLr[:, 10:11], CLr[:, 8:9], ALU.is_gt),
                e.tensor_tensor(CLr[:, 12:13], CLr[:, 10:11], CLr[:, 9:10], ALU.is_gt),
                e.tensor_tensor(CLr[:, 13:14], CLr[:, 9:10], CLr[:, 8:9], ALU.is_gt)],
                n=3, waits=dict(p0base))
            pr.add("vector", lambda e, CLr=CLr: [
                e.tensor_tensor(CLr[:, 14:15], CLr[:, 11:12], CLr[:, 12:13], ALU.mult),
                e.tensor_scalar(CLr[:, 15:16], CLr[:, 12:13], -1.0, 1.0, ALU.mult, ALU.add)],
                n=2, waits=dict(p0base))
            pr.add("vector", lambda e, CLr=CLr: e.tensor_tensor(
                CLr[:, 11:12], CLr[:, 13:14], CLr[:, 15:16], ALU.mult), waits=dict(p0base))
            rt_done[t] = rtm
            rt_done[(t, "v")] = pr.counts["vector"]
            pr.store("sync", lambda e, t=t, CLr=CLr: [
                e.dma_start(out=d_m.rearrange("a (t o) -> a t o", o=1)[1, t * 128:(t + 1) * 128],
                            in_=CLr[:, 14:15]),
                e.dma_start(out=d_m.rearrange("a (t o) -> a t o", o=1)[0, t * 128:(t + 1) * 128],
                            in_=CLr[:, 11:12])], n=2)
        pr.barrier()
        if debug:
            pr.dma("gpsimd", lambda e: e.dma_start(out=dbg["dbg_xn2"], in_=g_xn2_in))
        pr.async_op("gpsimd", lambda e: e.collective_compute(
            "AllGather", ALU.bypass, replica_groups=[list(range(NCORE))],
            ins=[g_xn2_in.opt()], outs=[g_xn2.opt()]), cc1)

        pr.dma("sync", lambda e: e.dma_start(out=A2[0:3, 0:2048], in_=x_halo))
        pr.add("vector", lambda e: e.reduce_sum(COLS[0:3, 0:1], A2[0:3, 0:2048], axis=AXX))
        pr.add("vector", lambda e: e.tensor_scalar_mul(COLS[0:3, 1:2], COLS[0:3, 0:1], -1.0 / D))
        pr.add("scalar", lambda e: e.activation(A2[0:3, 2048:4096], A2[0:3, 0:2048], AF.Square,
                                                bias=COLS[0:3, 1:2], accum_out=COLS[0:3, 2:3]))
        pr.add("vector", lambda e: e.tensor_scalar(COLS[0:3, 3:4], COLS[0:3, 2:3], 1.0 / D, EPS, ALU.mult, ALU.add))
        pr.add("scalar", lambda e: e.activation(COLS[0:3, 4:5], COLS[0:3, 3:4], AF.Sqrt))
        pr.add("vector", lambda e: e.reciprocal(COLS[0:3, 5:6], COLS[0:3, 4:5]))
        pr.add("vector", lambda e: e.tensor_scalar(A2[0:3, 0:2048], A2[0:3, 0:2048],
                                                   COLS[0:3, 1:2], COLS[0:3, 5:6], ALU.add, ALU.mult))
        XNH = sb("XNH", [128, 48], BF16)
        for ic in range(16):
            pr.add("tensor", lambda e, ic=ic: e.transpose(
                PSF[6][:, 0:3], A2[0:3, ic * 128:(ic + 1) * 128], IDf[0:3, 0:3]))
            pr.add("vector", lambda e, ic=ic: e.tensor_copy(XNH[:, ic * 3:ic * 3 + 3], PSF[6][:, 0:3]))

        CSL = A1
        IPO = A5[:, 0:3072]
        DRS2 = A5[:, 3072:4096]
        HALO = sb("HALO", [128, 102], BF16)
        WSL = A3
        SILG = A2b[:, 24576:25600]
        DTX = A2[0:64, 6144:8192]
        ACM = sb("ACM", [64, 384], F32)
        TSEG = sb("TSEG", [64, 1], F32)
        XTI = A2b[:, 16384:20480]
        X2T = A2b[:, 20480:24576]
        KSM = sb("KSM", [128, 256], BF16)
        RUNA = A2b
        pr.add("vector", lambda e: e.memset(TSEG[:, :], 0.0))
        pr.add("vector", lambda e: e.memset(RUNA[:, 0:4096], 0.0))

        pr.strict = "ip" not in relax
        ip_pf = PF(nc, "ip_pf")
        for tb in range(2):
            pr.strict = "ip" not in relax
            base = pr.mark()
            NOG = 23
            mm_done = {}
            evac_done = {}
            silu_done = {}
            sg_store = {}
            halo_ev = base["vector"]
            pf_tok = {}
            for og in range(NOG):
                ocs = list(range(og * 3, min(og * 3 + 3, NOC)))
                buf = og % 2
                Wb = WSL[:, buf * 6144: buf * 6144 + 6144]

                def ld_w(e, js, ocs=ocs, Wb=Wb):
                    ins = []
                    for j in js:
                        oc = ocs[j]
                        ow = 128 if oc < 66 else 64
                        ins.append(e.dma_start(
                            out=Wb[:, j * 2048: j * 2048 + 16 * ow]
                                .rearrange("q (ic o) -> q ic o", o=ow),
                            in_=w_ipT.rearrange("(ic q) o -> q ic o", q=128)[:, :, oc * 128:oc * 128 + ow]))
                    return ins
                ldw = dict(base)
                if og - 2 in mm_done:
                    ldw["tensor"] = mm_done[og - 2]
                jsg = list(range(0, len(ocs), 2))
                jss = list(range(1, len(ocs), 2))
                pf_tok[og] = ip_pf.load(pr, "gpsimd", lambda e, f=ld_w, js=jsg: f(e, js),
                                        n=len(jsg), slot=og, waits=ldw)
                if jss:
                    pf_tok[og] = ip_pf.load(pr, "sync", lambda e, f=ld_w, js=jss: f(e, js),
                                            n=len(jss), slot=og, waits=ldw)

                for j, oc in enumerate(ocs):
                    ow = 128 if oc < 66 else 64
                    bank = PSF[(og % 2) * 3 + j]
                    wmm = dict(base)
                    if (og - 2, j) in evac_done:
                        wmm["vector"] = evac_done[(og - 2, j)]

                    def mm_ip(e, j=j, ow=ow, bank=bank, Wb=Wb, tb=tb):
                        return [e.matmul(bank[0:ow, 0:512],
                                         Wb[:, j * 2048 + ic * ow: j * 2048 + (ic + 1) * ow],
                                         XNFM[:, ic * 1024 + tb * 512: ic * 1024 + tb * 512 + 512],
                                         start=(ic == 0), stop=(ic == 15)) for ic in range(16)]
                    pr.add("tensor", mm_ip, n=16, waits=wmm, pre=[pf_tok[og]] if j == 0 else None)
                    if tb == 0 and 32 <= oc <= 65:
                        def mm_halo(e, j=j, Wb=Wb):
                            return [e.matmul(PSF[6][0:128, 384 + j * 3:387 + j * 3],
                                             Wb[:, j * 2048 + ic * 128: j * 2048 + (ic + 1) * 128],
                                             XNH[:, ic * 3:(ic + 1) * 3],
                                             start=(ic == 0), stop=(ic == 15)) for ic in range(16)]
                        pr.add("tensor", mm_halo, n=16, waits=dict(base, vector=halo_ev))
                        hm = pr.mark()
                        pr.add("vector", lambda e, j=j, oc=oc: e.tensor_copy(
                            HALO[:, (oc - 32) * 3:(oc - 32) * 3 + 3],
                            PSF[6][0:128, 384 + j * 3:387 + j * 3]), waits=dict(base, tensor=hm["tensor"]))
                        halo_ev = pr.counts["vector"]
                mm_done[og] = pr.counts["tensor"]

                for j, oc in enumerate(ocs):
                    slot = (og % 2) * 3 + j
                    src_ipo = IPO[:, slot * 512:slot * 512 + 512]
                    wev = dict(base, tensor=mm_done[og])
                    if (og - 2, j) in silu_done:
                        wev["scalar"] = silu_done[(og - 2, j)]
                    if oc == 66:
                        pr.add("vector", lambda e, bank=bank: e.tensor_copy(
                            DTX[:, 0:512], bank[0:64, 0:512]), waits=wev)
                    else:
                        pr.add("vector", lambda e, slot=slot, oc=oc, og=og: e.tensor_copy(
                            IPO[:, slot * 512:slot * 512 + 512],
                            PSF[(og % 2) * 3 + (slot - (og % 2) * 3)][0:128, 0:512]), waits=wev)
                    evac_done[(og, j)] = pr.counts["vector"]
                    if oc < 32:
                        wsl_ = dict(base, vector=evac_done[(og, j)])
                        if oc - 2 in sg_store:
                            wsl_["sync"] = sg_store[oc - 2]
                        pr.add("scalar", lambda e, src_ipo=src_ipo, oc=oc: e.activation(
                            SILG[:, (oc % 2) * 512:(oc % 2) * 512 + 512], src_ipo, AF.Silu),
                            waits=wsl_)
                        silu_done[(og, j)] = pr.counts["scalar"]
                        pr.store("sync", lambda e, oc=oc, tb=tb: e.dma_start(
                            out=d_sg[oc, :, tb * 512:(tb + 1) * 512],
                            in_=SILG[:, (oc % 2) * 512:(oc % 2) * 512 + 512]))
                        sg_store[oc] = pr.counts["sync"]
                    elif oc < 66:
                        fc = oc - 32
                        acc = SMF[:, 0:512]
                        wcv = dict(base, vector=evac_done[(og, j)])
                        pr.add("vector", lambda e, src_ipo=src_ipo, fc=fc, acc=acc: e.tensor_scalar(
                            acc, src_ipo, CONVW[:, fc * 4 + 3:fc * 4 + 4], None, ALU.mult), waits=wcv)
                        for k in (2, 1, 0):
                            sh = 3 - k
                            pr.add("vector", lambda e, src_ipo=src_ipo, fc=fc, k=k, sh=sh, acc=acc:
                                   e.scalar_tensor_tensor(
                                       acc[:, sh:512], src_ipo[:, 0:512 - sh],
                                       CONVW[:, fc * 4 + k:fc * 4 + k + 1],
                                       acc[:, sh:512], ALU.mult, ALU.add), waits=wcv)
                            pr.add("vector", lambda e, fc=fc, k=k, sh=sh, acc=acc:
                                   e.scalar_tensor_tensor(
                                       acc[:, 0:sh], HALO[:, fc * 3 + 3 - sh:fc * 3 + 3],
                                       CONVW[:, fc * 4 + k:fc * 4 + k + 1],
                                       acc[:, 0:sh], ALU.mult, ALU.add), waits=wcv)
                        pr.add("vector", lambda e, src_ipo=src_ipo, fc=fc: e.tensor_copy(
                            HALO[:, fc * 3:fc * 3 + 3], src_ipo[:, 509:512]), waits=wcv)
                        cvm = pr.counts["vector"]
                        pr.add("scalar", lambda e, fc=fc, acc=acc: e.activation(
                            CSL[:, fc * 512:fc * 512 + 512], acc, AF.Silu,
                            bias=CONVB[:, fc:fc + 1]), waits=dict(base, vector=cvm))
                        silu_done[(og, j)] = pr.counts["scalar"]
            pr.barrier()

            pr.strict = "dres" not in relax
            drs_pf = getattr(pr, "_drs_pf", None)
            if drs_pf is None:
                drs_pf = PF(nc, "drs_pf")
                pr._drs_pf = drs_pf
            dbase = pr.mark()
            stt_done = {}
            drs_store = {}
            for fc in range(32):
                wld = dict(dbase)
                if fc - 2 in stt_done:
                    wld["vector"] = stt_done[fc - 2]
                tok = drs_pf.load(pr, "sync", lambda e, fc=fc, tb=tb: e.dma_start(
                    out=SILG[:, (fc % 2) * 512:(fc % 2) * 512 + 512],
                    in_=d_sg[fc, :, tb * 512:(tb + 1) * 512]), n=1, slot=fc, waits=wld)
                wst = dict(dbase)
                if fc - 2 in drs_store:
                    wst["sync"] = drs_store[fc - 2]
                pr.add("vector", lambda e, fc=fc: e.scalar_tensor_tensor(
                    DRS2[:, (fc % 2) * 512:(fc % 2) * 512 + 512],
                    CSL[:, fc * 512:fc * 512 + 512],
                    DCOL[:, fc:fc + 1],
                    SILG[:, (fc % 2) * 512:(fc % 2) * 512 + 512],
                    ALU.mult, ALU.mult), waits=wst, pre=[tok])
                stt_done[fc] = pr.counts["vector"]
                pr.store("sync", lambda e, fc=fc, tb=tb: e.dma_start(
                    out=d_dres[fc, :, tb * 512:(tb + 1) * 512],
                    in_=DRS2[:, (fc % 2) * 512:(fc % 2) * 512 + 512]))
                drs_store[fc] = pr.counts["sync"]
            pr.barrier()
            if debug and tb == 0:
                pr.dma("gpsimd", lambda e: [e.dma_start(
                    out=dbg["dbg_csl"][fc], in_=CSL[:, fc * 512:(fc + 1) * 512]) for fc in range(34)], n=34)

            pr.add("vector", lambda e: e.tensor_scalar(DTX[:, 512:1024], DTX[:, 0:512], DTBC[:, 0:1], None, ALU.add))
            pr.add("vector", lambda e: e.tensor_scalar(DTX[:, 1024:1536], DTX[:, 512:1024], -1.0, None, ALU.mult))
            pr.add("vector", lambda e: e.tensor_max(DTX[:, 1024:1536], DTX[:, 1024:1536], DTX[:, 512:1024]))
            pr.add("scalar", lambda e: e.activation(DTX[:, 1024:1536], DTX[:, 1024:1536], AF.Exp, scale=-1.0))
            pr.add("scalar", lambda e: e.activation(DTX[:, 1024:1536], DTX[:, 1024:1536], AF.Ln, bias=1.0))
            pr.add("vector", lambda e: e.tensor_scalar_max(DTX[:, 512:1024], DTX[:, 512:1024], 0.0))
            pr.add("vector", lambda e: e.tensor_add(DTX[:, 512:1024], DTX[:, 512:1024], DTX[:, 1024:1536]))
            pr.add("vector", lambda e: e.tensor_scalar(DTX[:, 1536:2048], DTX[:, 512:1024], ACOL[:, 0:1], None, ALU.mult))
            if debug and tb == 0:
                pr.dma("sync", lambda e: e.dma_start(out=dbg["dbg_dt"][:, 0:512], in_=DTX[:, 512:1024]))
                pr.dma("sync", lambda e: e.dma_start(out=dbg["dbg_dt"][:, 512:1024], in_=DTX[:, 1536:2048]))

            for ci in range(4):
                c = tb * 4 + ci
                cs = slice(ci * 128, ci * 128 + 128)
                pr.add("vector", lambda e, cs=cs: e.tensor_tensor_scan(
                    ACM[:, 0:128], ONES64[:, :], DTX[:, 1536:2048][:, cs], 0.0, ALU.mult, ALU.add))
                pr.add("vector", lambda e: e.tensor_add(TSEG[:, 0:1], TSEG[:, 0:1], ACM[:, 127:128]))
                pr.dma("sync", lambda e, c=c: e.dma_start(
                    out=d_lam.rearrange("c (h o) -> c h o", o=1)[c], in_=ACM[:, 127:128]))
                pr.dma("sync", lambda e, c=c: e.dma_start(out=d_acum[c], in_=ACM[:, 0:128]))
                pr.add("tensor", lambda e: e.transpose(PSF[6][:, 0:64], ACM[:, 0:128], IDf[0:64, 0:64]))
                pr.add("vector", lambda e: e.tensor_scalar_mul(SMF[:, 512:576], PSF[6][:, 0:64], -1.0))
                pr.dma("sync", lambda e, c=c: e.dma_start(out=d_nacT[c], in_=SMF[:, 512:576]))
                pr.add("vector", lambda e: e.tensor_scalar(
                    ACM[:, 128:256], ACM[:, 0:128], -1.0, ACM[:, 127:128], ALU.mult, ALU.add))
                pr.add("scalar", lambda e: e.activation(ACM[:, 128:256], ACM[:, 128:256], AF.Exp))
                pr.add("tensor", lambda e: e.transpose(PSF[6][:, 64:128], ACM[:, 128:256], IDf[0:64, 0:64]))
                pr.add("vector", lambda e: e.tensor_copy(KSM[:, 0:64], PSF[6][:, 64:128]))
                pr.dma("sync", lambda e, c=c: e.dma_start(out=ROWS[0:1, 0:64], in_=d_lam[c:c + 1, :]))
                pr.add("scalar", lambda e: e.activation(ROWS[0:1, 64:128], ROWS[0:1, 0:64], AF.Exp))
                pr.add("vector", lambda e: e.tensor_copy(ROWB[0:1, 0:64], ROWS[0:1, 64:128]))
                pr.add("tensor", lambda e, cs=cs: e.transpose(
                    PSF[6][:, 128:192], DTX[:, 512:1024][:, cs], IDf[0:64, 0:64]))
                pr.add("vector", lambda e: e.tensor_copy(KSM[:, 64:128], PSF[6][:, 128:192]))
                pr.dma("sync", lambda e, c=c, cs=cs: e.dma_start(
                    out=d_Cfm[c], in_=CSL[:, 33 * 512:34 * 512][:, cs]))
                pr.add("tensor", lambda e, cs=cs: e.transpose(
                    PS_HB[:, 0:128], CSL[:, 32 * 512:33 * 512][:, cs], IDb[:, :]))
                pr.add("vector", lambda e: e.tensor_copy(KSM[:, 128:256], PS_HB[:, 0:128]))
                pr.add("tensor", lambda e, cs=cs: e.matmul(
                    PSF[6][:, 192:320], CSL[:, 32 * 512:33 * 512][:, cs],
                    CSL[:, 33 * 512:34 * 512][:, cs], start=True, stop=True))
                pr.add("vector", lambda e: e.tensor_tensor(SMF[:, 0:128], PSF[6][:, 192:320], MSL[:, :], ALU.mult))
                pr.dma("sync", lambda e, c=c: e.dma_start(out=d_gmt[c], in_=SMF[:, 0:128]))
                PSB = PS01.bitcast(BF16)
                xbase = pr.mark()
                xv = {}
                for fc in range(32):
                    slot = PSB[:, (fc % 2) * 1024:(fc % 2) * 1024 + 128]
                    wt = dict(xbase)
                    if fc - 2 in xv:
                        wt["vector"] = xv[fc - 2]
                    pr.add("tensor", lambda e, fc=fc, cs=cs, slot=slot: e.transpose(
                        slot, CSL[:, fc * 512:(fc + 1) * 512][:, cs], IDb[:, :]), waits=wt)
                    tmx = pr.counts["tensor"]
                    pr.add("vector", lambda e, fc=fc, slot=slot: e.tensor_tensor(
                        XTI[:, fc * 128:(fc + 1) * 128].rearrange("p (h w) -> p h w", h=2),
                        slot.rearrange("p (h w) -> p h w", h=2),
                        KSM[:, 64 + 2 * fc:64 + 2 * fc + 2].rearrange("p (h o) -> p h o", o=1)
                        .broadcast_to((128, 2, 64)), ALU.mult), waits=dict(xbase, tensor=tmx))
                    xv[fc] = pr.counts["vector"]
                pr.barrier()
                pr.store("sync", lambda e, c=c: e.dma_start(out=d_xt[c], in_=XTI[:, :]))
                pr.add("vector", lambda e: e.tensor_tensor(
                    X2T[:, :].rearrange("p (h w) -> p h w", h=64),
                    XTI[:, :].rearrange("p (h w) -> p h w", h=64),
                    KSM[:, 0:64].rearrange("p (h o) -> p h o", o=1).broadcast_to((128, 64, 64)),
                    ALU.mult))
                sbase = pr.mark()
                sv = {}
                for jj in range(8):
                    bank = PSF[2 + jj % 2]
                    wt = dict(sbase)
                    if jj - 2 in sv:
                        wt["vector"] = sv[jj - 2]
                    pr.add("tensor", lambda e, jj=jj, bank=bank: e.matmul(
                        bank[:, 0:512], KSM[:, 128:256],
                        X2T[:, jj * 512:(jj + 1) * 512], start=True, stop=True), waits=wt)
                    tmx = pr.counts["tensor"]
                    pr.add("vector", lambda e, jj=jj, bank=bank: e.tensor_copy(
                        RUNA[:, 8192 + jj * 512:8192 + (jj + 1) * 512], bank[:, 0:512]),
                        waits=dict(sbase, tensor=tmx))
                    sv[jj] = pr.counts["vector"]
                pr.barrier()
                pr.store("sync", lambda e, c=c: e.dma_start(out=d_sin[c], in_=RUNA[:, 0:4096]))
                for g4 in range(2):
                    def lam_mm1(e, g4=g4):
                        return [e.matmul(
                            PSF[2 + j][:, 0:512], ONESB[0:1, :],
                            ROWB[0:1, (g4 * 4 + j) * 8:(g4 * 4 + j) * 8 + 8]
                            .rearrange("a (h o) -> a h o", o=1).broadcast_to((1, 8, 64)),
                            start=True, stop=True) for j in range(4)]
                    pr.add("tensor", lam_mm1, n=4)

                    def lam_ev1(e, g4=g4):
                        return [e.tensor_copy(
                            RUNA[:, 4096 + (g4 * 4 + j) * 512:4096 + (g4 * 4 + j + 1) * 512],
                            PSF[2 + j][:, 0:512]) for j in range(4)]
                    pr.add("vector", lam_ev1, n=4)
                pr.add("vector", lambda e: e.tensor_tensor(
                    RUNA[:, 0:4096], RUNA[:, 0:4096], RUNA[:, 4096:8192], ALU.mult))
                pr.add("vector", lambda e: e.tensor_tensor(
                    RUNA[:, 0:4096], RUNA[:, 0:4096], RUNA[:, 8192:12288], ALU.add))

        pr.dma("sync", lambda e: e.dma_start(
            out=g_st_in.rearrange("a (n hp) -> (a n) hp", n=128), in_=RUNA[:, 0:4096]))
        pr.dma("sync", lambda e: e.dma_start(
            out=g_T_in.rearrange("a (h o) -> (a h) o", o=1), in_=TSEG[:, 0:1]))
        pr.async_op("gpsimd", lambda e: e.collective_compute(
            "AllGather", ALU.bypass, replica_groups=[list(range(NCORE))],
            ins=[g_st_in.opt()], outs=[g_st.opt()]), cc2)
        pr.async_op("gpsimd", lambda e: e.collective_compute(
            "AllGather", ALU.bypass, replica_groups=[list(range(NCORE))],
            ins=[g_T_in.opt()], outs=[g_T.opt()]), cc3)
        if debug:
            pr.dma("gpsimd", lambda e: [e.dma_start(out=dbg["dbg_sin"][c], in_=d_sin[c])
                                      for c in range(NCH)], n=NCH)

        QKH = A2b
        VT = A3
        WQ3 = A4
        X2S = A1
        OT2 = A5
        OT1 = A1
        PT = A7[:, 0:1024]
        OEV = A7[:, 1024:2048]
        pr.dma("gpsimd", lambda e: [e.dma_start(
            out=WQ3[:, i * 4096:(i + 1) * 4096].rearrange("q (ic o) -> q ic o", o=256),
            in_=w.rearrange("(ic q) o -> q ic o", q=128))
            for i, w in enumerate((w_qT, w_kT, w_vT))], n=3, pre=[(cc1, 1)])

        pr.strict = "qkv" not in relax
        x2s_pf = PF(nc, "x2s_pf")
        qbase = pr.mark()
        qk_evac = {}
        v_evac = {}
        x2s_toks = {}
        for qb in range(16):
            src_core = qb // 2
            tloc = (qb % 2) * 512
            xsl = (qb % 2) * 8192
            wldx = dict(qbase)
            if qb - 2 in qk_evac:
                wldx["vector"] = v_evac[qb - 2]
            def ld_x2s(e, ics, src_core=src_core, tloc=tloc, xsl=xsl):
                return [e.dma_start(
                    out=X2S[:, xsl + ic * 512: xsl + (ic + 1) * 512],
                    in_=g_xn2[src_core * D + ic * 128: src_core * D + (ic + 1) * 128, tloc:tloc + 512])
                    for ic in ics]
            x2s_pf.load(pr, "sync", lambda e, f=ld_x2s: f(e, range(0, 16, 2)),
                        n=8, slot=qb, waits=wldx)
            x2s_toks[qb] = x2s_pf.load(pr, "gpsimd", lambda e, f=ld_x2s: f(e, range(1, 16, 2)),
                                       n=8, slot=qb, waits=wldx)
            for wi in range(2):
                for hc in range(2):
                    bk = (qb % 2) * 2 + (wi * 2 + hc) % 2
                    bk = [0, 1, 2, 3][(wi * 2 + hc)] if qb % 2 == 0 else [0, 1, 2, 3][(wi * 2 + hc)]
                    bank = PSF[wi * 2 + hc]
                    wmm = dict(qbase)
                    if qb - 1 in qk_evac:
                        wmm["vector"] = qk_evac[qb - 1]

                    def mm_qk(e, wi=wi, hc=hc, xsl=xsl, bank=bank):
                        return [e.matmul(bank[:, 0:512],
                                         WQ3[:, wi * 4096 + ic * 256 + hc * 128:
                                             wi * 4096 + ic * 256 + hc * 128 + 128],
                                         X2S[:, xsl + ic * 512: xsl + (ic + 1) * 512],
                                         start=(ic == 0), stop=(ic == 15)) for ic in range(16)]
                    pr.add("tensor", mm_qk, n=16, waits=wmm,
                           pre=[x2s_toks[qb]] if (wi == 0 and hc == 0) else None)
                    mmm = pr.counts["tensor"]
                    off = wi * 16384 + hc * 8192 + qb * 512
                    pr.add("vector", lambda e, off=off, bank=bank: e.tensor_copy(
                        QKH[:, off:off + 512], bank[:, 0:512]), waits=dict(qbase, tensor=mmm))
            qk_evac[qb] = pr.counts["vector"]
            vem = {}
            for t4 in range(4):
                bank = PSF[4 + t4 % 2]
                wmm = dict(qbase)
                if t4 - 2 in vem:
                    wmm["vector"] = vem[t4 - 2]
                elif qb - 1 in v_evac:
                    wmm["vector"] = v_evac[qb - 1]

                def mm_v(e, t4=t4, xsl=xsl, bank=bank):
                    return [e.matmul(bank[:, 0:256],
                                     X2S[:, xsl + ic * 512 + t4 * 128: xsl + ic * 512 + t4 * 128 + 128],
                                     WQ3[:, 2 * 4096 + ic * 256: 2 * 4096 + (ic + 1) * 256],
                                     start=(ic == 0), stop=(ic == 15)) for ic in range(16)]
                pr.add("tensor", mm_v, n=16, waits=wmm)
                mmm = pr.counts["tensor"]
                kb = qb * 4 + t4
                pr.add("vector", lambda e, kb=kb, bank=bank: e.tensor_copy(
                    VT[:, kb * 256:kb * 256 + 256], bank[:, 0:256]), waits=dict(qbase, tensor=mmm))
                vem[t4] = pr.counts["vector"]
            v_evac[qb] = pr.counts["vector"]
        pr.barrier()

        pr.strict = "att" not in relax
        OEV2 = A1[:, 16384:17408]
        abase = pr.mark()
        epi_v = [None]
        prev_emk = [None]
        prev_bc_t = [None]
        for h in range(2):
            OST = OT1[:, 8192:16384] if h == 1 else OT2[:, 0:8192]
            for qb in range(16):
                nkb = 4 * qb + 4
                npair = nkb // 2
                smm = {}
                emk = {}
                vmk = {}

                def emit_s(p):
                    w = dict(abase)
                    if p - 1 in emk:
                        w["scalar"] = emk[p - 1]
                    elif prev_emk[0] is not None:
                        w["scalar"] = prev_emk[0]

                    def f(e, h=h, qb=qb, p=p):
                        return [e.matmul(
                            PS01[:, i * 512:(i + 1) * 512],
                            QKH[:, 16384 + h * 8192 + (2 * p + i) * 128:
                                16384 + h * 8192 + (2 * p + i) * 128 + 128],
                            QKH[:, h * 8192 + qb * 512: h * 8192 + qb * 512 + 512],
                            start=True, stop=True) for i in range(2)]
                    pr.add("tensor", f, n=2, waits=w)
                    smm[p] = pr.counts["tensor"]

                def emit_exp(p):
                    pt = A7[:, (p % 2) * 1024:(p % 2) * 1024 + 1024]
                    pr.add("scalar", lambda e, pt=pt: e.activation(
                        pt, PS01[:, 0:1024], AF.Exp, bias=SHFT[:, 0:1], scale=ASCALE),
                        waits=dict(abase, tensor=smm[p]))
                    emk[p] = pr.counts["scalar"]
                    for i in range(2):
                        kb = 2 * p + i
                        j = kb - 4 * qb
                        if j >= 0:
                            pr.add("vector", lambda e, pt=pt, i=i, j=j: e.tensor_tensor(
                                pt[:, i * 512:(i + 1) * 512], pt[:, i * 512:(i + 1) * 512],
                                AMK[:, j * 512:(j + 1) * 512], ALU.mult),
                                waits=dict(abase, scalar=emk[p]))
                            vmk[p] = pr.counts["vector"]

                def emit_av(p):
                    pt = A7[:, (p % 2) * 1024:(p % 2) * 1024 + 1024]
                    w = dict(abase, scalar=emk[p])
                    if p in vmk:
                        w["vector"] = vmk[p]
                    if p == 0 and epi_v[0] is not None:
                        w["vector"] = max(w.get("vector", 0), epi_v[0])

                    def f(e, h=h, p=p, pt=pt, npair=npair):
                        ins = []
                        for i in range(2):
                            kb = 2 * p + i
                            ins.append(e.matmul(
                                PSF[2][:, 0:512],
                                VT[:, kb * 256 + h * 128: kb * 256 + h * 128 + 128],
                                pt[:, i * 512:(i + 1) * 512],
                                start=(kb == 0), stop=(kb == 2 * npair - 1)))
                        for i in range(2):
                            kb = 2 * p + i
                            ins.append(e.matmul(
                                PSF[3][0:1, 0:512], ONESCB[:, 0:1],
                                pt[:, i * 512:(i + 1) * 512],
                                start=(kb == 0), stop=(kb == 2 * npair - 1)))
                        return ins
                    pr.add("tensor", f, n=4, waits=w)

                emit_s(0)
                emit_exp(0)
                if npair > 1:
                    emit_s(1)
                for p in range(npair):
                    emit_av(p)
                    if p + 1 < npair:
                        emit_exp(p + 1)
                    if p + 2 < npair:
                        emit_s(p + 2)
                em_t = pr.counts["tensor"]
                pr.add("vector", lambda e: e.reciprocal(ROWS[0:1, 128:640], PSF[3][0:1, 0:512]),
                       waits=dict(abase, tensor=em_t))
                wrb = dict(abase)
                if prev_bc_t[0] is not None:
                    wrb["tensor"] = prev_bc_t[0]
                pr.add("vector", lambda e: e.tensor_copy(ROWB[0:1, 128:640], ROWS[0:1, 128:640]),
                       waits=wrb)
                rb_v = pr.counts["vector"]
                pr.add("tensor", lambda e: e.matmul(PSF[4][:, 0:512], ONESB[0:1, :],
                                                    ROWB[0:1, 128:640], start=True, stop=True),
                       waits=dict(abase, vector=rb_v))
                bc_t = pr.counts["tensor"]
                prev_bc_t[0] = bc_t
                pr.add("vector", lambda e: e.tensor_copy(OEV2[:, 0:512], PSF[4][:, 0:512]),
                       waits=dict(abase, tensor=bc_t))
                pr.add("vector", lambda e, OST=OST, qb=qb: e.tensor_tensor(
                    OST[:, qb * 512:(qb + 1) * 512], PSF[2][:, 0:512], OEV2[:, 0:512], ALU.mult),
                       waits=dict(abase))
                epi_v[0] = pr.counts["vector"]
                prev_emk[0] = emk[npair - 1]
        pr.barrier()
        if debug:
            pr.dma("gpsimd", lambda e: e.dma_start(out=dbg["dbg_att"][0:128, :], in_=OT2[:, 0:8192]))
            pr.dma("gpsimd", lambda e: e.dma_start(out=dbg["dbg_att"][128:256, :], in_=OT1[:, 8192:16384]))

        pr.strict = "wo" not in relax
        wo_pf = PF(nc, "wo_pf")
        WOB = WQ3[:, 12288:16384]
        A7f = A7.ap().bitcast(F32)
        wbase = pr.mark()
        wo_mm = {}
        wo_ev = {}
        wo_st = {}
        for osl in range(4):
            wld = dict(wbase)
            if (osl - 2, 63) in wo_mm:
                wld["tensor"] = wo_mm[(osl - 2, 63)]
            wsl_off = (osl % 2) * 2048
            wtok = wo_pf.load(pr, "gpsimd", lambda e, osl=osl, wsl_off=wsl_off: [e.dma_start(
                out=WOB[:, wsl_off + hc * 512:wsl_off + hc * 512 + 512],
                in_=w_oTs[hc * 128:(hc + 1) * 128, osl * 512:(osl + 1) * 512])
                for hc in range(2)], n=2, slot=osl, waits=wld)
            for tt in range(64):
                bank = PSF[4 + tt % 2]
                w = dict(wbase)
                if (osl, tt - 2) in wo_ev:
                    w["vector"] = wo_ev[(osl, tt - 2)]
                elif osl > 0 and (osl - 1, 62 + tt) in wo_ev:
                    w["vector"] = wo_ev[(osl - 1, 62 + tt)]

                def mm_wo(e, osl=osl, tt=tt, bank=bank, wsl_off=wsl_off):
                    ins = []
                    for hc in range(2):
                        lhs = (OT2[:, 0:8192] if hc == 0 else OT1[:, 8192:16384])
                        ins.append(e.matmul(
                            bank[:, 0:512], lhs[:, tt * 128:(tt + 1) * 128],
                            WOB[:, wsl_off + hc * 512:wsl_off + hc * 512 + 512],
                            start=(hc == 0), stop=(hc == 1)))
                    return ins
                pr.add("tensor", mm_wo, n=2, waits=w, pre=[wtok] if tt == 0 else None)
                wo_mm[(osl, tt)] = pr.counts["tensor"]
                wev = dict(wbase, tensor=wo_mm[(osl, tt)])
                if (osl, tt - 2) in wo_st:
                    wev["sync"] = wo_st[(osl, tt - 2)]
                elif osl > 0 and (osl - 1, 62 + tt) in wo_st:
                    wev["sync"] = wo_st[(osl - 1, 62 + tt)]
                pr.add("vector", lambda e, tt=tt, bank=bank: e.tensor_copy(
                    A7[:, (tt % 2) * 512:(tt % 2) * 512 + 512], bank[:, 0:512]), waits=wev)
                wo_ev[(osl, tt)] = pr.counts["vector"]
                pr.store("sync", lambda e, osl=osl, tt=tt: e.dma_start(
                    out=rs_in[tt * 128:(tt + 1) * 128, osl * 512:(osl + 1) * 512],
                    in_=A7[:, (tt % 2) * 512:(tt % 2) * 512 + 512]))
                wo_st[(osl, tt)] = pr.counts["sync"]
        pr.barrier()
        pr.async_op("gpsimd", lambda e: e.collective_compute(
            "ReduceScatter", ALU.add, replica_groups=[list(range(NCORE))],
            ins=[rs_in.opt()], outs=[rs_out.opt()]), cc4)

        PREF = A3
        pr.dma("sync", lambda e: e.dma_start(
            out=ROWS[0:1, 0:512], in_=g_T.rearrange("(o c) h -> o (c h)", o=1)),
            pre=[(cc2, 1), (cc3, 1)])
        pr.add("vector", lambda e: e.memset(PREF[:, 0:4096], 0.0))
        for k in range(NCORE - 1):
            pr.dma("sync", lambda e, k=k: e.dma_start(
                out=PREF[:, 4096:8192], in_=g_st[k].rearrange("(n hp) -> n hp", n=128)))
            pr.add("vector", lambda e, k=k: e.tensor_scalar(
                ROWS[0:1, 640:704], ROWS[0:1, k * 64:(k + 1) * 64],
                GATES[0:1, k:k + 1], None, ALU.mult))
            pr.add("scalar", lambda e: e.activation(ROWS[0:1, 704:768], ROWS[0:1, 640:704], AF.Exp))
            pr.add("vector", lambda e: e.tensor_copy(ROWB[0:1, 640:704], ROWS[0:1, 704:768]))
            for jj in range(8):
                pr.add("tensor", lambda e, jj=jj: e.matmul(
                    PSF[jj % 2][:, 0:512], ONESB[0:1, :],
                    ROWB[0:1, 640 + jj * 8:640 + jj * 8 + 8].rearrange("a (h o) -> a h o", o=1)
                    .broadcast_to((1, 8, 64)), start=True, stop=True))
                pr.add("vector", lambda e, jj=jj: e.tensor_copy(
                    PREF[:, 8192 + jj * 512:8192 + (jj + 1) * 512], PSF[jj % 2][:, 0:512]))
            pr.add("vector", lambda e: e.tensor_tensor(
                PREF[:, 0:4096], PREF[:, 0:4096], PREF[:, 8192:12288], ALU.mult))
            pr.add("vector", lambda e, k=k: e.scalar_tensor_tensor(
                PREF[:, 0:4096], PREF[:, 4096:8192], GREP[:, k:k + 1],
                PREF[:, 0:4096], ALU.mult, ALU.add))

        pr.dma("sync", lambda e: e.dma_start(out=ROWS[0:1, 0:512], in_=d_lam.rearrange("(o c) h -> o (c h)", o=1)))
        pr.add("vector", lambda e: e.memset(ROWS[0:1, 512:576], 0.0))

        BCA = A2
        MTC = A5
        STOT = A2b[:, 16384:20480]
        SINB = A2b[:, 20480:24576]
        NACT = sb("NACT", [128, 64], F32)
        GMTC = sb("GMTC", [128, 128], F32)
        CFMC = sb("CFMC", [128, 128], BF16)
        XTC = A4
        SGD = A4[:, 8192:10240]

        pr.strict = "p4" not in relax
        bc_pf = PF(nc, "bc_pf")
        sg_pf = PF(nc, "sg_pf")
        A1f = A1.ap().bitcast(F32)
        DECS = A2[:, 4096:8192]
        p4s = pr.mark()
        ce_mark = [None]
        ymm_last = [None]
        ev_marks = [None]
        rms_t = [None]
        cem = {}
        evm = {}
        bc_tok = {}
        sg_tok = {}

        def bc_load(i, waits):
            c2, h2 = divmod(i, 2)
            par = i % 2
            bc_tok[i] = bc_pf.load(pr, "gpsimd", lambda e, c2=c2, h2=h2, par=par: e.dma_start(
                out=A1f[:, par * 4096:(par + 1) * 4096],
                in_=d_acum[c2].rearrange("(o h) l -> o (h l)", o=1)[0:1, h2 * 4096:(h2 + 1) * 4096]
                .broadcast_to((128, 4096))), n=1, slot=i, waits=waits)

        def sg_load(gi, waits):
            c2, g2 = divmod(gi, 8)
            sl = gi % 2
            sg_tok[gi] = sg_pf.load(pr, "sync", lambda e, c2=c2, g2=g2, sl=sl: [e.dma_start(
                out=SGD[:, sl * 1024 + q * 128: sl * 1024 + (q + 1) * 128],
                in_=d_sg[g2 * 4 + q, :, c2 * 128:(c2 + 1) * 128]) for q in range(4)] + [
                e.dma_start(
                out=SGD[:, sl * 1024 + 512 + q * 128: sl * 1024 + 512 + (q + 1) * 128],
                in_=d_dres[g2 * 4 + q, :, c2 * 128:(c2 + 1) * 128]) for q in range(4)],
                n=8, slot=gi, waits=waits)

        bc_load(0, dict(p4s))
        bc_load(1, dict(p4s))
        sg_load(0, dict(p4s))
        for c in range(NCH):
            pr.dma("sync", lambda e, c=c: e.dma_start(out=SINB[:, :], in_=d_sin[c]))
            pr.add("scalar", lambda e: e.activation(ROWS[0:1, 576:640], ROWS[0:1, 512:576], AF.Exp))
            pr.add("vector", lambda e: e.tensor_copy(ROWB[0:1, 576:640], ROWS[0:1, 576:640]))
            for g4 in range(2):
                def lam_mm(e, g4=g4):
                    return [e.matmul(
                        PSF[j][:, 0:512], ONESB[0:1, :],
                        ROWB[0:1, 576 + (g4 * 4 + j) * 8:576 + (g4 * 4 + j) * 8 + 8]
                        .rearrange("a (h o) -> a h o", o=1).broadcast_to((1, 8, 64)),
                        start=True, stop=True) for j in range(4)]
                pr.add("tensor", lam_mm, n=4)

                def lam_tt(e, g4=g4):
                    return [e.tensor_tensor(
                        STOT[:, (g4 * 4 + j) * 512:(g4 * 4 + j + 1) * 512],
                        PSF[j][:, 0:512],
                        PREF[:, (g4 * 4 + j) * 512:(g4 * 4 + j + 1) * 512], ALU.mult)
                        for j in range(4)]
                pr.add("vector", lam_tt, n=4)
            pr.add("vector", lambda e: e.tensor_tensor(STOT[:, :], STOT[:, :], SINB[:, :], ALU.add))
            pr.add("vector", lambda e, c=c: e.tensor_add(
                ROWS[0:1, 512:576], ROWS[0:1, 512:576], ROWS[0:1, c * 64:(c + 1) * 64]))
            if debug:
                pr.dma("gpsimd", lambda e, c=c: e.dma_start(out=dbg["dbg_stot"][c], in_=STOT[:, :]))

            pr.dma("sync", lambda e, c=c: e.dma_start(out=NACT[:, :], in_=d_nacT[c]))
            pr.dma("sync", lambda e, c=c: e.dma_start(out=GMTC[:, :], in_=d_gmt[c]))
            pr.dma("sync", lambda e, c=c: e.dma_start(out=CFMC[:, :], in_=d_Cfm[c]))
            pr.dma("sync", lambda e, c=c: e.dma_start(out=XTC[:, 0:4096], in_=d_xt[c]))

            hbase = pr.mark()
            for half in range(2):
                i = 2 * c + half
                RAW = A1f[:, (i % 2) * 4096:(i % 2) * 4096 + 4096]
                pr.add("vector", lambda e, RAW=RAW, half=half: e.tensor_tensor(
                    DECS.rearrange("p (h l) -> p h l", l=128),
                    RAW.rearrange("p (h l) -> p h l", l=128),
                    NACT[:, half * 32:half * 32 + 32].rearrange("p (h o) -> p h o", o=1)
                    .broadcast_to((128, 32, 128)), ALU.add),
                    waits=dict(hbase), pre=[bc_tok[i]])
                pr.add("vector", lambda e: e.tensor_scalar(DECS, DECS, 0.0, None, ALU.min))
                ssv = pr.counts["vector"]
                pr.add("scalar", lambda e: e.activation(DECS, DECS, AF.Exp),
                       waits=dict(hbase, vector=ssv))
                pr.add("scalar", lambda e, RAW=RAW: e.activation(RAW, RAW, AF.Exp),
                       waits=dict(hbase, vector=ssv))
                exm = pr.counts["scalar"]
                wmt = dict(hbase, scalar=exm, vector=ssv)
                if ymm_last[0] is not None:
                    wmt["tensor"] = ymm_last[0]
                pr.add("vector", lambda e: e.tensor_tensor(
                    MTC[:, 0:4096].rearrange("p (h l) -> p h l", l=128),
                    DECS.rearrange("p (h l) -> p h l", l=128),
                    GMTC[:, :].rearrange("p (o l) -> p o l", o=1).broadcast_to((128, 32, 128)),
                    ALU.mult), waits=wmt)
                pr.add("vector", lambda e, RAW=RAW: e.tensor_tensor(
                    MTC[:, 4096:8192].rearrange("p (h l) -> p h l", l=128),
                    RAW.rearrange("p (h l) -> p h l", l=128),
                    CFMC[:, :].rearrange("p (o l) -> p o l", o=1).broadcast_to((128, 32, 128)),
                    ALU.mult), waits=wmt)
                cem[i] = pr.counts["vector"]
                ce_mark[0] = cem[i]
                mtce_v = cem[i]
                if i + 2 < 2 * NCH:
                    bc_load(i + 2, dict(p4s, vector=cem[i]))
                for rnd in range(4):
                    grp = half * 4 + rnd
                    gi = c * 8 + grp
                    if gi + 1 < 8 * NCH:
                        wsg = dict(p4s)
                        if gi - 1 in evm:
                            wsg["vector"] = evm[gi - 1]
                        sg_load(gi + 1, wsg)
                    sl = gi % 2
                    for pair in range(4):
                        h0 = half * 32 + rnd * 8 + pair * 2
                        bank = PSF[pair]
                        wy = dict(hbase, vector=mtce_v)
                        if ev_marks[0] is not None:
                            wy["vector"] = max(mtce_v, ev_marks[0])
                        for ii in range(2):
                            h = h0 + ii
                            hh = h - half * 32
                            tp = (0, 64) if ii == 1 else None
                            oap = bank[64:128, 0:128] if ii == 1 else bank[0:64, 0:128]
                            pr.add("tensor", lambda e, h=h, hh=hh, tp=tp, oap=oap: [
                                e.matmul(oap, XTC[:, h * 64:(h + 1) * 64],
                                         MTC[:, hh * 128:(hh + 1) * 128],
                                         start=True, stop=False, tile_position=tp),
                                e.matmul(oap, STOT[:, h * 64:(h + 1) * 64],
                                         MTC[:, 4096 + hh * 128:4096 + (hh + 1) * 128],
                                         start=False, stop=True, tile_position=tp)], n=2,
                                waits=wy if (pair == 0 and ii == 0) else {})
                    ymm_last[0] = pr.counts["tensor"]

                    def ev_mul(e, grp=grp, sl=sl):
                        return [e.tensor_tensor(
                            BCA[:, 12288 + (grp * 4 + pair) * 128:12288 + (grp * 4 + pair + 1) * 128],
                            PSF[pair][:, 0:128],
                            SGD[:, sl * 1024 + pair * 128: sl * 1024 + (pair + 1) * 128], ALU.mult)
                            for pair in range(4)]
                    pr.add("vector", ev_mul, n=4,
                           waits=dict(hbase, tensor=ymm_last[0]), pre=[sg_tok[gi]])

                    def ev_add(e, grp=grp, sl=sl):
                        return [e.tensor_tensor(
                            BCA[:, 12288 + (grp * 4 + pair) * 128:12288 + (grp * 4 + pair + 1) * 128],
                            BCA[:, 12288 + (grp * 4 + pair) * 128:12288 + (grp * 4 + pair + 1) * 128],
                            SGD[:, sl * 1024 + 512 + pair * 128: sl * 1024 + 512 + (pair + 1) * 128],
                            ALU.add) for pair in range(4)]
                    pr.add("vector", ev_add, n=4)
                    ev_marks[0] = pr.counts["vector"]
                    evm[gi] = ev_marks[0]
            pr.barrier()
            if debug:
                pr.add("vector", lambda e: e.tensor_copy(XTC[:, 4096:8192], BCA[:, 12288:16384]))
                pr.dma("gpsimd", lambda e, c=c: e.dma_start(out=dbg["dbg_yg"][c], in_=XTC[:, 4096:8192]))

            pr.add("scalar", lambda e: e.activation(BCA[:, 0:4096], BCA[:, 12288:16384], AF.Square))

            def rms_mm(e):
                return [e.matmul(PSF[4][0:1, 0:128], ONESCF[:, 0:1],
                                 BCA[:, fcg * 128:(fcg + 1) * 128],
                                 start=(fcg == 0), stop=(fcg == 31)) for fcg in range(32)]
            pr.add("tensor", rms_mm, n=32)
            rms_t[0] = pr.counts["tensor"]
            pr.add("vector", lambda e: e.tensor_scalar(
                ROWS[0:1, 640:768], PSF[4][0:1, 0:128], 1.0 / INTER, EPS, ALU.mult, ALU.add))
            pr.add("scalar", lambda e: e.activation(ROWS[0:1, 640:768], ROWS[0:1, 640:768], AF.Sqrt))
            pr.add("vector", lambda e: e.reciprocal(ROWS[0:1, 768:896], ROWS[0:1, 640:768]))
            pr.add("vector", lambda e: e.tensor_copy(ROWB[0:1, 768:896], ROWS[0:1, 768:896]))
            pr.add("tensor", lambda e: e.matmul(PSF[5][:, 0:128], ONESB[0:1, :],
                                                ROWB[0:1, 768:896], start=True, stop=True))
            pr.add("vector", lambda e: e.tensor_copy(SMF[:, 0:128], PSF[5][:, 0:128]))
            pr.add("vector", lambda e: e.tensor_tensor(
                BCA[:, 12288:16384].rearrange("p (g l) -> p g l", l=128),
                BCA[:, 12288:16384].rearrange("p (g l) -> p g l", l=128),
                SMF[:, 0:128].rearrange("p (o l) -> p o l", o=1).broadcast_to((128, 32, 128)),
                ALU.mult))
            pr.add("vector", lambda e: e.tensor_tensor(
                XTC[:, 4096:8192].rearrange("p (g l) -> p g l", l=128),
                BCA[:, 12288:16384].rearrange("p (g l) -> p g l", l=128),
                NWCOL[:, :].rearrange("p (g o) -> p g o", o=1).broadcast_to((128, 32, 128)),
                ALU.mult))
            pr.dma("sync", lambda e, c=c: e.dma_start(out=d_yn[c], in_=XTC[:, 4096:8192]))

        WOPX = A3
        YNL = A2b[:, 0:8192]
        STM = A4[:, 0:512]
        pr.strict = "op" not in relax
        ynl_pf = PF(nc, "ynl_pf")
        wop_pf = PF(nc, "wop_pf")
        STM2 = A4[:, 0:1024]
        stm_st = {}
        obase = pr.mark()
        op_mm = {}
        op_ev = {}
        for osl in range(4):
            wwx = dict(obase)
            if (osl - 1, 7) in op_mm:
                wwx["tensor"] = op_mm[(osl - 1, 7)]

            def ld_wop(e, fcgs, osl=osl):
                return [e.dma_start(
                    out=WOPX[:, fcg * 512:(fcg + 1) * 512],
                    in_=w_opT[fcg * 128:(fcg + 1) * 128, osl * 512:(osl + 1) * 512])
                    for fcg in fcgs]
            wop_pf.load(pr, "gpsimd", lambda e, f=ld_wop: f(e, range(0, 32, 2)),
                        n=16, slot=osl, waits=wwx)
            wtok = wop_pf.load(pr, "sync", lambda e, f=ld_wop: f(e, range(1, 32, 2)),
                               n=16, slot=osl, waits=wwx)
            for c in range(NCH):
                gi = osl * 8 + c
                wyn = dict(obase)
                if (osl, c - 2) in op_mm:
                    wyn["tensor"] = op_mm[(osl, c - 2)]
                elif osl > 0 and (osl - 1, 6 + c) in op_mm:
                    wyn["tensor"] = op_mm[(osl - 1, 6 + c)]
                ytok = ynl_pf.load(pr, "gpsimd", lambda e, c=c: e.dma_start(
                    out=YNL[:, (c % 2) * 4096:(c % 2) * 4096 + 4096], in_=d_yn[c]),
                    n=1, slot=c, waits=wyn)
                wmm = dict(obase)
                if (osl, c - 2) in op_ev:
                    wmm["vector"] = op_ev[(osl, c - 2)]
                elif osl > 0 and (osl - 1, 6 + c) in op_ev:
                    wmm["vector"] = op_ev[(osl - 1, 6 + c)]

                def mm_op(e, c=c):
                    return [e.matmul(PSF[4 + c % 2][:, 0:512],
                                     YNL[:, (c % 2) * 4096 + fcg * 128:(c % 2) * 4096 + (fcg + 1) * 128],
                                     WOPX[:, fcg * 512:(fcg + 1) * 512],
                                     start=(fcg == 0), stop=(fcg == 31)) for fcg in range(32)]
                pr.add("tensor", mm_op, n=32, waits=wmm,
                       pre=[ytok] + ([wtok] if c == 0 else []))
                op_mm[(osl, c)] = pr.counts["tensor"]
                wev = dict(obase, tensor=op_mm[(osl, c)])
                if (osl, c - 2) in op_ev:
                    wev["sync"] = stm_st.get((osl, c - 2), 0)
                elif osl > 0:
                    wev["sync"] = stm_st.get((osl - 1, 6 + c), 0)
                pr.add("vector", lambda e, c=c: e.tensor_copy(
                    STM2[:, (c % 2) * 512:(c % 2) * 512 + 512], PSF[4 + c % 2][:, 0:512]), waits=wev)
                op_ev[(osl, c)] = pr.counts["vector"]
                pr.store("sync", lambda e, c=c, osl=osl: e.dma_start(
                    out=d_s[c, :, osl * 512:(osl + 1) * 512],
                    in_=STM2[:, (c % 2) * 512:(c % 2) * 512 + 512]))
                stm_st[(osl, c)] = pr.counts["sync"]
        pr.barrier()
        if debug:
            pr.dma("gpsimd", lambda e: [e.dma_start(out=dbg["dbg_s"][c], in_=d_s[c])
                                      for c in range(NT)], n=NT)

        XFNT = A3
        XNT = A1
        XN3F = A5
        G1 = A2b
        WSLB = A4
        XNFR = A2b[:, 0:16384]
        DWN = sb("DWN", [32, 512], BF16)
        O2ALL = A5
        MC = sb("MC", [128, 2], F32)
        GT = A7[:, 0:2048]
        STL = A4[:, 0:2048]

        pr.strict = "p5a" not in relax
        ff1_pf = PF(nc, "ff1_pf")
        ff2_pf = PF(nc, "ff2_pf")
        for t in range(NT):
            xs = A2[:, 0:2048]
            xn = A2[:, 2048:4096]
            rst = A5[:, 4096:6144]
            scr = A2[:, 6144:8192]
            pr.dma("sync", lambda e, t=t, xs=xs: e.dma_start(
                out=xs, in_=x_own[t * 128:(t + 1) * 128, :]))
            pr.dma("sync", lambda e, t=t, rst=rst: e.dma_start(
                out=rst, in_=rs_out[t * 128:(t + 1) * 128, :]),
                pre=[(cc4, 1)] if t == 0 else None)
            pr.add("vector", lambda e, xs=xs: e.reduce_sum(COLS[:, 0:1], xs, axis=AXX))
            pr.add("vector", lambda e: e.tensor_scalar_mul(COLS[:, 1:2], COLS[:, 0:1], -1.0 / D))
            pr.add("scalar", lambda e, xs=xs, scr=scr: e.activation(
                scr, xs, AF.Square, bias=COLS[:, 1:2], accum_out=COLS[:, 2:3]))
            pr.add("vector", lambda e: e.tensor_scalar(COLS[:, 3:4], COLS[:, 2:3], 1.0 / D, EPS, ALU.mult, ALU.add))
            pr.add("scalar", lambda e: e.activation(COLS[:, 4:5], COLS[:, 3:4], AF.Sqrt))
            pr.add("vector", lambda e: e.reciprocal(COLS[:, 5:6], COLS[:, 4:5]))
            pr.add("vector", lambda e, xs=xs, xn=xn: e.tensor_scalar(
                xn, xs, COLS[:, 1:2], COLS[:, 5:6], ALU.add, ALU.mult))
            pr.add("vector", lambda e, t=t, xn=xn: e.tensor_copy(
                XNT[:, t * 2048:(t + 1) * 2048], xn))
            pr.add("vector", lambda e, t=t, xn=xn, rst=rst: e.tensor_tensor(
                XFNT[:, t * 2048:(t + 1) * 2048], xn, rst, ALU.add))
        if debug:
            pr.dma("gpsimd", lambda e: e.dma_start(out=dbg["dbg_rs"], in_=rs_out))

        for tb in range(2):
            for ti in range(4):
                t = tb * 4 + ti
                xf = A2[:, 0:2048]
                o3 = A2[:, 2048:4096]
                scr = A2[:, 6144:8192]
                pr.add("vector", lambda e, t=t, xf=xf: e.tensor_copy(
                    xf, XFNT[:, t * 2048:(t + 1) * 2048]))
                pr.add("vector", lambda e, xf=xf: e.reduce_sum(COLS[:, 0:1], xf, axis=AXX))
                pr.add("vector", lambda e: e.tensor_scalar_mul(COLS[:, 1:2], COLS[:, 0:1], -1.0 / D))
                pr.add("scalar", lambda e, xf=xf, scr=scr: e.activation(
                    scr, xf, AF.Square, bias=COLS[:, 1:2], accum_out=COLS[:, 2:3]))
                pr.add("vector", lambda e: e.tensor_scalar(COLS[:, 3:4], COLS[:, 2:3], 1.0 / D, EPS, ALU.mult, ALU.add))
                pr.add("scalar", lambda e: e.activation(COLS[:, 4:5], COLS[:, 3:4], AF.Sqrt))
                pr.add("vector", lambda e: e.reciprocal(COLS[:, 5:6], COLS[:, 4:5]))
                pr.add("vector", lambda e, xf=xf, o3=o3: e.tensor_scalar(
                    o3, xf, COLS[:, 1:2], COLS[:, 5:6], ALU.add, ALU.mult))
                l2base = pr.mark()
                for rnd in range(4):
                    fcs = [rnd * 4 + q for q in range(4)]
                    wt2 = dict(l2base)
                    if rnd > 0:
                        wt2["vector"] = pr.counts["vector"]
                    pr.add("tensor", lambda e, fcs=fcs, o3=o3: [e.transpose(
                        PSF[q][:, 0:128], o3[:, fc * 128:(fc + 1) * 128], IDf[:, :])
                        for q, fc in enumerate(fcs)], n=4, waits=wt2)
                    tm2 = pr.counts["tensor"]
                    pr.add("vector", lambda e, fcs=fcs, ti=ti: [e.tensor_scalar(
                        XN3F[:, fc * 512 + ti * 128: fc * 512 + ti * 128 + 128],
                        PSF[q][:, 0:128], L2S[:, fc:fc + 1], L2B[:, fc:fc + 1],
                        ALU.mult, ALU.add) for q, fc in enumerate(fcs)], n=4,
                        waits=dict(l2base, tensor=tm2))
                pr.barrier()
            pr.dma("sync", lambda e, tb=tb: [e.dma_start(
                out=XNFR[:, ic * 512:(ic + 1) * 512],
                in_=d_xnfm[ic, :, tb * 512:(tb + 1) * 512]) for ic in range(16)], n=16)

            def mm_dn(e):
                return [e.matmul(PSF[6][0:32, 0:512], WDT[:, ic * 32:(ic + 1) * 32],
                                 XNFR[:, ic * 512:(ic + 1) * 512],
                                 start=(ic == 0), stop=(ic == 15)) for ic in range(16)]
            pr.add("tensor", mm_dn, n=16)
            pr.add("vector", lambda e: e.tensor_copy(DWN[:, :], PSF[6][0:32, 0:512]))

            pr.strict = "p5b" not in relax
            fbase = pr.mark()
            f1_mm = {}
            f1_gel = {}
            for fg in range(16):
                buf = fg % 2
                Wb = WSLB[:, buf * 8192: buf * 8192 + 8192]
                wld = dict(fbase)
                if fg - 2 in f1_mm:
                    wld["tensor"] = f1_mm[fg - 2]
                def ld_f1(e, js, fg=fg, Wb=Wb):
                    return [e.dma_start(
                        out=Wb[:, j * 2048:(j + 1) * 2048].rearrange("q (ic o) -> q ic o", o=128),
                        in_=w_f1T.rearrange("(ic q) o -> q ic o", q=128)
                            [:, :, (fg * 4 + j) * 128:(fg * 4 + j + 1) * 128])
                        for j in js]
                ff1_pf.load(pr, "gpsimd", lambda e, f=ld_f1: f(e, (0, 2)), n=2, slot=fg, waits=wld)
                ftok = ff1_pf.load(pr, "sync", lambda e, f=ld_f1: f(e, (1, 3)), n=2, slot=fg, waits=wld)
                for j in range(4):
                    bank = PSF[(fg % 2) * 2 + j % 2]
                    wmm = dict(fbase)
                    prev_user = (fg, j - 2) if j >= 2 else (fg - 2, j + 2)
                    if prev_user in f1_gel:
                        wmm["scalar"] = f1_gel[prev_user]

                    def mm_f1(e, j=j, Wb=Wb, bank=bank):
                        return [e.matmul(bank[:, 0:512],
                                         Wb[:, j * 2048 + ic * 128: j * 2048 + (ic + 1) * 128],
                                         XN3F[:, ic * 512:(ic + 1) * 512],
                                         start=(ic == 0), stop=(ic == 15)) for ic in range(16)]
                    pr.add("tensor", mm_f1, n=16, waits=wmm, pre=[ftok] if j == 0 else None)
                    ffc = fg * 4 + j
                    tm = pr.counts["tensor"]
                    pr.add("scalar", lambda e, ffc=ffc, bank=bank: e.activation(
                        G1[:, ffc * 512:(ffc + 1) * 512], bank[:, 0:512], AF.Gelu_apprx_tanh),
                        waits=dict(fbase, tensor=tm))
                    f1_gel[(fg, j)] = pr.counts["scalar"]
                f1_mm[fg] = pr.counts["tensor"]
            pr.barrier()

            pr.strict = "p5c" not in relax
            f2base = pr.mark()
            f2_mm = {}
            f2_ev = {}
            for osl in range(4):
                for sl8 in range(8):
                    gslot = osl * 8 + sl8
                    buf2 = gslot % 2
                    W2 = WSLB[:, buf2 * 8192: buf2 * 8192 + 4096]
                    wld = dict(f2base)
                    if gslot - 2 in f2_mm:
                        wld["tensor"] = f2_mm[gslot - 2]
                    def ld_f2(e, qs, osl=osl, sl8=sl8, W2=W2):
                        return [e.dma_start(
                            out=W2[:, q * 512:(q + 1) * 512],
                            in_=w_f2T[(sl8 * 8 + q) * 128:(sl8 * 8 + q + 1) * 128,
                                      osl * 512:(osl + 1) * 512]) for q in qs]
                    ff2_pf.load(pr, "gpsimd", lambda e, f=ld_f2: f(e, (0, 2, 4, 6)),
                                n=4, slot=gslot, waits=wld)
                    f2tok = ff2_pf.load(pr, "sync", lambda e, f=ld_f2: f(e, (1, 3, 5, 7)),
                                        n=4, slot=gslot, waits=wld)
                    for ti in range(4):
                        w2m = dict(f2base)
                        if sl8 == 0 and (osl - 1, ti) in f2_ev:
                            w2m["vector"] = f2_ev[(osl - 1, ti)]

                        def mm_f2(e, ti=ti, sl8=sl8, W2=W2):
                            return [e.matmul(PSF[ti][:, 0:512],
                                             G1[:, (sl8 * 8 + q) * 512 + ti * 128:
                                                (sl8 * 8 + q) * 512 + ti * 128 + 128],
                                             W2[:, q * 512:(q + 1) * 512],
                                             start=(sl8 == 0 and q == 0),
                                             stop=(sl8 == 7 and q == 7)) for q in range(8)]
                        pr.add("tensor", mm_f2, n=8, waits=w2m,
                               pre=[f2tok] if ti == 0 else None)
                    f2_mm[gslot] = pr.counts["tensor"]
                for ti in range(4):
                    t = tb * 4 + ti
                    pr.add("vector", lambda e, t=t, ti=ti, osl=osl: e.tensor_tensor(
                        O2ALL[:, ti * 2048 + osl * 512: ti * 2048 + osl * 512 + 512],
                        PSF[ti][:, 0:512],
                        XFNT[:, t * 2048 + osl * 512: t * 2048 + osl * 512 + 512],
                        ALU.add), waits=dict(f2base, tensor=f2_mm[osl * 8 + 7]))
                    f2_ev[(osl, ti)] = pr.counts["vector"]
            pr.barrier()

            pr.strict = "p5d" not in relax
            WUT = WSLB[0:32, 2048:4096]
            pr.dma("gpsimd", lambda e, WUT=WUT: e.dma_start(out=WUT, in_=w_uT))

            for ti in range(4):
                t = tb * 4 + ti
                xs = A2[:, 0:2048]
                sg = A2[:, 2048:4096]
                of = A2[:, 4096:6144]
                f1 = A2[:, 6144:6656]
                f2 = A2[:, 6656:7168]
                f3 = A2[:, 7168:7680]
                pr.dma("sync", lambda e, t=t: e.dma_start(
                    out=MC[:, 0:1], in_=d_m.rearrange("a (t o) -> a t o", o=1)[0, t * 128:(t + 1) * 128]))
                pr.dma("sync", lambda e, t=t: e.dma_start(
                    out=MC[:, 1:2], in_=d_m.rearrange("a (t o) -> a t o", o=1)[1, t * 128:(t + 1) * 128]))
                pr.dma("sync", lambda e, t=t: e.dma_start(out=STL[:, :], in_=d_s[t]))
                pr.dma("sync", lambda e, t=t, xs=xs: e.dma_start(
                    out=xs, in_=x_own[t * 128:(t + 1) * 128, :]))
                cbase = pr.mark()
                pr.add("tensor", lambda e, ti=ti: [e.matmul(
                    PSF[3 + osl][:, 0:512], DWN[:, ti * 128:(ti + 1) * 128],
                    WUT[:, osl * 512:(osl + 1) * 512], start=True, stop=True)
                    for osl in range(4)], n=4, waits=cbase)
                upt = pr.counts["tensor"]
                f1b = A2[:, 8192:10240]
                f2b = A2[:, 10240:12288]
                f3b = A2[:, 12288:14336]
                pr.add("vector", lambda e, t=t, f1b=f1b: e.tensor_tensor(
                    f1b, XNT[:, t * 2048:(t + 1) * 2048], BREP[:, 0:2048], ALU.add),
                    waits=cbase)
                pr.add("vector", lambda e, ti=ti, f1b=f1b, f2b=f2b: e.tensor_tensor(
                    f2b, O2ALL[:, ti * 2048:(ti + 1) * 2048], f1b, ALU.subtract),
                    waits=cbase)
                pr.add("vector", lambda e, f1b=f1b, f2b=f2b, f3b=f3b: e.scalar_tensor_tensor(
                    f3b, f2b, MC[:, 1:2], f1b, ALU.mult, ALU.add), waits=cbase)
                pr.add("vector", lambda e, f2b=f2b: [e.tensor_tensor(
                    f2b[:, osl * 512:(osl + 1) * 512], PSF[3 + osl][:, 0:512],
                    BREP[:, osl * 512:osl * 512 + 512], ALU.subtract) for osl in range(4)],
                    n=4, waits=dict(cbase, tensor=upt))
                pr.add("vector", lambda e, f2b=f2b, f3b=f3b: e.scalar_tensor_tensor(
                    f3b, f2b, MC[:, 0:1], f3b, ALU.mult, ALU.add), waits=cbase)
                f3v = pr.counts["vector"]
                pr.add("scalar", lambda e, f3b=f3b: e.activation(GT[:, 0:2048], f3b, AF.Silu),
                       waits=dict(cbase, vector=f3v))
                if debug:
                    pr.add("vector", lambda e, f3b=f3b: e.tensor_copy(A2[:, 8192:10240], f3b),
                           waits=dict(cbase)) if False else None
                if debug:
                    pr.dma("sync", lambda e, t=t: e.dma_start(
                        out=dbg["dbg_ffn"][t], in_=A2[:, 12288:14336]))
                gts = pr.counts["scalar"]
                pr.add("vector", lambda e, sg=sg: e.tensor_tensor(sg, STL[:, :], GT[:, :], ALU.mult),
                       waits=dict(cbase, scalar=gts), rejoin=True)
                pr.add("vector", lambda e, xs=xs, sg=sg, of=of: e.tensor_tensor(of, sg, xs, ALU.add))
                pr.dma("sync", lambda e, t=t, of=of: e.dma_start(
                    out=out[t * 128:(t + 1) * 128, :], in_=of))

        pr.add("sync", lambda e: e.nop())

        with nc.Block() as block:
            pr.realize(block, sems)

    return nc


def _prep_inputs(inputs):
    f32 = lambda a: np.ascontiguousarray(np.asarray(a, np.float32))
    x = f32(inputs["L0_attn_input_states"]).reshape(S, D)
    wip = f32(inputs["L0_attn_in_proj_w"])
    convw = f32(inputs["L0_attn_conv_w"]).reshape(CONV_DIM, 4)
    convb = f32(inputs["L0_attn_conv_b"])
    alog = f32(inputs["L0_attn_A_log"])
    dtb = f32(inputs["L0_attn_dt_bias"])
    dvec = f32(inputs["L0_attn_D"])
    nw = f32(inputs["L0_attn_norm_w"])
    wop = f32(inputs["L0_attn_out_proj_w"])
    sb_ = f32(inputs["L0_ffn_simple_bias"])
    wd = f32(inputs["L0_ffn_w_down"])
    wu = f32(inputs["L0_ffn_w_up"])
    wr = f32(inputs["L0_ffn_w_router"])
    l1s = f32(inputs["L0_ffn_ln1_scale"]); l1b = f32(inputs["L0_ffn_ln1_bias"])
    wq = f32(inputs["L0_ffn_wq"]); wk = f32(inputs["L0_ffn_wk"])
    wv = f32(inputs["L0_ffn_wv"]); wo = f32(inputs["L0_ffn_wo"])
    l2s = f32(inputs["L0_ffn_ln2_scale"]); l2b = f32(inputs["L0_ffn_ln2_bias"])
    wf1 = f32(inputs["L0_ffn_w_ff1"]); wf2 = f32(inputs["L0_ffn_w_ff2"])

    identf = np.eye(128, dtype=np.float32)
    mask_sle = (np.arange(128)[:, None] <= np.arange(128)[None, :]).astype(np.float32)
    amask = np.zeros((4, 128, 512), np.float32)
    for j in range(4):
        amask[j] = (np.arange(128)[:, None] + 128 * j <= np.arange(512)[None, :])
    onesf = np.ones((1, 128), np.float32)

    bfc = lambda a: np.ascontiguousarray(np.asarray(a).astype(ml_dtypes.bfloat16))
    shared = dict(
        w_ipT=bfc(wip.T),
        conv_w=convw, conv_b=convb, a_log=alog, dt_bias=dtb, d_vec=dvec,
        norm_w=nw, w_opT=bfc(wop.T),
        sbias=sb_, w_dT=bfc(wd.T), w_uT=bfc(wu.T),
        w_rT=np.ascontiguousarray(wr.T),
        ln1_s=l1s, ln1_b=l1b, ln2_s=l2s, ln2_b=l2b,
        w_f1T=bfc(wf1.T), w_f2T=bfc(wf2.T),
        identf=identf, identb_src=identf, mask_sle=mask_sle, amask=amask,
        onesf=onesf,
    )
    wqT = bfc(wq.T); wkT = bfc(wk.T); wvT = bfc(wv.T)
    woT = bfc(wo.T)
    in_maps = []
    for c in range(NCORE):
        m = dict(shared)
        m["x_own"] = np.ascontiguousarray(x[c * TOK:(c + 1) * TOK])
        if c == 0:
            m["x_halo"] = np.zeros((3, D), np.float32)
        else:
            m["x_halo"] = np.ascontiguousarray(x[c * TOK - 3:c * TOK])
        hsl = slice(2 * c * DH, (2 * c + 2) * DH)
        m["w_qT"] = np.ascontiguousarray(wqT[:, hsl])
        m["w_kT"] = np.ascontiguousarray(wkT[:, hsl])
        m["w_vT"] = np.ascontiguousarray(wvT[:, hsl])
        m["w_oTs"] = np.ascontiguousarray(woT[hsl])
        g = np.zeros((1, NCORE), np.float32)
        g[0, :c] = 1.0
        m["gates"] = g
        in_maps.append(m)
    return in_maps


def kernel(**inputs):
    if "nc" not in _CACHE:
        _CACHE["nc"] = build_program(debug=False)
    nc = _CACHE["nc"]
    in_maps = _prep_inputs(inputs)
    res = run_bass_kernel_spmd(nc, in_maps, core_ids=list(range(NCORE)))
    outp = np.concatenate([np.asarray(r["out"], np.float32) for r in res.results], axis=0)
    return outp.reshape(1, S, D)
